# version 24
# baseline (speedup 1.0000x reference)
import contextlib
import numpy as np
import concourse.bass as bass
import concourse.mybir as mybir
from concourse.bass_utils import run_bass_kernel_spmd

F32 = mybir.dt.float32
BF16 = mybir.dt.bfloat16
AF = mybir.ActivationFunctionType
ALU = mybir.AluOpType
AX = mybir.AxisListType

NCORES = 8
B = 2
L = 16384
LC = L // NCORES
DM = 1024
DI = 2048
E4 = 2 * DI
NS = 16
RK = 64
NEXP = 8
IM = 512
NDT = DI // 128
NKM = DM // 128

T1A = 512
T1B = 1024
T2A = 512
T2B = 512
EPS = 1e-5

ENGINES = ("sync", "tensor", "vector", "scalar", "gpsimd")


class _Op:
    __slots__ = ("eng", "emit", "deps", "inc", "count", "used", "strict", "tag",
                 "semkey")

    def __init__(self, eng, emit, deps, inc, strict, tag):
        self.eng = eng
        self.emit = emit
        self.deps = deps
        self.inc = inc
        self.count = 0
        self.used = False
        self.strict = strict
        self.tag = tag
        self.semkey = eng


class Buf:
    __slots__ = ("name", "writer", "readers")

    def __init__(self, name=""):
        self.name = name
        self.writer = None
        self.readers = []


NDMALANES = 8


class Sched:
    def __init__(self):
        self.q = {e: [] for e in ENGINES}
        self.all = []
        self._dma_idx = {e: 0 for e in ENGINES}

    def add(self, eng, emit, reads=(), writes=(), deps=(), dma=False,
            strict=False, tag=""):
        d = []
        for b in reads:
            if b.writer is not None:
                d.append(b.writer)
        for b in writes:
            if b.writer is not None:
                d.append(b.writer)
            d.extend(b.readers)
        d.extend(x for x in deps if x is not None)
        op = _Op(eng, emit, d, 16 if dma else 1, strict, tag)
        if dma:
            lane = self._dma_idx[eng] % NDMALANES
            self._dma_idx[eng] += 1
            op.semkey = (eng, lane)
        for b in reads:
            b.readers.append(op)
        for b in writes:
            b.writer = op
            b.readers = []
        self.q[eng].append(op)
        self.all.append(op)
        return op

    def barrier(self):
        last = {}
        for e in ENGINES:
            for op in reversed(self.q[e]):
                if op.emit is None:
                    continue
                if op.semkey not in last:
                    last[op.semkey] = op
                if op.semkey == e:
                    break
        for e in ENGINES:
            deps = [op for op in last.values() if op.eng != e]
            deps += [op for op in last.values()
                     if op.eng == e and op.semkey != e]
            self.add(e, None, deps=deps, strict=True, tag="barrier")

    def finalize(self, nc, stack):
        pos = {}
        for e in ENGINES:
            for i, op in enumerate(self.q[e]):
                pos[id(op)] = i
        for op in self.all:
            if op.inc == 16 and op.emit is not None:
                op.used = True
            wd = []
            for dp in op.deps:
                if dp.eng != op.eng or dp.semkey != dp.eng:
                    dp.used = True
                    wd.append(dp)
                elif op.strict or pos[id(dp)] >= pos[id(op)] - 2:
                    if dp.emit is not None:
                        dp.used = True
                        wd.append(dp)
            op.deps = wd
        self.sems = {}
        counts = {}
        for e in ENGINES:
            for op in self.q[e]:
                if op.used:
                    assert op.emit is not None, f"dep on no-op ({op.tag})"
                    counts[op.semkey] = counts.get(op.semkey, 0) + op.inc
                op.count = counts.get(op.semkey, 0)
        for k, c in counts.items():
            assert c < 60000, f"semaphore overflow on {k}: {c}"
        for e in ENGINES:
            self.sems[e] = stack.enter_context(nc.semaphore(f"s_{e}"))
            for ln in range(NDMALANES):
                self.sems[(e, ln)] = stack.enter_context(
                    nc.semaphore(f"s_{e}_d{ln}"))

    def emit_engine(self, engmod, e):
        sems = self.sems
        waited = {}
        for op in self.q[e]:
            for dp in op.deps:
                if dp.count > waited.get(dp.semkey, 0):
                    engmod.wait_ge(sems[dp.semkey], dp.count)
                    waited[dp.semkey] = dp.count
            if op.emit is not None:
                inst = op.emit(engmod)
                if op.used:
                    inst.then_inc(sems[op.semkey], op.inc)

    def stats(self):
        return {e: (len(self.q[e]), self.q[e][-1].count if self.q[e] else 0)
                for e in ENGINES}


def _split_shards(x):
    xt = np.ascontiguousarray(np.swapaxes(np.asarray(x, np.float32), 1, 2))
    shards = []
    for i in range(NCORES):
        t0 = i * LC
        sl = np.zeros((B, DM, 3 + LC), np.float32)
        sl[:, :, 3:] = xt[:, :, t0:t0 + LC]
        if t0 >= 3:
            sl[:, :, 0:3] = xt[:, :, t0 - 3:t0]
        shards.append(np.ascontiguousarray(sl))
    return shards


def _vec2d(v):
    return np.ascontiguousarray(np.asarray(v, np.float32).reshape(NDT, 128).T)


def prepare_in_maps(L0_attn_x, L0_attn_W_in, L0_attn_conv_w, L0_attn_conv_b,
                    L0_attn_W_x, L0_attn_W_dt, L0_attn_b_dt, L0_attn_A_log,
                    L0_attn_Dp, L0_attn_W_out, L0_ffn_w_gate, L0_ffn_w_fc,
                    L0_ffn_w_proj):
    f32 = np.float32
    xsh = _split_shards(L0_attn_x)

    w_inT = np.ascontiguousarray(np.asarray(L0_attn_W_in, f32).T)
    w_xT = np.ascontiguousarray(np.asarray(L0_attn_W_x, f32).T)
    w_dtT = np.ascontiguousarray(np.asarray(L0_attn_W_dt, f32).T)
    w_outT = np.ascontiguousarray(np.asarray(L0_attn_W_out, f32).T)
    w_gateT = np.ascontiguousarray(np.asarray(L0_ffn_w_gate, f32).T)
    w_fcT = np.ascontiguousarray(np.swapaxes(np.asarray(L0_ffn_w_fc, f32), 1, 2))
    w_projT = np.ascontiguousarray(np.swapaxes(np.asarray(L0_ffn_w_proj, f32), 1, 2))

    cw2 = np.zeros((128, NDT * 4), f32)
    cwa = np.asarray(L0_attn_conv_w, f32)
    for dt in range(NDT):
        cw2[:, dt * 4:(dt + 1) * 4] = cwa[dt * 128:(dt + 1) * 128, :]
    a2 = np.zeros((128, NDT * NS), f32)
    al = np.asarray(L0_attn_A_log, f32)
    for dt in range(NDT):
        a2[:, dt * NS:(dt + 1) * NS] = al[dt * 128:(dt + 1) * 128, :]

    cb2v = _vec2d(L0_attn_conv_b)
    bdt2v = _vec2d(L0_attn_b_dt)
    dp2v = _vec2d(L0_attn_Dp)
    in_maps = []
    for i in range(NCORES):
        mask = np.zeros((128, 8), f32)
        mask[:, :i] = 1.0
        in_maps.append({
            "xT": xsh[i], "w_inT": w_inT, "w_xT": w_xT, "w_dtT": w_dtT,
            "w_outT": w_outT, "w_gateT": w_gateT, "w_fcT": w_fcT,
            "w_projT": w_projT, "conv_w2": cw2,
            "conv_b2": cb2v, "b_dt2": bdt2v,
            "dp2": dp2v, "a_log2": a2,
            "ident": np.eye(128, dtype=f32), "rkmask": mask,
        })
    return in_maps


def kernel(**inputs):
    in_maps = prepare_in_maps(**inputs)
    nc = build_program()
    res = run_bass_kernel_spmd(nc, in_maps, core_ids=list(range(NCORES)))
    out = np.empty((B, L, DM), np.float32)
    for i in range(NCORES):
        out[:, i * LC:(i + 1) * LC, :] = res.results[i]["out"]
    return out


class Arena:
    def __init__(self, nc, total_f32=53100):
        self.t = nc.alloc_sbuf_tensor("arena", [128, total_f32], F32)
        self.total = total_f32
        self.perm_off = 0
        self.off = 0

    def perm(self, n_f32, dtype=F32, name=""):
        ap = self._view(self.perm_off, n_f32, dtype)
        self.perm_off += n_f32
        self.off = max(self.off, self.perm_off)
        return ap

    def phase_reset(self):
        self.off = self.perm_off

    def alloc(self, n_f32, dtype=F32, name=""):
        ap = self._view(self.off, n_f32, dtype)
        self.off += n_f32
        assert self.off <= self.total, f"arena overflow at {name}: {self.off}"
        return ap

    def _view(self, off, n_f32, dtype):
        ap = self.t.ap()[:, off:off + n_f32]
        if dtype == BF16:
            ap = ap.bitcast(BF16)
        return ap


def build_program(stop_phase=99, dbg_out=False):
    nc = bass.Bass("TRN2", target_bir_lowering=False)
    dram = {}

    def din(name, shape, dtype=F32):
        dram[name] = nc.dram_tensor(name, list(shape), dtype, kind="ExternalInput").ap()

    def dint(name, shape, dtype=F32):
        kind = "ExternalOutput" if dbg_out else "Internal"
        dram[name] = nc.dram_tensor(name, list(shape), dtype, kind=kind).ap()

    din("xT", (B, DM, 3 + LC))
    din("w_inT", (DM, E4))
    din("w_xT", (DI, 96))
    din("w_dtT", (RK, DI))
    din("w_outT", (DI, DM))
    din("w_gateT", (DM, NEXP))
    din("w_fcT", (NEXP, DM, IM))
    din("w_projT", (NEXP, IM, DM))
    din("conv_w2", (128, NDT * 4))
    din("conv_b2", (128, NDT))
    din("b_dt2", (128, NDT))
    din("dp2", (128, NDT))
    din("a_log2", (128, NDT * NS))
    din("ident", (128, 128))
    din("rkmask", (128, NCORES))

    dram["out"] = nc.dram_tensor("out", [B, LC, DM], F32, kind="ExternalOutput").ap()

    dint("sc_xconv", (B, DI, LC))
    dint("sc_sz", (B, DI, LC))
    dint("sc_dt", (B, DI, LC))
    dint("sc_S", (B, DI, LC))
    dint("sc_BC", (B, 32, LC), BF16)
    dint("sc_gated", (B, DI, LC), BF16)
    dint("sc_xn1f", (B, DM, LC))
    dint("sc_xn1b", (B, DM, LC), BF16)
    if dbg_out:
        dram["dbg_bbc"] = nc.dram_tensor("dbg_bbc", [B, 2, 2, 128, NS * T1B], F32, kind="ExternalOutput").ap()
        dram["dbg_u"] = nc.dram_tensor("dbg_u", [B, DI, LC], F32, kind="ExternalOutput").ap()
        dram["dbg_y"] = nc.dram_tensor("dbg_y", [B, DI, LC], F32, kind="ExternalOutput").ap()
        dram["dbg_dA"] = nc.dram_tensor("dbg_dA", [B, DI, LC], F32, kind="ExternalOutput").ap()
    dram["cc_in"] = nc.dram_tensor("cc_in", [128, 1024], F32, kind="Internal").ap()
    dram["cc_out"] = nc.dram_tensor("cc_out", [128 * NCORES, 1024], F32, kind="Internal").ap()

    sch = Sched()
    with contextlib.ExitStack() as stack:
        build_body(nc, stack, sch, dram, stop_phase)
    return nc


def build_body(nc, stack, sch, dram, stop_phase):
    DBG = "dbg_u" in dram
    ar = Arena(nc)
    banks = [stack.enter_context(nc.psum_tensor(f"pb{i}", [128, 512], F32))
             for i in range(8)]
    bank = [bk.ap() for bk in banks]

    A_sb = ar.perm(NDT * NS, name="A")
    cw_sb = ar.perm(NDT * 4, name="cw")
    cb_sb = ar.perm(NDT, name="cb")
    bdt_sb = ar.perm(NDT, name="bdt")
    dp_sb = ar.perm(NDT, name="dp")
    rk_sb = ar.perm(NCORES, name="rk")
    rkc_sb = ar.perm(NCORES, name="rkc")
    id_sb = ar.perm(128, name="ident")
    onesr_sb = ar.perm(128, name="onesrow")
    onesc_sb = ar.perm(1, name="onescol")
    epsc_sb = ar.perm(1, name="epscol")
    zeros_sb = ar.perm(T1A, name="zeros")
    srun_sb = ar.perm(B * NDT, name="srun")
    hst_sb = ar.perm(B * NDT * NS, name="hst")
    D_sb = ar.perm(B * NDT * NS, name="D")
    h0_sb = ar.perm(B * NDT * NS, name="h0")


    bufs = {}

    def buf(name):
        if name not in bufs:
            bufs[name] = Buf(name)
        return bufs[name]

    def dma(eng, dst, src, *, reads=(), writes=(), deps=(), tag=""):
        return sch.add(eng, lambda m, d=dst, s=src: m.dma_start(out=d, in_=s),
                       reads=reads, writes=writes, deps=deps, dma=True, tag=tag)

    def mm(pb, lhsT, rhs, start, stop, reads, writes, tag=""):
        return sch.add("tensor",
                       lambda m, o=pb, a=lhsT, r=rhs, s1=start, s2=stop:
                       m.matmul(o, a, r, start=s1, stop=s2),
                       reads=reads, writes=writes, tag=tag)

    xT = dram["xT"]
    sc_xconv, sc_sz, sc_dt, sc_S, sc_BC = (dram["sc_xconv"], dram["sc_sz"],
                                           dram["sc_dt"], dram["sc_S"],
                                           dram["sc_BC"])
    sc_gated, sc_xn1f, sc_xn1b = (dram["sc_gated"], dram["sc_xn1f"],
                                  dram["sc_xn1b"])

    ar.phase_reset()
    wi_bf = [ar.alloc(E4 // 2, BF16, "wi") for _ in range(NKM)]
    wx_bf = ar.alloc(NDT * 96 // 2 + 48, BF16, "wx")
    wdt_bf = ar.alloc(DI // 2, BF16, "wdt")

    for name, dst in (("conv_w2", cw_sb), ("conv_b2", cb_sb),
                      ("b_dt2", bdt_sb), ("dp2", dp_sb), ("rkmask", rk_sb),
                      ("ident", id_sb), ("a_log2", A_sb)):
        dma("sync", dst, dram[name], writes=[buf("c_" + name)], tag="ldc")

    sch.add("scalar",
            lambda m: m.activation(out=A_sb, in_=A_sb, func=AF.Exp, bias=0.0, scale=1.0),
            reads=[buf("c_a_log2")], writes=[buf("c_a_log2")], tag="Aexp")
    sch.add("vector",
            lambda m: m.tensor_scalar(out=A_sb, in0=A_sb, scalar1=-1.0,
                                      scalar2=None, op0=ALU.mult),
            reads=[buf("c_a_log2")], writes=[buf("c_a_log2")], strict=True, tag="Aneg")
    sch.add("vector",
            lambda m: m.tensor_scalar(out=rkc_sb, in0=rk_sb, scalar1=-1.0,
                                      scalar2=1.0, op0=ALU.mult, op1=ALU.add),
            reads=[buf("c_rkmask")], writes=[buf("c_rkc")], tag="rkc")
    sch.add("vector", lambda m: m.memset(zeros_sb, 0.0), writes=[buf("c_zeros")])
    sch.add("vector", lambda m: m.memset(srun_sb, 0.0), writes=[buf("c_srun")])
    sch.add("vector", lambda m: m.memset(hst_sb, 0.0), writes=[buf("c_hst")])
    sch.add("vector", lambda m: m.memset(onesr_sb[0:1, :], 1.0), writes=[buf("c_onesr")])
    sch.add("vector", lambda m: m.memset(onesc_sb, 1.0), writes=[buf("c_onesc")])
    sch.add("vector", lambda m: m.memset(epsc_sb, EPS), writes=[buf("c_eps")])

    for k in range(NKM):
        dma("gpsimd", wi_bf[k], dram["w_inT"][k * 128:(k + 1) * 128, :],
            writes=[buf(f"w_wi{k}")], tag="ldwi")
    for k in range(NDT):
        dma("gpsimd", wx_bf[:, k * 96:(k + 1) * 96],
            dram["w_xT"][k * 128:(k + 1) * 128, :], writes=[buf("w_wx")], tag="ldwx")
    dma("gpsimd", wdt_bf[0:RK, 0:DI], dram["w_dtT"], writes=[buf("w_wdt")], tag="ldwdt")

    xb_bf = [ar.alloc(NKM * (T1A + 3) // 2 + 64, BF16, "xb") for _ in range(2)]
    xcf = [ar.alloc(T1A + 3, name="xcf") for _ in range(2)]
    halbuf = ar.alloc(NDT * 3, name="hal")
    accb_t = [ar.alloc(T1A, name="acc") for _ in range(NDT)]
    xconv_t = [ar.alloc(T1A, name="xconv") for _ in range(2)]
    sz_t = [ar.alloc(T1A, name="sz") for _ in range(2)]
    xcv_bf = ar.alloc(NDT * T1A // 2, BF16, "xcv")
    dtl_bf = ar.alloc(T1A // 2, BF16, "dtl")
    bc_f = [ar.alloc(T1A, name="bc") for _ in range(2)]
    e1b_t = [ar.alloc(T1A, name="e1") for _ in range(2)]
    e2b_t = [ar.alloc(T1A, name="e2") for _ in range(2)]
    rlb_t = [ar.alloc(T1A, name="rl") for _ in range(2)]
    dt_t = [ar.alloc(T1A, name="dt") for _ in range(2)]
    S_t = [ar.alloc(T1A, name="S") for _ in range(2)]

    NT1A = LC // T1A

    if stop_phase >= 1:
        for b in range(B):
            for tt in range(NT1A):
                c0 = tt * T1A
                tp_ = tt % 2
                xb = xb_bf[tp_]
                bxb = buf(f"xb{tp_}")
                for k in range(NKM):
                    dma("gpsimd", xb[:, k * (T1A + 3):(k + 1) * (T1A + 3)],
                        xT[b, k * 128:(k + 1) * 128, c0:c0 + T1A + 3],
                        writes=[bxb], tag="ldx")
                if tt == 0:
                    for m_ in range(NDT):
                        for k in range(NKM):
                            mm(bank[7][:, 0:3], wi_bf[k][:, m_ * 128:(m_ + 1) * 128],
                               xb[:, k * (T1A + 3):k * (T1A + 3) + 3],
                               k == 0, k == NKM - 1,
                               reads=[bxb, buf(f"w_wi{k}")], writes=[buf("pb7")], tag="halo")
                        sch.add("vector",
                                lambda m, m_=m_: m.tensor_copy(
                                    halbuf[:, m_ * 3:(m_ + 1) * 3], bank[7][:, 0:3]),
                                reads=[buf("pb7")], writes=[buf("hal")], tag="haloev")

                for m_ in range(NDT):
                    pbk = bank[m_ % 4]
                    bpb = buf(f"pb{m_ % 4}")
                    for k in range(NKM):
                        mm(pbk, wi_bf[k][:, m_ * 128:(m_ + 1) * 128],
                           xb[:, k * (T1A + 3) + 3:(k + 1) * (T1A + 3)],
                           k == 0, k == NKM - 1,
                           reads=[bxb, buf(f"w_wi{k}")], writes=[bpb], tag="xz")
                    xcfull = xcf[m_ % 2]
                    bxc = buf(f"xcf{m_ % 2}")
                    sch.add("vector",
                            lambda m, x=xcfull, m_=m_: m.tensor_copy(
                                x[:, 0:3], halbuf[:, m_ * 3:(m_ + 1) * 3]),
                            reads=[buf("hal")], writes=[bxc], strict=True, tag="halcp")
                    sch.add("scalar",
                            lambda m, x=xcfull, p=pbk: m.activation(
                                out=x[:, 3:3 + T1A], in_=p, func=AF.Copy,
                                bias=0.0, scale=1.0),
                            reads=[bpb], writes=[bxc], strict=True, tag="xcev")
                    sch.add("vector",
                            lambda m, x=xcfull, m_=m_: m.tensor_copy(
                                halbuf[:, m_ * 3:(m_ + 1) * 3], x[:, T1A:T1A + 3]),
                            reads=[bxc], writes=[buf("hal")], strict=True, tag="halupd")
                    accb = accb_t[m_]
                    bacc = buf(f"acc{m_}")
                    sch.add("scalar",
                            lambda m, x=xcfull, m_=m_, accb=accb: m.activation(
                                out=accb, in_=x[:, 0:T1A], func=AF.Copy,
                                bias=0.0, scale=cw_sb[:, m_ * 4:m_ * 4 + 1]),
                            reads=[bxc, buf("c_conv_w2")], writes=[bacc], tag="conv0")
                    for j in (1, 2, 3):
                        sch.add("vector",
                                lambda m, x=xcfull, m_=m_, j=j, accb=accb:
                                m.scalar_tensor_tensor(
                                    out=accb, in0=x[:, j:j + T1A],
                                    scalar=cw_sb[:, m_ * 4 + j:m_ * 4 + j + 1],
                                    in1=accb, op0=ALU.mult, op1=ALU.add),
                                reads=[bxc, buf("c_conv_w2"), bacc],
                                writes=[bacc], strict=True, tag=f"conv{j}")
                for m_ in range(NDT):
                    accb = accb_t[m_]
                    bacc = buf(f"acc{m_}")
                    xcv = xconv_t[m_ % 2]
                    bxcv = buf(f"xconv{m_ % 2}")
                    sch.add("scalar",
                            lambda m, o=xcv, m_=m_, accb=accb: m.activation(
                                out=o, in_=accb, func=AF.Silu,
                                bias=cb_sb[:, m_:m_ + 1], scale=1.0),
                            reads=[bacc, buf("c_conv_b2")], writes=[bxcv], tag="silu")
                    sch.add("vector",
                            lambda m, o=xcv_bf[:, m_ * T1A:(m_ + 1) * T1A], i=xcv:
                            m.tensor_copy(o, i),
                            reads=[bxcv], writes=[buf("xcvbf")], tag="xcvcast")
                    dma("gpsimd", sc_xconv[b, m_ * 128:(m_ + 1) * 128, c0:c0 + T1A], xcv,
                        reads=[bxcv], tag="stxconv")

                for m_ in range(NDT):
                    pbk = bank[4 + m_ % 3]
                    bpb = buf(f"pb{4 + m_ % 3}")
                    for k in range(NKM):
                        mm(pbk, wi_bf[k][:, (NDT + m_) * 128:(NDT + m_ + 1) * 128],
                           xb[:, k * (T1A + 3) + 3:(k + 1) * (T1A + 3)],
                           k == 0, k == NKM - 1,
                           reads=[bxb, buf(f"w_wi{k}")], writes=[bpb], tag="z")
                    szv = sz_t[m_ % 2]
                    bsz = buf(f"sz{m_ % 2}")
                    sch.add("scalar",
                            lambda m, o=szv, p=pbk: m.activation(
                                out=o, in_=p, func=AF.Silu, bias=0.0, scale=1.0),
                            reads=[bpb], writes=[bsz], tag="siluz")
                    dma("gpsimd", sc_sz[b, m_ * 128:(m_ + 1) * 128, c0:c0 + T1A], szv,
                        reads=[bsz], tag="stsz")

                for k in range(NDT):
                    mm(bank[4][0:96, :], wx_bf[:, k * 96:(k + 1) * 96],
                       xcv_bf[:, k * T1A:(k + 1) * T1A],
                       k == 0, k == NDT - 1,
                       reads=[buf("xcvbf"), buf("w_wx")], writes=[buf("pb4")], tag="xdbl")
                sch.add("scalar",
                        lambda m: m.activation(out=dtl_bf[0:RK, 0:T1A],
                                               in_=bank[4][0:RK, :],
                                               func=AF.Copy, bias=0.0, scale=1.0),
                        reads=[buf("pb4")], writes=[buf("dtl")], tag="dtlcast")
                bcv = bc_f[tt % 2]
                bbc = buf(f"bc{tt % 2}")
                sch.add("vector",
                        lambda m, o=bcv: m.tensor_copy(o[0:32, :], bank[4][64:96, :]),
                        reads=[buf("pb4")], writes=[bbc], tag="bcev")
                dma("gpsimd", sc_BC[b, :, c0:c0 + T1A], bcv[0:32, :], reads=[bbc], tag="stbc")

                for m_ in range(NDT):
                    pbk = bank[m_ % 2]
                    bpb = buf(f"pb{m_ % 2}")
                    mm(pbk, wdt_bf[0:RK, m_ * 128:(m_ + 1) * 128], dtl_bf[0:RK, 0:T1A],
                       True, True, reads=[buf("dtl"), buf("w_wdt")], writes=[bpb], tag="dtp")
                    e1b = e1b_t[m_ % 2]
                    e2b = e2b_t[m_ % 2]
                    rlb = rlb_t[m_ % 2]
                    be1 = buf(f"e1{m_ % 2}")
                    be2 = buf(f"e2{m_ % 2}")
                    brl = buf(f"rl{m_ % 2}")
                    sch.add("scalar",
                            lambda m, p=pbk, m_=m_, e1b=e1b: m.activation(
                                out=e1b, in_=p, func=AF.Abs,
                                bias=bdt_sb[:, m_:m_ + 1], scale=1.0),
                            reads=[bpb, buf("c_b_dt2")], writes=[be1], tag="sp_abs")
                    sch.add("scalar",
                            lambda m, e1b=e1b, e2b=e2b: m.activation(
                                out=e2b, in_=e1b, func=AF.Exp, bias=0.0, scale=-1.0),
                            reads=[be1], writes=[be2], strict=True, tag="sp_exp")
                    sch.add("scalar",
                            lambda m, e2b=e2b: m.activation(
                                out=e2b, in_=e2b, func=AF.Ln, bias=1.0, scale=1.0),
                            reads=[be2], writes=[be2], strict=True, tag="sp_ln")
                    sch.add("scalar",
                            lambda m, p=pbk, m_=m_, rlb=rlb: m.activation(
                                out=rlb, in_=p, func=AF.Relu,
                                bias=bdt_sb[:, m_:m_ + 1], scale=1.0),
                            reads=[bpb, buf("c_b_dt2")], writes=[brl], tag="sp_relu")
                    dtv = dt_t[m_ % 2]
                    bdtv = buf(f"dt{m_ % 2}")
                    sch.add("vector",
                            lambda m, o=dtv, e2b=e2b, rlb=rlb: m.tensor_add(o, e2b, rlb),
                            reads=[be2, brl], writes=[bdtv], tag="sp_add")
                    Sv = S_t[m_ % 2]
                    bS = buf(f"S{m_ % 2}")
                    scol = srun_sb[:, b * NDT + m_:b * NDT + m_ + 1]
                    sch.add("vector",
                            lambda m, o=Sv, i=dtv, s=scol: m.tensor_tensor_scan(
                                out=o, data0=zeros_sb, data1=i, initial=s,
                                op0=ALU.add, op1=ALU.add),
                            reads=[bdtv, buf("c_zeros"), buf("c_srun")], writes=[bS],
                            strict=True, tag="scum")
                    sch.add("scalar",
                            lambda m, i=Sv, s=scol: m.activation(
                                out=s, in_=i[:, T1A - 1:T1A], func=AF.Copy,
                                bias=0.0, scale=1.0),
                            reads=[bS], writes=[buf("c_srun")], strict=True, tag="scol")
                    dma("gpsimd", sc_dt[b, m_ * 128:(m_ + 1) * 128, c0:c0 + T1A], dtv,
                        reads=[bdtv], tag="stdt")
                    dma("gpsimd", sc_S[b, m_ * 128:(m_ + 1) * 128, c0:c0 + T1A], Sv,
                        reads=[bS], tag="stS")

        sch.barrier()

    ar.phase_reset()
    Bbc = ar.alloc(NS * T1B // 2, BF16, "Bbc")
    Cbc = [ar.alloc(NS * T1B // 2, BF16, "Cbc") for _ in range(2)]
    dt_s = [ar.alloc(T1B, name="dt1b")] * 2
    xc_s = [ar.alloc(T1B, name="xc1b") for _ in range(2)]
    sz_s = [ar.alloc(T1B, name="sz1b") for _ in range(2)]
    u_s = [ar.alloc(T1B // 2, BF16, "u1b")] * 2
    dA_s = [ar.alloc(T1B, name="dA") for _ in range(2)]
    dBx_s = [ar.alloc(T1B // 2, BF16, "dBx") for _ in range(2)]
    h_s = [ar.alloc(T1B, name="h") for _ in range(2)]
    g_s = [ar.alloc(T1B // 2, BF16, "g1b") for _ in range(4)]
    yp_s = [[ar.alloc(T1B, name="yp") for _ in range(4)] for _ in range(2)]
    y_s = yp_s[0][0]
    yf_s = [ar.alloc(T1B, name="yf") for _ in range(2)]
    g_bf = [ar.alloc(T1B // 2, BF16, "gbf") for _ in range(2)]
    FD_j = yp_s[0][1]

    NT1B = LC // T1B
    if stop_phase >= 2:
        pending_gate = None
        for b in range(B):
            for blk in range(NT1B):
                c0 = blk * T1B
                kp_ = blk % 2
                for which, dst, bname in ((0, Bbc, "bbcB"), (1, Cbc[kp_], f"C{kp_}_bbcC")):
                    for n in range(NS):
                        dma("sync", dst[:, n * T1B:(n + 1) * T1B],
                            sc_BC[b, which * NS + n, c0:c0 + T1B]
                            .unsqueeze(0).partition_broadcast(128),
                            writes=[buf(f"{bname}{n}")], tag="ldbc")
                if DBG:
                    dma("sync", dram["dbg_bbc"][b, blk, 0, :, :], Bbc,
                        reads=[buf("bbcB")], tag="dbgb")
                    dma("sync", dram["dbg_bbc"][b, blk, 1, :, :], Cbc,
                        reads=[buf("bbcC")], tag="dbgc")
                for dtile in range(NDT):
                    dp_ = dtile % 2
                    bdt = buf("dt1b")
                    bxc = buf(f"xc1b{dp_}")
                    bsz = buf(f"sz1b{dp_}")
                    bu = buf("u1b")
                    dma("scalar", dt_s[dp_],
                        sc_dt[b, dtile * 128:(dtile + 1) * 128, c0:c0 + T1B],
                        writes=[bdt], tag="lddt")
                    dma("scalar", xc_s[dp_],
                        sc_xconv[b, dtile * 128:(dtile + 1) * 128, c0:c0 + T1B],
                        writes=[bxc], tag="ldxc")
                    dma("scalar", sz_s[dp_],
                        sc_sz[b, dtile * 128:(dtile + 1) * 128, c0:c0 + T1B],
                        writes=[bsz], tag="ldsz")
                    sch.add("vector",
                            lambda m, dp_=dp_: m.tensor_mul(u_s[dp_], dt_s[dp_], xc_s[dp_]),
                            reads=[bdt, bxc], writes=[bu], tag="umul")
                    if DBG:
                        dma("sync", dram["dbg_u"][b, dtile * 128:(dtile + 1) * 128, c0:c0 + T1B],
                            u_s[dp_], reads=[bu], tag="dbgu")
                    for n in range(NS):
                        np_ = n % 2
                        acol = A_sb[:, dtile * NS + n:dtile * NS + n + 1]
                        hcol = hst_sb[:, (b * NDT + dtile) * NS + n:
                                      (b * NDT + dtile) * NS + n + 1]
                        bdA = buf(f"dA{np_}")
                        bdBx = buf(f"dBx{np_}")
                        bh = buf(f"h{np_}")
                        bg = buf(f"g{np_}")
                        sch.add("scalar",
                                lambda m, a=acol, dp_=dp_, np_=np_: m.activation(
                                    out=dA_s[np_], in_=dt_s[dp_], func=AF.Exp,
                                    bias=0.0, scale=a),
                                reads=[bdt, buf("c_a_log2")], writes=[bdA], tag="dAexp")
                        sch.add("vector",
                                lambda m, n=n, dp_=dp_, np_=np_: m.tensor_mul(
                                    dBx_s[np_], u_s[dp_], Bbc[:, n * T1B:(n + 1) * T1B]),
                                reads=[bu, buf(f"bbcB{n}")], writes=[bdBx], tag="dBx")
                        sch.add("vector",
                                lambda m, h=hcol, np_=np_: m.tensor_tensor_scan(
                                    out=h_s[np_], data0=dA_s[np_], data1=dBx_s[np_],
                                    initial=h, op0=ALU.mult, op1=ALU.add),
                                reads=[bdA, bdBx, buf("c_hst")],
                                writes=[bh], strict=True, tag="scan")
                        sch.add("vector",
                                lambda m, h=hcol, np_=np_: m.tensor_copy(
                                    h, h_s[np_][:, T1B - 1:T1B]),
                                reads=[bh], writes=[buf("c_hst")], strict=True,
                                tag="hcol")
                        g4 = n % 4
                        bg = buf(f"g{n % 4}")
                        sch.add("vector",
                                lambda m, n=n, np_=np_, kp_=kp_: m.tensor_mul(
                                    g_s[n % 4], h_s[np_], Cbc[kp_][:, n * T1B:(n + 1) * T1B]),
                                reads=[bh, buf(f"C{kp_}_bbcC{n}")], writes=[bg],
                                tag="ymul")
                        bp = buf(f"yp{dp_}{g4}")
                        if n < 4:
                            dma("gpsimd", yp_s[dp_][g4], g_s[g4],
                                reads=[bg], writes=[bp], tag="ycp")
                        else:
                            sch.add("gpsimd",
                                    lambda m, g4=g4, dp_=dp_: m.dma_start(
                                        out=yp_s[dp_][g4], in_=g_s[g4],
                                        accum_op=ALU.add),
                                    reads=[bg, bp], writes=[bp], dma=True, tag="yacc")
                    if DBG:
                        dma("sync", dram["dbg_y"][b, dtile * 128:(dtile + 1) * 128, c0:c0 + T1B],
                            y_s, reads=[buf("yp00")], tag="dbgy")
                    sch.add("gpsimd",
                            lambda m, dp_=dp_: m.dma_start(
                                out=yp_s[dp_][0], in_=yp_s[dp_][1], accum_op=ALU.add),
                            reads=[buf(f"yp{dp_}1"), buf(f"yp{dp_}0")],
                            writes=[buf(f"yp{dp_}0")], dma=True, tag="ycomb")
                    sch.add("gpsimd",
                            lambda m, dp_=dp_: m.dma_start(
                                out=yp_s[dp_][2], in_=yp_s[dp_][3], accum_op=ALU.add),
                            reads=[buf(f"yp{dp_}3"), buf(f"yp{dp_}2")],
                            writes=[buf(f"yp{dp_}2")], dma=True, tag="ycomb")
                    sch.add("gpsimd",
                            lambda m, dp_=dp_: m.dma_start(
                                out=yp_s[dp_][0], in_=yp_s[dp_][2], accum_op=ALU.add),
                            reads=[buf(f"yp{dp_}2"), buf(f"yp{dp_}0")],
                            writes=[buf(f"yp{dp_}0")], dma=True, tag="ycomb")

                    def emit_gate(b=b, dtile=dtile, dp_=dp_, c0=c0,
                                  bxc=bxc, bsz=bsz):
                        sch.add("vector",
                                lambda m, dtile=dtile, dp_=dp_:
                                m.scalar_tensor_tensor(
                                    out=yf_s[dp_], in0=xc_s[dp_],
                                    scalar=dp_sb[:, dtile:dtile + 1],
                                    in1=yp_s[dp_][0], op0=ALU.mult, op1=ALU.add),
                                reads=[bxc, buf(f"yp{dp_}0"), buf("c_dp2")],
                                writes=[buf(f"yf{dp_}")], tag="yfull")
                        sch.add("vector",
                                lambda m, dp_=dp_: m.tensor_mul(
                                    g_bf[dp_][:, 0:T1B], yf_s[dp_], sz_s[dp_]),
                                reads=[buf(f"yf{dp_}"), bsz], writes=[buf(f"gbf{dp_}")],
                                tag="gate")
                        dma("sync",
                            sc_gated[b, dtile * 128:(dtile + 1) * 128, c0:c0 + T1B],
                            g_bf[dp_][:, 0:T1B], reads=[buf(f"gbf{dp_}")], tag="stgated")

                    if pending_gate is not None:
                        pending_gate()
                    pending_gate = emit_gate

        if pending_gate is not None:
            pending_gate()
            pending_gate = None
        for b in range(B):
            for dtile in range(NDT):
                scol = srun_sb[:, b * NDT + dtile:b * NDT + dtile + 1]
                sch.add("scalar",
                        lambda m, s=scol, dtile=dtile, b=b: m.activation(
                            out=D_sb[:, (b * NDT + dtile) * NS:
                                     (b * NDT + dtile + 1) * NS],
                            in_=A_sb[:, dtile * NS:(dtile + 1) * NS], func=AF.Exp,
                            bias=0.0, scale=s),
                        reads=[buf("c_srun"), buf("c_a_log2")], writes=[buf("c_D")],
                        tag="Dexp")

        sch.barrier()

        ar.phase_reset()
        wo_bf = ar.alloc(NDT * DM // 2, BF16, "wo")
        Cbc2 = [ar.alloc(NS * T2A // 2, BF16, "Cbc2") for _ in range(2)]
        S_2 = [ar.alloc(T2A, name="S2a") for _ in range(2)]
        sz_2 = [ar.alloc(T2A, name="sz2a") for _ in range(2)]
        Pc_2 = [ar.alloc(T2A // 2, BF16, "Pc") for _ in range(4)]
        w_2 = [ar.alloc(T2A // 2, BF16, "w2a") for _ in range(2)]
        corr_2 = ar.alloc(T2A, name="corr")
        corrB_2 = ar.alloc(T2A, name="corrB")
        gat_bf = [ar.alloc(T2A // 2, BF16, "gat2a") for _ in range(2)]
        gs_bf = ar.alloc(NDT * T2A // 2, BF16, "gs")
        r1_f = ar.alloc(NKM * T2A, name="r1")
        xt_f = ar.alloc(NKM * T2A, name="xt2a")
        sq_f = ar.alloc(T2A, name="sq2a")
        xn1_f = ar.alloc(NKM * T2A, name="xn1")
        row_f = ar.alloc(3 * T2A, name="rows")

        for k in range(NDT):
            dma("gpsimd", wo_bf[:, k * DM:(k + 1) * DM],
                dram["w_outT"][k * 128:(k + 1) * 128, :], writes=[buf("w_wo")],
                tag="ldwo")
        cc_in, cc_out = dram["cc_in"], dram["cc_out"]
        dma("gpsimd", cc_in[:, 0:512], hst_sb, reads=[buf("c_hst")],
            writes=[buf("ccin")], tag="ccF")
        dma("gpsimd", cc_in[:, 512:1024], D_sb, reads=[buf("c_D")],
            writes=[buf("ccin")], tag="ccD")
        sch.add("gpsimd",
                lambda m: m.collective_compute(
                    "AllGather", ALU.bypass, replica_groups=[list(range(NCORES))],
                    ins=[cc_in.opt()], outs=[cc_out.opt()]),
                reads=[buf("ccin")], writes=[buf("ccout")], tag="allgather")
        sch.add("vector", lambda m: m.memset(h0_sb, 0.0), writes=[buf("c_h0")])
        for j in range(NCORES):
            bfd = buf("fdj")
            dma("sync", FD_j, cc_out[j * 128:(j + 1) * 128, :],
                reads=[buf("ccout")], writes=[bfd], tag="ldfd")
            mcol = rk_sb[:, j:j + 1]
            sch.add("vector",
                    lambda m, mc=mcol, j=j: m.scalar_tensor_tensor(
                        out=FD_j[:, 512:1024], in0=FD_j[:, 512:1024], scalar=mc,
                        in1=rkc_sb[:, j:j + 1].broadcast_to([128, 512]),
                        op0=ALU.mult, op1=ALU.add),
                    reads=[bfd, buf("c_rkmask"), buf("c_rkc")], writes=[bfd],
                    strict=True, tag="dm")
            sch.add("vector",
                    lambda m: m.tensor_mul(h0_sb, h0_sb, FD_j[:, 512:1024]),
                    reads=[bfd, buf("c_h0")], writes=[buf("c_h0")], strict=True, tag="h0d")
            sch.add("vector",
                    lambda m, mc=mcol: m.scalar_tensor_tensor(
                        out=h0_sb, in0=FD_j[:, 0:512], scalar=mc, in1=h0_sb,
                        op0=ALU.mult, op1=ALU.add),
                    reads=[bfd, buf("c_rkmask"), buf("c_h0")], writes=[buf("c_h0")],
                    strict=True, tag="h0f")

    NT2A = LC // T2A
    if stop_phase >= 3:
        def ldc2(b, blk):
            c0 = blk * T2A
            bp_ = blk % 2
            for n in range(NS):
                dma("sync", Cbc2[bp_][:, n * T2A:(n + 1) * T2A],
                    sc_BC[b, NS + n, c0:c0 + T2A]
                    .unsqueeze(0).partition_broadcast(128),
                    writes=[buf(f"cbc2{bp_}")], tag="ldc2")

        blocks2a = [(b, blk) for b in range(B) for blk in range(NT2A)]
        ldc2(*blocks2a[0])
        for bi, (b, blk) in enumerate(blocks2a):
                c0 = blk * T2A
                bp_ = blk % 2
                bcbc2 = buf(f"cbc2{bp_}")
                for dtile in range(NDT):
                    dp_ = dtile % 2
                    bS = buf(f"S2a{dp_}")
                    bsz = buf(f"sz2a{dp_}")
                    bgat = buf(f"gat2a{dp_}")
                    dma("scalar", S_2[dp_],
                        sc_S[b, dtile * 128:(dtile + 1) * 128, c0:c0 + T2A],
                        writes=[bS], tag="ldS")
                    dma("scalar", sz_2[dp_],
                        sc_sz[b, dtile * 128:(dtile + 1) * 128, c0:c0 + T2A],
                        writes=[bsz], tag="ldsz2")
                    dma("scalar", gat_bf[dp_][:, 0:T2A],
                        sc_gated[b, dtile * 128:(dtile + 1) * 128, c0:c0 + T2A],
                        writes=[bgat], tag="ldgat")
                    for n in range(NS):
                        np_ = n % 2
                        pp4 = n % 4
                        acol = A_sb[:, dtile * NS + n:dtile * NS + n + 1]
                        hcol = h0_sb[:, (b * NDT + dtile) * NS + n:
                                     (b * NDT + dtile) * NS + n + 1]
                        bPc = buf(f"Pc{pp4}")
                        bw = buf(f"w2a{np_}")
                        sch.add("scalar",
                                lambda m, a=acol, dp_=dp_, pp4=pp4: m.activation(
                                    out=Pc_2[pp4], in_=S_2[dp_], func=AF.Exp,
                                    bias=0.0, scale=a),
                                reads=[bS, buf("c_a_log2")], writes=[bPc],
                                tag="Pcexp")
                        tgt = (corr_2, corrB_2)[n % 2] if n < 2 else w_2[np_]
                        btg = (buf("corr"), buf("corrB"))[n % 2] if n < 2 else bw
                        sch.add("vector",
                                lambda m, h=hcol, n=n, o=tgt, pp4=pp4, bp_=bp_:
                                m.scalar_tensor_tensor(
                                    out=o, in0=Pc_2[pp4], scalar=h,
                                    in1=Cbc2[bp_][:, n * T2A:(n + 1) * T2A],
                                    op0=ALU.mult, op1=ALU.mult),
                                reads=[bPc, buf("c_h0"), bcbc2],
                                writes=[btg], tag="wstt")
                        if n >= 2:
                            acc, bacc = ((corr_2, buf("corr")) if n % 2 == 0
                                         else (corrB_2, buf("corrB")))
                            sch.add("vector",
                                    lambda m, np_=np_, acc=acc: m.tensor_add(
                                        acc, acc, w_2[np_]),
                                    reads=[bw, bacc], writes=[bacc],
                                    strict=True, tag="corradd")
                    sch.add("vector",
                            lambda m: m.tensor_add(corr_2, corr_2, corrB_2),
                            reads=[buf("corrB"), buf("corr")], writes=[buf("corr")],
                            strict=True, tag="corrfin")
                    sch.add("vector",
                            lambda m, dp_=dp_: m.tensor_mul(corr_2, corr_2, sz_2[dp_]),
                            reads=[buf("corr"), bsz], writes=[buf("corr")],
                            strict=True, tag="corrsz")
                    sch.add("vector",
                            lambda m, dtile=dtile, dp_=dp_: m.tensor_add(
                                gs_bf[:, dtile * T2A:(dtile + 1) * T2A],
                                gat_bf[dp_][:, 0:T2A], corr_2),
                            reads=[buf("corr"), bgat], writes=[buf(f"gs{dtile}")],
                            strict=True, tag="gs")
                if bi + 1 < len(blocks2a):
                    ldc2(*blocks2a[bi + 1])
                for k in range(NKM):
                    dma("sync", xt_f[:, k * T2A:(k + 1) * T2A],
                        xT[b, k * 128:(k + 1) * 128, 3 + c0:3 + c0 + T2A],
                        writes=[buf("xt2a")], tag="ldxt2")
                for k in range(NDT):
                    for mo in range(NKM):
                        mm(bank[mo], wo_bf[:, k * DM + mo * 128:k * DM + (mo + 1) * 128],
                           gs_bf[:, k * T2A:(k + 1) * T2A], k == 0, k == NDT - 1,
                           reads=[buf(f"gs{k}"), buf("w_wo")], writes=[buf(f"pb{mo}")],
                           tag="wout")
                for mo in range(NKM):
                    sch.add("vector",
                            lambda m, mo=mo: m.tensor_add(
                                r1_f[:, mo * T2A:(mo + 1) * T2A], bank[mo],
                                xt_f[:, mo * T2A:(mo + 1) * T2A]),
                            reads=[buf(f"pb{mo}"), buf("xt2a")], writes=[buf("r1")],
                            tag="r1")
                for k in range(NKM):
                    mm(bank[4][0:1, 0:T2A], onesc_sb,
                       r1_f[:, k * T2A:(k + 1) * T2A], k == 0, k == NKM - 1,
                       reads=[buf("r1"), buf("c_onesc")], writes=[buf("pb4")], tag="ln1m")
                for k in range(NKM):
                    sch.add("scalar",
                            lambda m, k=k: m.activation(
                                out=sq_f, in_=r1_f[:, k * T2A:(k + 1) * T2A],
                                func=AF.Square, bias=0.0, scale=1.0),
                            reads=[buf("r1")], writes=[buf("sq2a")], tag="ln1sq")
                    mm(bank[5][0:1, 0:T2A], onesc_sb, sq_f, k == 0, k == NKM - 1,
                       reads=[buf("sq2a"), buf("c_onesc")], writes=[buf("pb5")],
                       tag="ln1v")
                mrow = row_f[0:1, 0:T2A]
                vrow = row_f[0:1, T2A:2 * T2A]
                rrow = row_f[0:1, 2 * T2A:3 * T2A]
                sch.add("vector",
                        lambda m: m.tensor_scalar(out=mrow, in0=bank[4][0:1, 0:T2A],
                                                  scalar1=1.0 / DM, scalar2=None,
                                                  op0=ALU.mult),
                        reads=[buf("pb4")], writes=[buf("rows")], tag="mean")
                sch.add("vector",
                        lambda m: m.tensor_scalar(out=vrow, in0=bank[5][0:1, 0:T2A],
                                                  scalar1=1.0 / DM, scalar2=None,
                                                  op0=ALU.mult),
                        reads=[buf("pb5")], writes=[buf("rows")], strict=True, tag="var0")
                sch.add("vector",
                        lambda m: m.tensor_mul(rrow, mrow, mrow),
                        reads=[buf("rows")], writes=[buf("rows")], strict=True, tag="m2")
                sch.add("vector",
                        lambda m: m.tensor_sub(vrow, vrow, rrow),
                        reads=[buf("rows")], writes=[buf("rows")], strict=True, tag="var")
                sch.add("scalar",
                        lambda m: m.activation(out=rrow, in_=vrow, func=AF.Ln,
                                               bias=epsc_sb[0:1, :], scale=1.0),
                        reads=[buf("rows"), buf("c_eps")], writes=[buf("rows")], strict=True, tag="lnv")
                sch.add("scalar",
                        lambda m: m.activation(out=rrow, in_=rrow, func=AF.Exp,
                                               bias=0.0, scale=-0.5),
                        reads=[buf("rows")], writes=[buf("rows")], strict=True, tag="rstd")
                mm(bank[6][:, 0:T2A], onesr_sb[0:1, :], mrow, True, True,
                   reads=[buf("rows"), buf("c_onesr")], writes=[buf("pb6")], tag="mbc")
                mm(bank[7][:, 0:T2A], onesr_sb[0:1, :], rrow, True, True,
                   reads=[buf("rows"), buf("c_onesr")], writes=[buf("pb7")], tag="rbc")
                for k in range(NKM):
                    sch.add("vector",
                            lambda m, k=k: m.tensor_sub(
                                xn1_f[:, k * T2A:(k + 1) * T2A],
                                r1_f[:, k * T2A:(k + 1) * T2A], bank[6][:, 0:T2A]),
                            reads=[buf("r1"), buf("pb6")], writes=[buf("xn1")], tag="xm")
                    sch.add("vector",
                            lambda m, k=k: m.tensor_mul(
                                xn1_f[:, k * T2A:(k + 1) * T2A],
                                xn1_f[:, k * T2A:(k + 1) * T2A], bank[7][:, 0:T2A]),
                            reads=[buf("xn1"), buf("pb7")], writes=[buf("xn1")],
                            strict=True, tag="xr")
                    dma("sync", sc_xn1f[b, k * 128:(k + 1) * 128, c0:c0 + T2A],
                        xn1_f[:, k * T2A:(k + 1) * T2A], reads=[buf("xn1")], tag="stxn1f")
                    dma("gpsimd", sc_xn1b[b, k * 128:(k + 1) * 128, c0:c0 + T2A],
                        xn1_f[:, k * T2A:(k + 1) * T2A], reads=[buf("xn1")], tag="stxn1b")

        sch.barrier()

    ar.phase_reset()
    wfc_bf = [ar.alloc(NKM * IM // 2, BF16, "wfc") for _ in range(2)]
    wpj_bf = ar.alloc(NEXP * 4 * DM // 2, BF16, "wpj")
    wg_bf = ar.alloc(NKM * NEXP // 2 + 8, BF16, "wg")
    xn1b_s = [ar.alloc(NKM * T2B // 2, BF16, "xn1b2b") for _ in range(2)]
    xn1f_s = ar.alloc(T2B, name="xn1f2b")
    rt_f = [ar.alloc(T2B, name="rtf")] * 2
    rtT_f = ar.alloc(NEXP, name="rtT")
    gcol = ar.alloc(8, name="gcols")
    gmask = ar.alloc(4 * NEXP, name="gmask")
    gT_f = [ar.alloc(T2B, name="gT")] * 2
    gstage = ar.alloc(T2B + 8, name="gstage")
    gbc_f = [ar.alloc(NEXP * T2B // 2, BF16, "gbc") for _ in range(2)]
    hg_bf = ar.alloc(4 * NEXP * T2B // 2, BF16, "hg")
    r2_f = [ar.alloc(NKM * T2B, name="r2") for _ in range(2)]
    r2T_f = ar.alloc(DM, name="r2T")
    sq2_f = ar.alloc(DM, name="sq2b")
    st_c = ar.alloc(8, name="stats2b")
    out_f = ar.alloc(DM, name="out2b")

    out_d = dram["out"]
    NT2B = LC // T2B
    if stop_phase >= 4:
        for e in range(NEXP):
            for k in range(4):
                dma("gpsimd", wpj_bf[:, (e * 4 + k) * DM:(e * 4 + k + 1) * DM],
                    dram["w_projT"][e, k * 128:(k + 1) * 128, :], writes=[buf("w_pj")],
                    tag="ldwpj")
        for k in range(NKM):
            dma("gpsimd", wg_bf[:, k * NEXP:(k + 1) * NEXP],
                dram["w_gateT"][k * 128:(k + 1) * 128, :], writes=[buf("w_wg")],
                tag="ldwg")

        m1c = gcol[:, 0:1]
        nm1c = gcol[:, 1:2]
        m2c = gcol[:, 2:3]
        e2c = gcol[:, 3:4]
        w1c = gcol[:, 4:5]
        w2c = gcol[:, 5:6]
        scol2 = st_c[:, 0:1]
        vcol2 = st_c[:, 1:2]
        rcol2 = st_c[:, 2:3]
        mcol2 = st_c[:, 3:4]

        def emit_ln2(b, blk, qp_):
            c0 = blk * T2B
            for sub in range(T2B // 128):
                sc0 = sub * 128
                for k in range(NKM):
                    pbk = bank[k % 2]
                    bpb = buf(f"pb{k % 2}")
                    sch.add("tensor",
                            lambda m, k=k, sc0=sc0, p=pbk, qp_=qp_: m.transpose(
                                p[:, 0:128],
                                r2_f[qp_][:, k * T2B + sc0:k * T2B + sc0 + 128], id_sb),
                            reads=[buf(f"r2{qp_}"), buf("c_ident")], writes=[bpb],
                            tag="trans2")
                    sch.add("scalar",
                            lambda m, k=k, p=pbk: m.activation(
                                out=r2T_f[:, k * 128:(k + 1) * 128], in_=p[:, 0:128],
                                func=AF.Copy, bias=0.0, scale=1.0),
                            reads=[bpb], writes=[buf("r2T")], tag="trev")
                sch.add("vector",
                        lambda m: m.tensor_reduce(out=scol2, in_=r2T_f, axis=AX.X,
                                                  op=ALU.add),
                        reads=[buf("r2T")], writes=[buf("st2")], strict=True,
                        tag="sum2")
                sch.add("vector",
                        lambda m: m.tensor_scalar(out=mcol2, in0=scol2,
                                                  scalar1=1.0 / DM, scalar2=None,
                                                  op0=ALU.mult),
                        reads=[buf("st2")], writes=[buf("st2")], strict=True,
                        tag="mean2")
                sch.add("scalar",
                        lambda m: m.activation(out=sq2_f, in_=r2T_f, func=AF.Square,
                                               bias=0.0, scale=1.0),
                        reads=[buf("r2T")], writes=[buf("sq2b")], tag="sq2")
                sch.add("vector",
                        lambda m: m.tensor_reduce(out=vcol2, in_=sq2_f, axis=AX.X,
                                                  op=ALU.add),
                        reads=[buf("sq2b")], writes=[buf("st2")], strict=True,
                        tag="sumsq2")
                sch.add("vector",
                        lambda m: m.tensor_scalar(out=vcol2, in0=vcol2,
                                                  scalar1=1.0 / DM, scalar2=None,
                                                  op0=ALU.mult),
                        reads=[buf("st2")], writes=[buf("st2")], strict=True, tag="v2")
                sch.add("vector",
                        lambda m: m.tensor_mul(rcol2, mcol2, mcol2),
                        reads=[buf("st2")], writes=[buf("st2")], strict=True,
                        tag="m22")
                sch.add("vector",
                        lambda m: m.tensor_sub(vcol2, vcol2, rcol2),
                        reads=[buf("st2")], writes=[buf("st2")], strict=True,
                        tag="var2")
                sch.add("scalar",
                        lambda m: m.activation(out=rcol2, in_=vcol2, func=AF.Ln,
                                               bias=epsc_sb, scale=1.0),
                        reads=[buf("st2"), buf("c_eps")], writes=[buf("st2")],
                        strict=True, tag="lnv2")
                sch.add("scalar",
                        lambda m: m.activation(out=rcol2, in_=rcol2, func=AF.Exp,
                                               bias=0.0, scale=-0.5),
                        reads=[buf("st2")], writes=[buf("st2")], strict=True,
                        tag="rstd2")
                sch.add("vector",
                        lambda m: m.tensor_scalar(
                            out=out_f, in0=r2T_f, scalar1=mcol2, scalar2=rcol2,
                            op0=ALU.subtract, op1=ALU.mult),
                        reads=[buf("r2T"), buf("st2")], writes=[buf("outb")],
                        strict=True, tag="xn2")
                dma("sync", out_d[b, c0 + sc0:c0 + sc0 + 128, :], out_f,
                    reads=[buf("outb")], tag="stout")

        pending_ln2 = None
        for b in range(B):
            for blk in range(NT2B):
                c0 = blk * T2B
                qp_ = blk % 2
                xn1b = xn1b_s[qp_]
                bxn1b = buf(f"xn1b2b{qp_}")
                rtv = rt_f[qp_]
                brt = buf("rtf")
                gTv = gT_f[qp_]
                bgT = buf("gTf")
                gbcv = gbc_f[qp_]
                bgbc = buf(f"gbc{qp_}")
                for k in range(NKM):
                    dma("scalar", xn1b[:, k * T2B:(k + 1) * T2B],
                        sc_xn1b[b, k * 128:(k + 1) * 128, c0:c0 + T2B],
                        writes=[bxn1b], tag="ldxn1b")
                for k in range(NKM):
                    mm(bank[0][0:NEXP, 0:T2B], wg_bf[:, k * NEXP:(k + 1) * NEXP],
                       xn1b[:, k * T2B:(k + 1) * T2B], k == 0, k == NKM - 1,
                       reads=[bxn1b, buf("w_wg")], writes=[buf("pb0")],
                       tag="router")
                sch.add("vector",
                        lambda m, rtv=rtv: m.tensor_copy(rtv[0:NEXP, 0:T2B],
                                                         bank[0][0:NEXP, 0:T2B]),
                        reads=[buf("pb0")], writes=[brt], tag="rtev")
                for sub in range(T2B // 128):
                    sc0 = sub * 128
                    mm(bank[1][:, 0:NEXP], rtv[0:NEXP, sc0:sc0 + 128],
                       id_sb[0:NEXP, 0:NEXP], True, True,
                       reads=[brt, buf("c_ident")], writes=[buf("pb1")], tag="rtT")
                    sch.add("vector",
                            lambda m: m.tensor_copy(rtT_f[:, 0:NEXP], bank[1][:, 0:NEXP]),
                            reads=[buf("pb1")], writes=[buf("rtT")], tag="rtTev")
                    sch.add("vector",
                            lambda m: m.tensor_reduce(out=m1c, in_=rtT_f[:, 0:NEXP],
                                                      axis=AX.X, op=ALU.max),
                            reads=[buf("rtT")], writes=[buf("gcols")], strict=True,
                            tag="m1")
                    sch.add("vector",
                            lambda m: m.tensor_single_scalar(
                                out=gmask[:, 0:NEXP], in_=rtT_f[:, 0:NEXP], scalar=m1c,
                                op=ALU.is_ge),
                            reads=[buf("rtT"), buf("gcols")], writes=[buf("gmask")],
                            strict=True, tag="ismax")
                    sch.add("vector",
                            lambda m: m.scalar_tensor_tensor(
                                out=gmask[:, NEXP:2 * NEXP], in0=gmask[:, 0:NEXP],
                                scalar=-1e30, in1=rtT_f[:, 0:NEXP],
                                op0=ALU.mult, op1=ALU.add),
                            reads=[buf("gmask"), buf("rtT")], writes=[buf("gmask")],
                            strict=True, tag="maskmax")
                    sch.add("vector",
                            lambda m: m.tensor_reduce(out=m2c,
                                                      in_=gmask[:, NEXP:2 * NEXP],
                                                      axis=AX.X, op=ALU.max),
                            reads=[buf("gmask")], writes=[buf("gcols")], strict=True,
                            tag="m2")
                    sch.add("vector",
                            lambda m: m.tensor_scalar(out=nm1c, in0=m1c, scalar1=-1.0,
                                                      scalar2=None, op0=ALU.mult),
                            reads=[buf("gcols")], writes=[buf("gcols")], strict=True,
                            tag="nm1")
                    sch.add("scalar",
                            lambda m: m.activation(out=e2c, in_=m2c, func=AF.Exp,
                                                   bias=nm1c, scale=1.0),
                            reads=[buf("gcols")], writes=[buf("gcols")], strict=True,
                            tag="e2")
                    sch.add("vector",
                            lambda m: m.tensor_scalar(out=w1c, in0=e2c, scalar1=1.0,
                                                      scalar2=None, op0=ALU.add),
                            reads=[buf("gcols")], writes=[buf("gcols")], strict=True,
                            tag="den")
                    sch.add("vector",
                            lambda m: m.reciprocal(out=w1c, in_=w1c),
                            reads=[buf("gcols")], writes=[buf("gcols")], strict=True,
                            tag="w1")
                    sch.add("vector",
                            lambda m: m.tensor_mul(w2c, e2c, w1c),
                            reads=[buf("gcols")], writes=[buf("gcols")], strict=True,
                            tag="w2")
                    sch.add("vector",
                            lambda m: m.tensor_single_scalar(
                                out=gmask[:, 2 * NEXP:3 * NEXP],
                                in_=gmask[:, NEXP:2 * NEXP], scalar=m2c, op=ALU.is_ge),
                            reads=[buf("gmask"), buf("gcols")], writes=[buf("gmask")],
                            strict=True, tag="ismax2")
                    sch.add("vector",
                            lambda m: m.tensor_single_scalar(
                                out=gmask[:, 0:NEXP], in_=gmask[:, 0:NEXP], scalar=w1c,
                                op=ALU.mult),
                            reads=[buf("gmask"), buf("gcols")], writes=[buf("gmask")],
                            strict=True, tag="gw1")
                    sch.add("vector",
                            lambda m: m.scalar_tensor_tensor(
                                out=gmask[:, 3 * NEXP:4 * NEXP],
                                in0=gmask[:, 2 * NEXP:3 * NEXP], scalar=w2c,
                                in1=gmask[:, 0:NEXP], op0=ALU.mult, op1=ALU.add),
                            reads=[buf("gmask"), buf("gcols")], writes=[buf("gmask")],
                            strict=True, tag="gates")
                    mm(bank[2][0:NEXP, 0:128], gmask[:, 3 * NEXP:4 * NEXP],
                       id_sb[:, 0:128], True, True,
                       reads=[buf("gmask"), buf("c_ident")], writes=[buf("pb2")],
                       tag="gTmm")
                    sch.add("vector",
                            lambda m, sc0=sc0, gTv=gTv: m.tensor_copy(
                                gTv[0:NEXP, sc0:sc0 + 128], bank[2][0:NEXP, 0:128]),
                            reads=[buf("pb2")], writes=[bgT], tag="gTev")
                for e in range(NEXP):
                    stg = gstage[0:1, 0:T2B]
                    bst = buf("gst")
                    dma("sync", stg, gTv[e:e + 1, 0:T2B], reads=[bgT],
                        writes=[bst], tag="gstage")
                    mm(bank[3][:, 0:T2B], onesr_sb[0:1, :], stg, True, True,
                       reads=[bst, buf("c_onesr")], writes=[buf("pb3")], tag="gbcmm")
                    sch.add("scalar",
                            lambda m, e=e, gbcv=gbcv: m.activation(
                                out=gbcv[:, e * T2B:(e + 1) * T2B], in_=bank[3][:, 0:T2B],
                                func=AF.Copy, bias=0.0, scale=1.0),
                            reads=[buf("pb3")], writes=[bgbc], tag="gbcev")
                for e in range(NEXP):
                    ep_ = e % 2
                    bwfc = buf(f"w_fc{ep_}")
                    dma("gpsimd",
                        wfc_bf[ep_][:, 0:NKM * IM].rearrange("p (k i) -> p k i", i=IM),
                        dram["w_fcT"][e].rearrange("(k p) i -> p k i", p=128),
                        writes=[bwfc], tag="ldwfc")
                    for mi in range(4):
                        pbk = bank[4 + (e * 4 + mi) % 2]
                        bpb = buf(f"pb{4 + (e * 4 + mi) % 2}")
                        for k in range(NKM):
                            mm(pbk[:, 0:T2B],
                               wfc_bf[ep_][:, k * IM + mi * 128:k * IM + (mi + 1) * 128],
                               xn1b[:, k * T2B:(k + 1) * T2B], k == 0, k == NKM - 1,
                               reads=[bxn1b, bwfc], writes=[bpb], tag="fc")
                        hslot = e * 4 + mi
                        sch.add("scalar",
                                lambda m, p=pbk, hs=hslot: m.activation(
                                    out=hg_bf[:, hs * T2B:(hs + 1) * T2B],
                                    in_=p[:, 0:T2B], func=AF.Gelu, bias=0.0, scale=1.0),
                                reads=[bpb], writes=[buf("hg")], tag="gelu")
                        sch.add("vector",
                                lambda m, hs=hslot, e=e, gbcv=gbcv: m.tensor_mul(
                                    hg_bf[:, hs * T2B:(hs + 1) * T2B],
                                    hg_bf[:, hs * T2B:(hs + 1) * T2B],
                                    gbcv[:, e * T2B:(e + 1) * T2B]),
                                reads=[buf("hg"), bgbc], writes=[buf("hg")],
                                strict=True, tag="hgmul")
                for mo in range(NKM):
                    pbk = bank[6 + mo % 2]
                    bpb = buf(f"pb{6 + mo % 2}")
                    first = True
                    for e in range(NEXP):
                        for ki in range(4):
                            mm(pbk[:, 0:T2B],
                               wpj_bf[:, (e * 4 + ki) * DM + mo * 128:
                                      (e * 4 + ki) * DM + (mo + 1) * 128],
                               hg_bf[:, (e * 4 + ki) * T2B:(e * 4 + ki + 1) * T2B],
                               first, (e == NEXP - 1 and ki == 3),
                               reads=[buf("hg"), buf("w_pj")], writes=[bpb], tag="proj")
                            first = False
                    dma("scalar", xn1f_s,
                        sc_xn1f[b, mo * 128:(mo + 1) * 128, c0:c0 + T2B],
                        writes=[buf("xn1f2b")], tag="ldxn1f")
                    sch.add("vector",
                            lambda m, mo=mo, p=pbk, qp_=qp_: m.tensor_add(
                                r2_f[qp_][:, mo * T2B:(mo + 1) * T2B], p[:, 0:T2B],
                                xn1f_s),
                            reads=[bpb, buf("xn1f2b")], writes=[buf(f"r2{qp_}")],
                            tag="r2")
                if pending_ln2 is not None:
                    pending_ln2()
                pending_ln2 = (lambda b=b, blk=blk, qp_=qp_: emit_ln2(b, blk, qp_))
        if pending_ln2 is not None:
            pending_ln2()

        sch.barrier()

    sch.finalize(nc, stack)
    block = stack.enter_context(nc.Block())

    @block.sync
    def _(m):
        sch.emit_engine(m, "sync")

    @block.tensor
    def _(m):
        sch.emit_engine(m, "tensor")

    @block.vector
    def _(m):
        sch.emit_engine(m, "vector")

    @block.scalar
    def _(m):
        sch.emit_engine(m, "scalar")

    @block.gpsimd
    def _(m):
        sch.emit_engine(m, "gpsimd")


# version 25
# speedup vs baseline: 1.0214x; 1.0214x over previous
import contextlib
import numpy as np
import concourse.bass as bass
import concourse.mybir as mybir
from concourse.bass_utils import run_bass_kernel_spmd

F32 = mybir.dt.float32
BF16 = mybir.dt.bfloat16
AF = mybir.ActivationFunctionType
ALU = mybir.AluOpType
AX = mybir.AxisListType

NCORES = 8
B = 2
L = 16384
LC = L // NCORES
DM = 1024
DI = 2048
E4 = 2 * DI
NS = 16
RK = 64
NEXP = 8
IM = 512
NDT = DI // 128
NKM = DM // 128

T1A = 512
T1B = 1024
T2A = 512
T2B = 512
EPS = 1e-5

ENGINES = ("sync", "tensor", "vector", "scalar", "gpsimd")


class _Op:
    __slots__ = ("eng", "emit", "deps", "inc", "count", "used", "strict", "tag",
                 "semkey")

    def __init__(self, eng, emit, deps, inc, strict, tag):
        self.eng = eng
        self.emit = emit
        self.deps = deps
        self.inc = inc
        self.count = 0
        self.used = False
        self.strict = strict
        self.tag = tag
        self.semkey = eng


class Buf:
    __slots__ = ("name", "writer", "readers")

    def __init__(self, name=""):
        self.name = name
        self.writer = None
        self.readers = []


NDMALANES = 8


class Sched:
    def __init__(self):
        self.q = {e: [] for e in ENGINES}
        self.all = []
        self._dma_idx = {e: 0 for e in ENGINES}

    def add(self, eng, emit, reads=(), writes=(), deps=(), dma=False,
            strict=False, tag=""):
        d = []
        for b in reads:
            if b.writer is not None:
                d.append(b.writer)
        for b in writes:
            if b.writer is not None:
                d.append(b.writer)
            d.extend(b.readers)
        d.extend(x for x in deps if x is not None)
        op = _Op(eng, emit, d, 16 if dma else 1, strict, tag)
        if dma:
            lane = self._dma_idx[eng] % NDMALANES
            self._dma_idx[eng] += 1
            op.semkey = (eng, lane)
        for b in reads:
            b.readers.append(op)
        for b in writes:
            b.writer = op
            b.readers = []
        self.q[eng].append(op)
        self.all.append(op)
        return op

    def barrier(self):
        last = {}
        for e in ENGINES:
            for op in reversed(self.q[e]):
                if op.emit is None:
                    continue
                if op.semkey not in last:
                    last[op.semkey] = op
                if op.semkey == e:
                    break
        for e in ENGINES:
            deps = [op for op in last.values() if op.eng != e]
            deps += [op for op in last.values()
                     if op.eng == e and op.semkey != e]
            self.add(e, None, deps=deps, strict=True, tag="barrier")

    def finalize(self, nc, stack):
        pos = {}
        for e in ENGINES:
            for i, op in enumerate(self.q[e]):
                pos[id(op)] = i
        for op in self.all:
            if op.inc == 16 and op.emit is not None:
                op.used = True
            wd = []
            for dp in op.deps:
                if dp.eng != op.eng or dp.semkey != dp.eng:
                    dp.used = True
                    wd.append(dp)
                elif op.strict or pos[id(dp)] >= pos[id(op)] - 2:
                    if dp.emit is not None:
                        dp.used = True
                        wd.append(dp)
            op.deps = wd
        self.sems = {}
        counts = {}
        for e in ENGINES:
            for op in self.q[e]:
                if op.used:
                    assert op.emit is not None, f"dep on no-op ({op.tag})"
                    counts[op.semkey] = counts.get(op.semkey, 0) + op.inc
                op.count = counts.get(op.semkey, 0)
        for k, c in counts.items():
            assert c < 60000, f"semaphore overflow on {k}: {c}"
        for e in ENGINES:
            self.sems[e] = stack.enter_context(nc.semaphore(f"s_{e}"))
            for ln in range(NDMALANES):
                self.sems[(e, ln)] = stack.enter_context(
                    nc.semaphore(f"s_{e}_d{ln}"))

    def emit_engine(self, engmod, e):
        sems = self.sems
        waited = {}
        for op in self.q[e]:
            for dp in op.deps:
                if dp.count > waited.get(dp.semkey, 0):
                    engmod.wait_ge(sems[dp.semkey], dp.count)
                    waited[dp.semkey] = dp.count
            if op.emit is not None:
                inst = op.emit(engmod)
                if op.used:
                    inst.then_inc(sems[op.semkey], op.inc)

    def stats(self):
        return {e: (len(self.q[e]), self.q[e][-1].count if self.q[e] else 0)
                for e in ENGINES}


def _split_shards(x):
    xt = np.ascontiguousarray(np.swapaxes(np.asarray(x, np.float32), 1, 2))
    shards = []
    for i in range(NCORES):
        t0 = i * LC
        sl = np.zeros((B, DM, 3 + LC), np.float32)
        sl[:, :, 3:] = xt[:, :, t0:t0 + LC]
        if t0 >= 3:
            sl[:, :, 0:3] = xt[:, :, t0 - 3:t0]
        shards.append(np.ascontiguousarray(sl))
    return shards


def _vec2d(v):
    return np.ascontiguousarray(np.asarray(v, np.float32).reshape(NDT, 128).T)


def prepare_in_maps(L0_attn_x, L0_attn_W_in, L0_attn_conv_w, L0_attn_conv_b,
                    L0_attn_W_x, L0_attn_W_dt, L0_attn_b_dt, L0_attn_A_log,
                    L0_attn_Dp, L0_attn_W_out, L0_ffn_w_gate, L0_ffn_w_fc,
                    L0_ffn_w_proj):
    f32 = np.float32
    xsh = _split_shards(L0_attn_x)

    w_inT = np.ascontiguousarray(np.asarray(L0_attn_W_in, f32).T)
    w_xT = np.ascontiguousarray(np.asarray(L0_attn_W_x, f32).T)
    w_dtT = np.ascontiguousarray(np.asarray(L0_attn_W_dt, f32).T)
    w_outT = np.ascontiguousarray(np.asarray(L0_attn_W_out, f32).T)
    w_gateT = np.ascontiguousarray(np.asarray(L0_ffn_w_gate, f32).T)
    w_fcT = np.ascontiguousarray(np.swapaxes(np.asarray(L0_ffn_w_fc, f32), 1, 2))
    w_projT = np.ascontiguousarray(np.swapaxes(np.asarray(L0_ffn_w_proj, f32), 1, 2))

    cw2 = np.zeros((128, NDT * 4), f32)
    cwa = np.asarray(L0_attn_conv_w, f32)
    for dt in range(NDT):
        cw2[:, dt * 4:(dt + 1) * 4] = cwa[dt * 128:(dt + 1) * 128, :]
    a2 = np.zeros((128, NDT * NS), f32)
    al = np.asarray(L0_attn_A_log, f32)
    for dt in range(NDT):
        a2[:, dt * NS:(dt + 1) * NS] = al[dt * 128:(dt + 1) * 128, :]

    cb2v = _vec2d(L0_attn_conv_b)
    bdt2v = _vec2d(L0_attn_b_dt)
    dp2v = _vec2d(L0_attn_Dp)
    in_maps = []
    for i in range(NCORES):
        mask = np.zeros((128, 8), f32)
        mask[:, :i] = 1.0
        in_maps.append({
            "xT": xsh[i], "w_inT": w_inT, "w_xT": w_xT, "w_dtT": w_dtT,
            "w_outT": w_outT, "w_gateT": w_gateT, "w_fcT": w_fcT,
            "w_projT": w_projT, "conv_w2": cw2,
            "conv_b2": cb2v, "b_dt2": bdt2v,
            "dp2": dp2v, "a_log2": a2,
            "ident": np.eye(128, dtype=f32), "rkmask": mask,
        })
    return in_maps


def kernel(**inputs):
    in_maps = prepare_in_maps(**inputs)
    nc = build_program()
    res = run_bass_kernel_spmd(nc, in_maps, core_ids=list(range(NCORES)))
    out = np.empty((B, L, DM), np.float32)
    for i in range(NCORES):
        out[:, i * LC:(i + 1) * LC, :] = res.results[i]["out"]
    return out


class Arena:
    def __init__(self, nc, total_f32=53100):
        self.t = nc.alloc_sbuf_tensor("arena", [128, total_f32], F32)
        self.total = total_f32
        self.perm_off = 0
        self.off = 0

    def perm(self, n_f32, dtype=F32, name=""):
        ap = self._view(self.perm_off, n_f32, dtype)
        self.perm_off += n_f32
        self.off = max(self.off, self.perm_off)
        return ap

    def phase_reset(self):
        self.off = self.perm_off

    def alloc(self, n_f32, dtype=F32, name=""):
        ap = self._view(self.off, n_f32, dtype)
        self.off += n_f32
        assert self.off <= self.total, f"arena overflow at {name}: {self.off}"
        return ap

    def _view(self, off, n_f32, dtype):
        ap = self.t.ap()[:, off:off + n_f32]
        if dtype == BF16:
            ap = ap.bitcast(BF16)
        return ap


def build_program(stop_phase=99, dbg_out=False):
    nc = bass.Bass("TRN2", target_bir_lowering=False)
    dram = {}

    def din(name, shape, dtype=F32):
        dram[name] = nc.dram_tensor(name, list(shape), dtype, kind="ExternalInput").ap()

    def dint(name, shape, dtype=F32):
        kind = "ExternalOutput" if dbg_out else "Internal"
        dram[name] = nc.dram_tensor(name, list(shape), dtype, kind=kind).ap()

    din("xT", (B, DM, 3 + LC))
    din("w_inT", (DM, E4))
    din("w_xT", (DI, 96))
    din("w_dtT", (RK, DI))
    din("w_outT", (DI, DM))
    din("w_gateT", (DM, NEXP))
    din("w_fcT", (NEXP, DM, IM))
    din("w_projT", (NEXP, IM, DM))
    din("conv_w2", (128, NDT * 4))
    din("conv_b2", (128, NDT))
    din("b_dt2", (128, NDT))
    din("dp2", (128, NDT))
    din("a_log2", (128, NDT * NS))
    din("ident", (128, 128))
    din("rkmask", (128, NCORES))

    dram["out"] = nc.dram_tensor("out", [B, LC, DM], F32, kind="ExternalOutput").ap()

    dint("sc_xconv", (B, DI, LC))
    dint("sc_sz", (B, DI, LC))
    dint("sc_dt", (B, DI, LC))
    dint("sc_S", (B, DI, LC))
    dint("sc_BC", (B, 32, LC), BF16)
    dint("sc_gated", (B, DI, LC), BF16)
    dint("sc_xn1f", (B, DM, LC))
    dint("sc_xn1b", (B, DM, LC), BF16)
    if dbg_out:
        dram["dbg_bbc"] = nc.dram_tensor("dbg_bbc", [B, 2, 2, 128, NS * T1B], F32, kind="ExternalOutput").ap()
        dram["dbg_u"] = nc.dram_tensor("dbg_u", [B, DI, LC], F32, kind="ExternalOutput").ap()
        dram["dbg_y"] = nc.dram_tensor("dbg_y", [B, DI, LC], F32, kind="ExternalOutput").ap()
        dram["dbg_dA"] = nc.dram_tensor("dbg_dA", [B, DI, LC], F32, kind="ExternalOutput").ap()
    dram["cc_in"] = nc.dram_tensor("cc_in", [128, 1024], F32, kind="Internal").ap()
    dram["cc_out"] = nc.dram_tensor("cc_out", [128 * NCORES, 1024], F32, kind="Internal").ap()

    sch = Sched()
    with contextlib.ExitStack() as stack:
        build_body(nc, stack, sch, dram, stop_phase)
    return nc


def build_body(nc, stack, sch, dram, stop_phase):
    DBG = "dbg_u" in dram
    ar = Arena(nc)
    banks = [stack.enter_context(nc.psum_tensor(f"pb{i}", [128, 512], F32))
             for i in range(8)]
    bank = [bk.ap() for bk in banks]

    A_sb = ar.perm(NDT * NS, name="A")
    cw_sb = ar.perm(NDT * 4, name="cw")
    cb_sb = ar.perm(NDT, name="cb")
    bdt_sb = ar.perm(NDT, name="bdt")
    dp_sb = ar.perm(NDT, name="dp")
    rk_sb = ar.perm(NCORES, name="rk")
    rkc_sb = ar.perm(NCORES, name="rkc")
    id_sb = ar.perm(128, name="ident")
    onesr_sb = ar.perm(128, name="onesrow")
    onesc_sb = ar.perm(1, name="onescol")
    epsc_sb = ar.perm(1, name="epscol")
    zeros_sb = ar.perm(T1A, name="zeros")
    srun_sb = ar.perm(B * NDT, name="srun")
    hst_sb = ar.perm(B * NDT * NS, name="hst")
    D_sb = ar.perm(B * NDT * NS, name="D")
    h0_sb = ar.perm(B * NDT * NS, name="h0")


    bufs = {}

    def buf(name):
        if name not in bufs:
            bufs[name] = Buf(name)
        return bufs[name]

    def dma(eng, dst, src, *, reads=(), writes=(), deps=(), tag=""):
        return sch.add(eng, lambda m, d=dst, s=src: m.dma_start(out=d, in_=s),
                       reads=reads, writes=writes, deps=deps, dma=True, tag=tag)

    def mm(pb, lhsT, rhs, start, stop, reads, writes, tag=""):
        return sch.add("tensor",
                       lambda m, o=pb, a=lhsT, r=rhs, s1=start, s2=stop:
                       m.matmul(o, a, r, start=s1, stop=s2),
                       reads=reads, writes=writes, tag=tag)

    xT = dram["xT"]
    sc_xconv, sc_sz, sc_dt, sc_S, sc_BC = (dram["sc_xconv"], dram["sc_sz"],
                                           dram["sc_dt"], dram["sc_S"],
                                           dram["sc_BC"])
    sc_gated, sc_xn1f, sc_xn1b = (dram["sc_gated"], dram["sc_xn1f"],
                                  dram["sc_xn1b"])

    ar.phase_reset()
    wi_bf = [ar.alloc(E4 // 2, BF16, "wi") for _ in range(NKM)]
    wx_bf = ar.alloc(NDT * 96 // 2 + 48, BF16, "wx")
    wdt_bf = ar.alloc(DI // 2, BF16, "wdt")

    for name, dst in (("conv_w2", cw_sb), ("conv_b2", cb_sb),
                      ("b_dt2", bdt_sb), ("dp2", dp_sb), ("rkmask", rk_sb),
                      ("ident", id_sb), ("a_log2", A_sb)):
        dma("sync", dst, dram[name], writes=[buf("c_" + name)], tag="ldc")

    sch.add("scalar",
            lambda m: m.activation(out=A_sb, in_=A_sb, func=AF.Exp, bias=0.0, scale=1.0),
            reads=[buf("c_a_log2")], writes=[buf("c_a_log2")], tag="Aexp")
    sch.add("vector",
            lambda m: m.tensor_scalar(out=A_sb, in0=A_sb, scalar1=-1.0,
                                      scalar2=None, op0=ALU.mult),
            reads=[buf("c_a_log2")], writes=[buf("c_a_log2")], strict=True, tag="Aneg")
    sch.add("vector",
            lambda m: m.tensor_scalar(out=rkc_sb, in0=rk_sb, scalar1=-1.0,
                                      scalar2=1.0, op0=ALU.mult, op1=ALU.add),
            reads=[buf("c_rkmask")], writes=[buf("c_rkc")], tag="rkc")
    sch.add("vector", lambda m: m.memset(zeros_sb, 0.0), writes=[buf("c_zeros")])
    sch.add("vector", lambda m: m.memset(srun_sb, 0.0), writes=[buf("c_srun")])
    sch.add("vector", lambda m: m.memset(hst_sb, 0.0), writes=[buf("c_hst")])
    sch.add("vector", lambda m: m.memset(onesr_sb[0:1, :], 1.0), writes=[buf("c_onesr")])
    sch.add("vector", lambda m: m.memset(onesc_sb, 1.0), writes=[buf("c_onesc")])
    sch.add("vector", lambda m: m.memset(epsc_sb, EPS), writes=[buf("c_eps")])

    for k in range(NKM):
        dma("gpsimd", wi_bf[k], dram["w_inT"][k * 128:(k + 1) * 128, :],
            writes=[buf(f"w_wi{k}")], tag="ldwi")
    for k in range(NDT):
        dma("gpsimd", wx_bf[:, k * 96:(k + 1) * 96],
            dram["w_xT"][k * 128:(k + 1) * 128, :], writes=[buf("w_wx")], tag="ldwx")
    dma("gpsimd", wdt_bf[0:RK, 0:DI], dram["w_dtT"], writes=[buf("w_wdt")], tag="ldwdt")

    xb_bf = [ar.alloc(NKM * (T1A + 3) // 2 + 64, BF16, "xb") for _ in range(2)]
    xcf = [ar.alloc(T1A + 3, name="xcf") for _ in range(2)]
    halbuf = ar.alloc(NDT * 3, name="hal")
    accb_t = [ar.alloc(T1A, name="acc") for _ in range(NDT)]
    xconv_t = [ar.alloc(T1A, name="xconv") for _ in range(2)]
    sz_t = [ar.alloc(T1A, name="sz") for _ in range(2)]
    xcv_bf = ar.alloc(NDT * T1A // 2, BF16, "xcv")
    dtl_bf = ar.alloc(T1A // 2, BF16, "dtl")
    bc_f = [ar.alloc(T1A, name="bc") for _ in range(2)]
    e1b_t = [ar.alloc(T1A, name="e1") for _ in range(2)]
    e2b_t = [ar.alloc(T1A, name="e2") for _ in range(2)]
    rlb_t = [ar.alloc(T1A, name="rl") for _ in range(2)]
    dt_t = [ar.alloc(T1A, name="dt") for _ in range(2)]
    S_t = [ar.alloc(T1A, name="S") for _ in range(2)]

    NT1A = LC // T1A

    if stop_phase >= 1:
        for b in range(B):
            for tt in range(NT1A):
                c0 = tt * T1A
                tp_ = tt % 2
                xb = xb_bf[tp_]
                bxb = buf(f"xb{tp_}")
                for k in range(NKM):
                    dma("gpsimd", xb[:, k * (T1A + 3):(k + 1) * (T1A + 3)],
                        xT[b, k * 128:(k + 1) * 128, c0:c0 + T1A + 3],
                        writes=[bxb], tag="ldx")
                if tt == 0:
                    for m_ in range(NDT):
                        for k in range(NKM):
                            mm(bank[7][:, 0:3], wi_bf[k][:, m_ * 128:(m_ + 1) * 128],
                               xb[:, k * (T1A + 3):k * (T1A + 3) + 3],
                               k == 0, k == NKM - 1,
                               reads=[bxb, buf(f"w_wi{k}")], writes=[buf("pb7")], tag="halo")
                        sch.add("vector",
                                lambda m, m_=m_: m.tensor_copy(
                                    halbuf[:, m_ * 3:(m_ + 1) * 3], bank[7][:, 0:3]),
                                reads=[buf("pb7")], writes=[buf("hal")], tag="haloev")

                for m_ in range(NDT):
                    pbk = bank[m_ % 4]
                    bpb = buf(f"pb{m_ % 4}")
                    for k in range(NKM):
                        mm(pbk, wi_bf[k][:, m_ * 128:(m_ + 1) * 128],
                           xb[:, k * (T1A + 3) + 3:(k + 1) * (T1A + 3)],
                           k == 0, k == NKM - 1,
                           reads=[bxb, buf(f"w_wi{k}")], writes=[bpb], tag="xz")
                    xcfull = xcf[m_ % 2]
                    bxc = buf(f"xcf{m_ % 2}")
                    sch.add("vector",
                            lambda m, x=xcfull, m_=m_: m.tensor_copy(
                                x[:, 0:3], halbuf[:, m_ * 3:(m_ + 1) * 3]),
                            reads=[buf("hal")], writes=[bxc], strict=True, tag="halcp")
                    sch.add("scalar",
                            lambda m, x=xcfull, p=pbk: m.activation(
                                out=x[:, 3:3 + T1A], in_=p, func=AF.Copy,
                                bias=0.0, scale=1.0),
                            reads=[bpb], writes=[bxc], strict=True, tag="xcev")
                    sch.add("vector",
                            lambda m, x=xcfull, m_=m_: m.tensor_copy(
                                halbuf[:, m_ * 3:(m_ + 1) * 3], x[:, T1A:T1A + 3]),
                            reads=[bxc], writes=[buf("hal")], strict=True, tag="halupd")
                    accb = accb_t[m_]
                    bacc = buf(f"acc{m_}")
                    sch.add("scalar",
                            lambda m, x=xcfull, m_=m_, accb=accb: m.activation(
                                out=accb, in_=x[:, 0:T1A], func=AF.Copy,
                                bias=0.0, scale=cw_sb[:, m_ * 4:m_ * 4 + 1]),
                            reads=[bxc, buf("c_conv_w2")], writes=[bacc], tag="conv0")
                    for j in (1, 2, 3):
                        sch.add("vector",
                                lambda m, x=xcfull, m_=m_, j=j, accb=accb:
                                m.scalar_tensor_tensor(
                                    out=accb, in0=x[:, j:j + T1A],
                                    scalar=cw_sb[:, m_ * 4 + j:m_ * 4 + j + 1],
                                    in1=accb, op0=ALU.mult, op1=ALU.add),
                                reads=[bxc, buf("c_conv_w2"), bacc],
                                writes=[bacc], strict=True, tag=f"conv{j}")
                for m_ in range(NDT):
                    accb = accb_t[m_]
                    bacc = buf(f"acc{m_}")
                    xcv = xconv_t[m_ % 2]
                    bxcv = buf(f"xconv{m_ % 2}")
                    sch.add("scalar",
                            lambda m, o=xcv, m_=m_, accb=accb: m.activation(
                                out=o, in_=accb, func=AF.Silu,
                                bias=cb_sb[:, m_:m_ + 1], scale=1.0),
                            reads=[bacc, buf("c_conv_b2")], writes=[bxcv], tag="silu")
                    sch.add("vector",
                            lambda m, o=xcv_bf[:, m_ * T1A:(m_ + 1) * T1A], i=xcv:
                            m.tensor_copy(o, i),
                            reads=[bxcv], writes=[buf("xcvbf")], tag="xcvcast")
                    dma("gpsimd", sc_xconv[b, m_ * 128:(m_ + 1) * 128, c0:c0 + T1A], xcv,
                        reads=[bxcv], tag="stxconv")

                for m_ in range(NDT):
                    pbk = bank[4 + m_ % 3]
                    bpb = buf(f"pb{4 + m_ % 3}")
                    for k in range(NKM):
                        mm(pbk, wi_bf[k][:, (NDT + m_) * 128:(NDT + m_ + 1) * 128],
                           xb[:, k * (T1A + 3) + 3:(k + 1) * (T1A + 3)],
                           k == 0, k == NKM - 1,
                           reads=[bxb, buf(f"w_wi{k}")], writes=[bpb], tag="z")
                    szv = sz_t[m_ % 2]
                    bsz = buf(f"sz{m_ % 2}")
                    sch.add("scalar",
                            lambda m, o=szv, p=pbk: m.activation(
                                out=o, in_=p, func=AF.Silu, bias=0.0, scale=1.0),
                            reads=[bpb], writes=[bsz], tag="siluz")
                    dma("gpsimd", sc_sz[b, m_ * 128:(m_ + 1) * 128, c0:c0 + T1A], szv,
                        reads=[bsz], tag="stsz")

                for k in range(NDT):
                    mm(bank[4][0:96, :], wx_bf[:, k * 96:(k + 1) * 96],
                       xcv_bf[:, k * T1A:(k + 1) * T1A],
                       k == 0, k == NDT - 1,
                       reads=[buf("xcvbf"), buf("w_wx")], writes=[buf("pb4")], tag="xdbl")
                sch.add("scalar",
                        lambda m: m.activation(out=dtl_bf[0:RK, 0:T1A],
                                               in_=bank[4][0:RK, :],
                                               func=AF.Copy, bias=0.0, scale=1.0),
                        reads=[buf("pb4")], writes=[buf("dtl")], tag="dtlcast")
                bcv = bc_f[tt % 2]
                bbc = buf(f"bc{tt % 2}")
                sch.add("vector",
                        lambda m, o=bcv: m.tensor_copy(o[0:32, :], bank[4][64:96, :]),
                        reads=[buf("pb4")], writes=[bbc], tag="bcev")
                dma("gpsimd", sc_BC[b, :, c0:c0 + T1A], bcv[0:32, :], reads=[bbc], tag="stbc")

                for m_ in range(NDT):
                    pbk = bank[m_ % 2]
                    bpb = buf(f"pb{m_ % 2}")
                    mm(pbk, wdt_bf[0:RK, m_ * 128:(m_ + 1) * 128], dtl_bf[0:RK, 0:T1A],
                       True, True, reads=[buf("dtl"), buf("w_wdt")], writes=[bpb], tag="dtp")
                    e1b = e1b_t[m_ % 2]
                    e2b = e2b_t[m_ % 2]
                    rlb = rlb_t[m_ % 2]
                    be1 = buf(f"e1{m_ % 2}")
                    be2 = buf(f"e2{m_ % 2}")
                    brl = buf(f"rl{m_ % 2}")
                    sch.add("scalar",
                            lambda m, p=pbk, m_=m_, e1b=e1b: m.activation(
                                out=e1b, in_=p, func=AF.Abs,
                                bias=bdt_sb[:, m_:m_ + 1], scale=1.0),
                            reads=[bpb, buf("c_b_dt2")], writes=[be1], tag="sp_abs")
                    sch.add("scalar",
                            lambda m, e1b=e1b, e2b=e2b: m.activation(
                                out=e2b, in_=e1b, func=AF.Exp, bias=0.0, scale=-1.0),
                            reads=[be1], writes=[be2], strict=True, tag="sp_exp")
                    sch.add("scalar",
                            lambda m, e2b=e2b: m.activation(
                                out=e2b, in_=e2b, func=AF.Ln, bias=1.0, scale=1.0),
                            reads=[be2], writes=[be2], strict=True, tag="sp_ln")
                    sch.add("scalar",
                            lambda m, p=pbk, m_=m_, rlb=rlb: m.activation(
                                out=rlb, in_=p, func=AF.Relu,
                                bias=bdt_sb[:, m_:m_ + 1], scale=1.0),
                            reads=[bpb, buf("c_b_dt2")], writes=[brl], tag="sp_relu")
                    dtv = dt_t[m_ % 2]
                    bdtv = buf(f"dt{m_ % 2}")
                    sch.add("vector",
                            lambda m, o=dtv, e2b=e2b, rlb=rlb: m.tensor_add(o, e2b, rlb),
                            reads=[be2, brl], writes=[bdtv], tag="sp_add")
                    Sv = S_t[m_ % 2]
                    bS = buf(f"S{m_ % 2}")
                    scol = srun_sb[:, b * NDT + m_:b * NDT + m_ + 1]
                    sch.add("vector",
                            lambda m, o=Sv, i=dtv, s=scol: m.tensor_tensor_scan(
                                out=o, data0=zeros_sb, data1=i, initial=s,
                                op0=ALU.add, op1=ALU.add),
                            reads=[bdtv, buf("c_zeros"), buf("c_srun")], writes=[bS],
                            strict=True, tag="scum")
                    sch.add("vector",
                            lambda m, i=Sv, s=scol: m.tensor_copy(s, i[:, T1A - 1:T1A]),
                            reads=[bS], writes=[buf("c_srun")], strict=True, tag="scol")
                    dma("gpsimd", sc_dt[b, m_ * 128:(m_ + 1) * 128, c0:c0 + T1A], dtv,
                        reads=[bdtv], tag="stdt")
                    dma("gpsimd", sc_S[b, m_ * 128:(m_ + 1) * 128, c0:c0 + T1A], Sv,
                        reads=[bS], tag="stS")

        sch.barrier()

    ar.phase_reset()
    Bbc = ar.alloc(NS * T1B // 2, BF16, "Bbc")
    Cbc = ar.alloc(NS * T1B // 2, BF16, "Cbc")
    dt_s = [ar.alloc(T1B, name="dt1b")] * 2
    xc_s = [ar.alloc(T1B, name="xc1b") for _ in range(2)]
    sz_s = [ar.alloc(T1B, name="sz1b") for _ in range(2)]
    u_s = [ar.alloc(T1B // 2, BF16, "u1b")] * 2
    dA_s = [ar.alloc(T1B, name="dA") for _ in range(2)]
    dBx_s = [ar.alloc(T1B // 2, BF16, "dBx") for _ in range(2)]
    h_s = [ar.alloc(T1B, name="h") for _ in range(2)]
    g_s = [ar.alloc(T1B // 2, BF16, "g1b") for _ in range(4)]
    yp_s = [[ar.alloc(T1B, name="yp") for _ in range(4)] for _ in range(2)]
    y_s = yp_s[0][0]
    yf_s = [ar.alloc(T1B, name="yf") for _ in range(2)]
    g_bf = [ar.alloc(T1B // 2, BF16, "gbf") for _ in range(2)]
    FD_j = yp_s[0][1]

    NT1B = LC // T1B
    if stop_phase >= 2:
        pending_gate = None
        for b in range(B):
            for blk in range(NT1B):
                c0 = blk * T1B
                for which, dst, bname in ((0, Bbc, "bbcB"), (1, Cbc, "bbcC")):
                    for n in range(NS):
                        dma("sync", dst[:, n * T1B:(n + 1) * T1B],
                            sc_BC[b, which * NS + n, c0:c0 + T1B]
                            .unsqueeze(0).partition_broadcast(128),
                            writes=[buf(f"{bname}{n}")], tag="ldbc")
                if DBG:
                    dma("sync", dram["dbg_bbc"][b, blk, 0, :, :], Bbc,
                        reads=[buf("bbcB")], tag="dbgb")
                    dma("sync", dram["dbg_bbc"][b, blk, 1, :, :], Cbc,
                        reads=[buf("bbcC")], tag="dbgc")
                for dtile in range(NDT):
                    dp_ = dtile % 2
                    bdt = buf("dt1b")
                    bxc = buf(f"xc1b{dp_}")
                    bsz = buf(f"sz1b{dp_}")
                    bu = buf("u1b")
                    dma("scalar", dt_s[dp_],
                        sc_dt[b, dtile * 128:(dtile + 1) * 128, c0:c0 + T1B],
                        writes=[bdt], tag="lddt")
                    dma("scalar", xc_s[dp_],
                        sc_xconv[b, dtile * 128:(dtile + 1) * 128, c0:c0 + T1B],
                        writes=[bxc], tag="ldxc")
                    dma("scalar", sz_s[dp_],
                        sc_sz[b, dtile * 128:(dtile + 1) * 128, c0:c0 + T1B],
                        writes=[bsz], tag="ldsz")
                    sch.add("vector",
                            lambda m, dp_=dp_: m.tensor_mul(u_s[dp_], dt_s[dp_], xc_s[dp_]),
                            reads=[bdt, bxc], writes=[bu], tag="umul")
                    if DBG:
                        dma("sync", dram["dbg_u"][b, dtile * 128:(dtile + 1) * 128, c0:c0 + T1B],
                            u_s[dp_], reads=[bu], tag="dbgu")
                    for n in range(NS):
                        np_ = n % 2
                        acol = A_sb[:, dtile * NS + n:dtile * NS + n + 1]
                        hcol = hst_sb[:, (b * NDT + dtile) * NS + n:
                                      (b * NDT + dtile) * NS + n + 1]
                        bdA = buf(f"dA{np_}")
                        bdBx = buf(f"dBx{np_}")
                        bh = buf(f"h{np_}")
                        bg = buf(f"g{np_}")
                        sch.add("scalar",
                                lambda m, a=acol, dp_=dp_, np_=np_: m.activation(
                                    out=dA_s[np_], in_=dt_s[dp_], func=AF.Exp,
                                    bias=0.0, scale=a),
                                reads=[bdt, buf("c_a_log2")], writes=[bdA], tag="dAexp")
                        sch.add("vector",
                                lambda m, n=n, dp_=dp_, np_=np_: m.tensor_mul(
                                    dBx_s[np_], u_s[dp_], Bbc[:, n * T1B:(n + 1) * T1B]),
                                reads=[bu, buf(f"bbcB{n}")], writes=[bdBx], tag="dBx")
                        sch.add("vector",
                                lambda m, h=hcol, np_=np_: m.tensor_tensor_scan(
                                    out=h_s[np_], data0=dA_s[np_], data1=dBx_s[np_],
                                    initial=h, op0=ALU.mult, op1=ALU.add),
                                reads=[bdA, bdBx, buf("c_hst")],
                                writes=[bh], strict=True, tag="scan")
                        sch.add("vector",
                                lambda m, h=hcol, np_=np_: m.tensor_copy(
                                    h, h_s[np_][:, T1B - 1:T1B]),
                                reads=[bh], writes=[buf("c_hst")], strict=True,
                                tag="hcol")
                        g4 = n % 4
                        bg = buf(f"g{n % 4}")
                        sch.add("vector",
                                lambda m, n=n, np_=np_: m.tensor_mul(
                                    g_s[n % 4], h_s[np_], Cbc[:, n * T1B:(n + 1) * T1B]),
                                reads=[bh, buf(f"bbcC{n}")], writes=[bg],
                                tag="ymul")
                        bp = buf(f"yp{dp_}{g4}")
                        if n < 4:
                            dma("gpsimd", yp_s[dp_][g4], g_s[g4],
                                reads=[bg], writes=[bp], tag="ycp")
                        else:
                            sch.add("gpsimd",
                                    lambda m, g4=g4, dp_=dp_: m.dma_start(
                                        out=yp_s[dp_][g4], in_=g_s[g4],
                                        accum_op=ALU.add),
                                    reads=[bg, bp], writes=[bp], dma=True, tag="yacc")
                    if DBG:
                        dma("sync", dram["dbg_y"][b, dtile * 128:(dtile + 1) * 128, c0:c0 + T1B],
                            y_s, reads=[buf("yp00")], tag="dbgy")
                    sch.add("gpsimd",
                            lambda m, dp_=dp_: m.dma_start(
                                out=yp_s[dp_][0], in_=yp_s[dp_][1], accum_op=ALU.add),
                            reads=[buf(f"yp{dp_}1"), buf(f"yp{dp_}0")],
                            writes=[buf(f"yp{dp_}0")], dma=True, tag="ycomb")
                    sch.add("gpsimd",
                            lambda m, dp_=dp_: m.dma_start(
                                out=yp_s[dp_][2], in_=yp_s[dp_][3], accum_op=ALU.add),
                            reads=[buf(f"yp{dp_}3"), buf(f"yp{dp_}2")],
                            writes=[buf(f"yp{dp_}2")], dma=True, tag="ycomb")
                    sch.add("gpsimd",
                            lambda m, dp_=dp_: m.dma_start(
                                out=yp_s[dp_][0], in_=yp_s[dp_][2], accum_op=ALU.add),
                            reads=[buf(f"yp{dp_}2"), buf(f"yp{dp_}0")],
                            writes=[buf(f"yp{dp_}0")], dma=True, tag="ycomb")

                    def emit_gate(b=b, dtile=dtile, dp_=dp_, c0=c0,
                                  bxc=bxc, bsz=bsz):
                        sch.add("vector",
                                lambda m, dtile=dtile, dp_=dp_:
                                m.scalar_tensor_tensor(
                                    out=yf_s[dp_], in0=xc_s[dp_],
                                    scalar=dp_sb[:, dtile:dtile + 1],
                                    in1=yp_s[dp_][0], op0=ALU.mult, op1=ALU.add),
                                reads=[bxc, buf(f"yp{dp_}0"), buf("c_dp2")],
                                writes=[buf(f"yf{dp_}")], tag="yfull")
                        sch.add("vector",
                                lambda m, dp_=dp_: m.tensor_mul(
                                    g_bf[dp_][:, 0:T1B], yf_s[dp_], sz_s[dp_]),
                                reads=[buf(f"yf{dp_}"), bsz], writes=[buf(f"gbf{dp_}")],
                                tag="gate")
                        dma("sync",
                            sc_gated[b, dtile * 128:(dtile + 1) * 128, c0:c0 + T1B],
                            g_bf[dp_][:, 0:T1B], reads=[buf(f"gbf{dp_}")], tag="stgated")

                    if pending_gate is not None:
                        pending_gate()
                    pending_gate = emit_gate

        if pending_gate is not None:
            pending_gate()
            pending_gate = None
        for b in range(B):
            for dtile in range(NDT):
                scol = srun_sb[:, b * NDT + dtile:b * NDT + dtile + 1]
                sch.add("scalar",
                        lambda m, s=scol, dtile=dtile, b=b: m.activation(
                            out=D_sb[:, (b * NDT + dtile) * NS:
                                     (b * NDT + dtile + 1) * NS],
                            in_=A_sb[:, dtile * NS:(dtile + 1) * NS], func=AF.Exp,
                            bias=0.0, scale=s),
                        reads=[buf("c_srun"), buf("c_a_log2")], writes=[buf("c_D")],
                        tag="Dexp")

        sch.barrier()

        ar.phase_reset()
        wo_bf = ar.alloc(NDT * DM // 2, BF16, "wo")
        Cbc2 = [ar.alloc(NS * T2A // 2, BF16, "Cbc2") for _ in range(2)]
        S_2 = [ar.alloc(T2A, name="S2a") for _ in range(2)]
        sz_2 = [ar.alloc(T2A, name="sz2a") for _ in range(2)]
        Pc_2 = [ar.alloc(T2A // 2, BF16, "Pc") for _ in range(4)]
        w_2 = [ar.alloc(T2A // 2, BF16, "w2a") for _ in range(2)]
        corr_2 = ar.alloc(T2A, name="corr")
        corrB_2 = ar.alloc(T2A, name="corrB")
        gat_bf = [ar.alloc(T2A // 2, BF16, "gat2a") for _ in range(2)]
        gs_bf = ar.alloc(NDT * T2A // 2, BF16, "gs")
        r1_f = ar.alloc(NKM * T2A, name="r1")
        xt_f = ar.alloc(NKM * T2A, name="xt2a")
        sq_f = ar.alloc(T2A, name="sq2a")
        xn1_f = ar.alloc(NKM * T2A, name="xn1")
        row_f = ar.alloc(3 * T2A, name="rows")

        for k in range(NDT):
            dma("gpsimd", wo_bf[:, k * DM:(k + 1) * DM],
                dram["w_outT"][k * 128:(k + 1) * 128, :], writes=[buf("w_wo")],
                tag="ldwo")
        cc_in, cc_out = dram["cc_in"], dram["cc_out"]
        dma("gpsimd", cc_in[:, 0:512], hst_sb, reads=[buf("c_hst")],
            writes=[buf("ccin")], tag="ccF")
        dma("gpsimd", cc_in[:, 512:1024], D_sb, reads=[buf("c_D")],
            writes=[buf("ccin")], tag="ccD")
        sch.add("gpsimd",
                lambda m: m.collective_compute(
                    "AllGather", ALU.bypass, replica_groups=[list(range(NCORES))],
                    ins=[cc_in.opt()], outs=[cc_out.opt()]),
                reads=[buf("ccin")], writes=[buf("ccout")], tag="allgather")
        sch.add("vector", lambda m: m.memset(h0_sb, 0.0), writes=[buf("c_h0")])
        for j in range(NCORES):
            bfd = buf("fdj")
            dma("sync", FD_j, cc_out[j * 128:(j + 1) * 128, :],
                reads=[buf("ccout")], writes=[bfd], tag="ldfd")
            mcol = rk_sb[:, j:j + 1]
            sch.add("vector",
                    lambda m, mc=mcol, j=j: m.scalar_tensor_tensor(
                        out=FD_j[:, 512:1024], in0=FD_j[:, 512:1024], scalar=mc,
                        in1=rkc_sb[:, j:j + 1].broadcast_to([128, 512]),
                        op0=ALU.mult, op1=ALU.add),
                    reads=[bfd, buf("c_rkmask"), buf("c_rkc")], writes=[bfd],
                    strict=True, tag="dm")
            sch.add("vector",
                    lambda m: m.tensor_mul(h0_sb, h0_sb, FD_j[:, 512:1024]),
                    reads=[bfd, buf("c_h0")], writes=[buf("c_h0")], strict=True, tag="h0d")
            sch.add("vector",
                    lambda m, mc=mcol: m.scalar_tensor_tensor(
                        out=h0_sb, in0=FD_j[:, 0:512], scalar=mc, in1=h0_sb,
                        op0=ALU.mult, op1=ALU.add),
                    reads=[bfd, buf("c_rkmask"), buf("c_h0")], writes=[buf("c_h0")],
                    strict=True, tag="h0f")

    NT2A = LC // T2A
    if stop_phase >= 3:
        def ldc2(b, blk):
            c0 = blk * T2A
            bp_ = blk % 2
            for n in range(NS):
                dma("sync", Cbc2[bp_][:, n * T2A:(n + 1) * T2A],
                    sc_BC[b, NS + n, c0:c0 + T2A]
                    .unsqueeze(0).partition_broadcast(128),
                    writes=[buf(f"cbc2{bp_}")], tag="ldc2")

        blocks2a = [(b, blk) for b in range(B) for blk in range(NT2A)]
        ldc2(*blocks2a[0])
        for bi, (b, blk) in enumerate(blocks2a):
                c0 = blk * T2A
                bp_ = blk % 2
                bcbc2 = buf(f"cbc2{bp_}")
                for dtile in range(NDT):
                    dp_ = dtile % 2
                    bS = buf(f"S2a{dp_}")
                    bsz = buf(f"sz2a{dp_}")
                    bgat = buf(f"gat2a{dp_}")
                    dma("scalar", S_2[dp_],
                        sc_S[b, dtile * 128:(dtile + 1) * 128, c0:c0 + T2A],
                        writes=[bS], tag="ldS")
                    dma("scalar", sz_2[dp_],
                        sc_sz[b, dtile * 128:(dtile + 1) * 128, c0:c0 + T2A],
                        writes=[bsz], tag="ldsz2")
                    dma("scalar", gat_bf[dp_][:, 0:T2A],
                        sc_gated[b, dtile * 128:(dtile + 1) * 128, c0:c0 + T2A],
                        writes=[bgat], tag="ldgat")
                    for n in range(NS):
                        np_ = n % 2
                        pp4 = n % 4
                        acol = A_sb[:, dtile * NS + n:dtile * NS + n + 1]
                        hcol = h0_sb[:, (b * NDT + dtile) * NS + n:
                                     (b * NDT + dtile) * NS + n + 1]
                        bPc = buf(f"Pc{pp4}")
                        bw = buf(f"w2a{np_}")
                        sch.add("scalar",
                                lambda m, a=acol, dp_=dp_, pp4=pp4: m.activation(
                                    out=Pc_2[pp4], in_=S_2[dp_], func=AF.Exp,
                                    bias=0.0, scale=a),
                                reads=[bS, buf("c_a_log2")], writes=[bPc],
                                tag="Pcexp")
                        tgt = (corr_2, corrB_2)[n % 2] if n < 2 else w_2[np_]
                        btg = (buf("corr"), buf("corrB"))[n % 2] if n < 2 else bw
                        sch.add("vector",
                                lambda m, h=hcol, n=n, o=tgt, pp4=pp4, bp_=bp_:
                                m.scalar_tensor_tensor(
                                    out=o, in0=Pc_2[pp4], scalar=h,
                                    in1=Cbc2[bp_][:, n * T2A:(n + 1) * T2A],
                                    op0=ALU.mult, op1=ALU.mult),
                                reads=[bPc, buf("c_h0"), bcbc2],
                                writes=[btg], tag="wstt")
                        if n >= 2:
                            acc, bacc = ((corr_2, buf("corr")) if n % 2 == 0
                                         else (corrB_2, buf("corrB")))
                            sch.add("vector",
                                    lambda m, np_=np_, acc=acc: m.tensor_add(
                                        acc, acc, w_2[np_]),
                                    reads=[bw, bacc], writes=[bacc],
                                    strict=True, tag="corradd")
                    sch.add("vector",
                            lambda m: m.tensor_add(corr_2, corr_2, corrB_2),
                            reads=[buf("corrB"), buf("corr")], writes=[buf("corr")],
                            strict=True, tag="corrfin")
                    sch.add("vector",
                            lambda m, dp_=dp_: m.tensor_mul(corr_2, corr_2, sz_2[dp_]),
                            reads=[buf("corr"), bsz], writes=[buf("corr")],
                            strict=True, tag="corrsz")
                    sch.add("vector",
                            lambda m, dtile=dtile, dp_=dp_: m.tensor_add(
                                gs_bf[:, dtile * T2A:(dtile + 1) * T2A],
                                gat_bf[dp_][:, 0:T2A], corr_2),
                            reads=[buf("corr"), bgat], writes=[buf(f"gs{dtile}")],
                            strict=True, tag="gs")
                if bi + 1 < len(blocks2a):
                    ldc2(*blocks2a[bi + 1])
                for k in range(NKM):
                    dma("sync", xt_f[:, k * T2A:(k + 1) * T2A],
                        xT[b, k * 128:(k + 1) * 128, 3 + c0:3 + c0 + T2A],
                        writes=[buf("xt2a")], tag="ldxt2")
                for k in range(NDT):
                    for mo in range(NKM):
                        mm(bank[mo], wo_bf[:, k * DM + mo * 128:k * DM + (mo + 1) * 128],
                           gs_bf[:, k * T2A:(k + 1) * T2A], k == 0, k == NDT - 1,
                           reads=[buf(f"gs{k}"), buf("w_wo")], writes=[buf(f"pb{mo}")],
                           tag="wout")
                for mo in range(NKM):
                    sch.add("vector",
                            lambda m, mo=mo: m.tensor_add(
                                r1_f[:, mo * T2A:(mo + 1) * T2A], bank[mo],
                                xt_f[:, mo * T2A:(mo + 1) * T2A]),
                            reads=[buf(f"pb{mo}"), buf("xt2a")], writes=[buf("r1")],
                            tag="r1")
                for k in range(NKM):
                    mm(bank[4][0:1, 0:T2A], onesc_sb,
                       r1_f[:, k * T2A:(k + 1) * T2A], k == 0, k == NKM - 1,
                       reads=[buf("r1"), buf("c_onesc")], writes=[buf("pb4")], tag="ln1m")
                for k in range(NKM):
                    sch.add("scalar",
                            lambda m, k=k: m.activation(
                                out=sq_f, in_=r1_f[:, k * T2A:(k + 1) * T2A],
                                func=AF.Square, bias=0.0, scale=1.0),
                            reads=[buf("r1")], writes=[buf("sq2a")], tag="ln1sq")
                    mm(bank[5][0:1, 0:T2A], onesc_sb, sq_f, k == 0, k == NKM - 1,
                       reads=[buf("sq2a"), buf("c_onesc")], writes=[buf("pb5")],
                       tag="ln1v")
                mrow = row_f[0:1, 0:T2A]
                vrow = row_f[0:1, T2A:2 * T2A]
                rrow = row_f[0:1, 2 * T2A:3 * T2A]
                sch.add("vector",
                        lambda m: m.tensor_scalar(out=mrow, in0=bank[4][0:1, 0:T2A],
                                                  scalar1=1.0 / DM, scalar2=None,
                                                  op0=ALU.mult),
                        reads=[buf("pb4")], writes=[buf("rows")], tag="mean")
                sch.add("vector",
                        lambda m: m.tensor_scalar(out=vrow, in0=bank[5][0:1, 0:T2A],
                                                  scalar1=1.0 / DM, scalar2=None,
                                                  op0=ALU.mult),
                        reads=[buf("pb5")], writes=[buf("rows")], strict=True, tag="var0")
                sch.add("vector",
                        lambda m: m.tensor_mul(rrow, mrow, mrow),
                        reads=[buf("rows")], writes=[buf("rows")], strict=True, tag="m2")
                sch.add("vector",
                        lambda m: m.tensor_sub(vrow, vrow, rrow),
                        reads=[buf("rows")], writes=[buf("rows")], strict=True, tag="var")
                sch.add("scalar",
                        lambda m: m.activation(out=rrow, in_=vrow, func=AF.Ln,
                                               bias=epsc_sb[0:1, :], scale=1.0),
                        reads=[buf("rows"), buf("c_eps")], writes=[buf("rows")], strict=True, tag="lnv")
                sch.add("scalar",
                        lambda m: m.activation(out=rrow, in_=rrow, func=AF.Exp,
                                               bias=0.0, scale=-0.5),
                        reads=[buf("rows")], writes=[buf("rows")], strict=True, tag="rstd")
                mm(bank[6][:, 0:T2A], onesr_sb[0:1, :], mrow, True, True,
                   reads=[buf("rows"), buf("c_onesr")], writes=[buf("pb6")], tag="mbc")
                mm(bank[7][:, 0:T2A], onesr_sb[0:1, :], rrow, True, True,
                   reads=[buf("rows"), buf("c_onesr")], writes=[buf("pb7")], tag="rbc")
                for k in range(NKM):
                    sch.add("vector",
                            lambda m, k=k: m.tensor_sub(
                                xn1_f[:, k * T2A:(k + 1) * T2A],
                                r1_f[:, k * T2A:(k + 1) * T2A], bank[6][:, 0:T2A]),
                            reads=[buf("r1"), buf("pb6")], writes=[buf("xn1")], tag="xm")
                    sch.add("vector",
                            lambda m, k=k: m.tensor_mul(
                                xn1_f[:, k * T2A:(k + 1) * T2A],
                                xn1_f[:, k * T2A:(k + 1) * T2A], bank[7][:, 0:T2A]),
                            reads=[buf("xn1"), buf("pb7")], writes=[buf("xn1")],
                            strict=True, tag="xr")
                    dma("sync", sc_xn1f[b, k * 128:(k + 1) * 128, c0:c0 + T2A],
                        xn1_f[:, k * T2A:(k + 1) * T2A], reads=[buf("xn1")], tag="stxn1f")
                    dma("gpsimd", sc_xn1b[b, k * 128:(k + 1) * 128, c0:c0 + T2A],
                        xn1_f[:, k * T2A:(k + 1) * T2A], reads=[buf("xn1")], tag="stxn1b")

        sch.barrier()

    ar.phase_reset()
    wfc_bf = [ar.alloc(NKM * IM // 2, BF16, "wfc") for _ in range(2)]
    wpj_bf = ar.alloc(NEXP * 4 * DM // 2, BF16, "wpj")
    wg_bf = ar.alloc(NKM * NEXP // 2 + 8, BF16, "wg")
    xn1b_s = [ar.alloc(NKM * T2B // 2, BF16, "xn1b2b") for _ in range(2)]
    xn1f_s = ar.alloc(T2B, name="xn1f2b")
    rt_f = [ar.alloc(T2B, name="rtf")] * 2
    rtT_f = ar.alloc(NEXP, name="rtT")
    gcol = ar.alloc(8, name="gcols")
    gmask = ar.alloc(4 * NEXP, name="gmask")
    gT_f = [ar.alloc(T2B, name="gT")] * 2
    gstage = ar.alloc(T2B + 8, name="gstage")
    gbc_f = [ar.alloc(NEXP * T2B // 2, BF16, "gbc") for _ in range(2)]
    hg_bf = ar.alloc(4 * NEXP * T2B // 2, BF16, "hg")
    r2_f = [ar.alloc(NKM * T2B, name="r2") for _ in range(2)]
    r2T_f = ar.alloc(DM, name="r2T")
    sq2_f = ar.alloc(DM, name="sq2b")
    st_c = ar.alloc(8, name="stats2b")
    out_f = ar.alloc(DM, name="out2b")

    out_d = dram["out"]
    NT2B = LC // T2B
    if stop_phase >= 4:
        for e in range(NEXP):
            for k in range(4):
                dma("gpsimd", wpj_bf[:, (e * 4 + k) * DM:(e * 4 + k + 1) * DM],
                    dram["w_projT"][e, k * 128:(k + 1) * 128, :], writes=[buf("w_pj")],
                    tag="ldwpj")
        for k in range(NKM):
            dma("gpsimd", wg_bf[:, k * NEXP:(k + 1) * NEXP],
                dram["w_gateT"][k * 128:(k + 1) * 128, :], writes=[buf("w_wg")],
                tag="ldwg")

        m1c = gcol[:, 0:1]
        nm1c = gcol[:, 1:2]
        m2c = gcol[:, 2:3]
        e2c = gcol[:, 3:4]
        w1c = gcol[:, 4:5]
        w2c = gcol[:, 5:6]
        scol2 = st_c[:, 0:1]
        vcol2 = st_c[:, 1:2]
        rcol2 = st_c[:, 2:3]
        mcol2 = st_c[:, 3:4]

        def emit_ln2(b, blk, qp_):
            c0 = blk * T2B
            for sub in range(T2B // 128):
                sc0 = sub * 128
                for k in range(NKM):
                    pbk = bank[k % 2]
                    bpb = buf(f"pb{k % 2}")
                    sch.add("tensor",
                            lambda m, k=k, sc0=sc0, p=pbk, qp_=qp_: m.transpose(
                                p[:, 0:128],
                                r2_f[qp_][:, k * T2B + sc0:k * T2B + sc0 + 128], id_sb),
                            reads=[buf(f"r2{qp_}"), buf("c_ident")], writes=[bpb],
                            tag="trans2")
                    sch.add("scalar",
                            lambda m, k=k, p=pbk: m.activation(
                                out=r2T_f[:, k * 128:(k + 1) * 128], in_=p[:, 0:128],
                                func=AF.Copy, bias=0.0, scale=1.0),
                            reads=[bpb], writes=[buf("r2T")], tag="trev")
                sch.add("vector",
                        lambda m: m.tensor_reduce(out=scol2, in_=r2T_f, axis=AX.X,
                                                  op=ALU.add),
                        reads=[buf("r2T")], writes=[buf("st2")], strict=True,
                        tag="sum2")
                sch.add("vector",
                        lambda m: m.tensor_scalar(out=mcol2, in0=scol2,
                                                  scalar1=1.0 / DM, scalar2=None,
                                                  op0=ALU.mult),
                        reads=[buf("st2")], writes=[buf("st2")], strict=True,
                        tag="mean2")
                sch.add("scalar",
                        lambda m: m.activation(out=sq2_f, in_=r2T_f, func=AF.Square,
                                               bias=0.0, scale=1.0),
                        reads=[buf("r2T")], writes=[buf("sq2b")], tag="sq2")
                sch.add("vector",
                        lambda m: m.tensor_reduce(out=vcol2, in_=sq2_f, axis=AX.X,
                                                  op=ALU.add),
                        reads=[buf("sq2b")], writes=[buf("st2")], strict=True,
                        tag="sumsq2")
                sch.add("vector",
                        lambda m: m.tensor_scalar(out=vcol2, in0=vcol2,
                                                  scalar1=1.0 / DM, scalar2=None,
                                                  op0=ALU.mult),
                        reads=[buf("st2")], writes=[buf("st2")], strict=True, tag="v2")
                sch.add("vector",
                        lambda m: m.tensor_mul(rcol2, mcol2, mcol2),
                        reads=[buf("st2")], writes=[buf("st2")], strict=True,
                        tag="m22")
                sch.add("vector",
                        lambda m: m.tensor_sub(vcol2, vcol2, rcol2),
                        reads=[buf("st2")], writes=[buf("st2")], strict=True,
                        tag="var2")
                sch.add("scalar",
                        lambda m: m.activation(out=rcol2, in_=vcol2, func=AF.Ln,
                                               bias=epsc_sb, scale=1.0),
                        reads=[buf("st2"), buf("c_eps")], writes=[buf("st2")],
                        strict=True, tag="lnv2")
                sch.add("scalar",
                        lambda m: m.activation(out=rcol2, in_=rcol2, func=AF.Exp,
                                               bias=0.0, scale=-0.5),
                        reads=[buf("st2")], writes=[buf("st2")], strict=True,
                        tag="rstd2")
                sch.add("vector",
                        lambda m: m.tensor_scalar(
                            out=out_f, in0=r2T_f, scalar1=mcol2, scalar2=rcol2,
                            op0=ALU.subtract, op1=ALU.mult),
                        reads=[buf("r2T"), buf("st2")], writes=[buf("outb")],
                        strict=True, tag="xn2")
                dma("sync", out_d[b, c0 + sc0:c0 + sc0 + 128, :], out_f,
                    reads=[buf("outb")], tag="stout")

        pending_ln2 = None
        for b in range(B):
            for blk in range(NT2B):
                c0 = blk * T2B
                qp_ = blk % 2
                xn1b = xn1b_s[qp_]
                bxn1b = buf(f"xn1b2b{qp_}")
                rtv = rt_f[qp_]
                brt = buf("rtf")
                gTv = gT_f[qp_]
                bgT = buf("gTf")
                gbcv = gbc_f[qp_]
                bgbc = buf(f"gbc{qp_}")
                for k in range(NKM):
                    dma("scalar", xn1b[:, k * T2B:(k + 1) * T2B],
                        sc_xn1b[b, k * 128:(k + 1) * 128, c0:c0 + T2B],
                        writes=[bxn1b], tag="ldxn1b")
                for k in range(NKM):
                    mm(bank[0][0:NEXP, 0:T2B], wg_bf[:, k * NEXP:(k + 1) * NEXP],
                       xn1b[:, k * T2B:(k + 1) * T2B], k == 0, k == NKM - 1,
                       reads=[bxn1b, buf("w_wg")], writes=[buf("pb0")],
                       tag="router")
                sch.add("vector",
                        lambda m, rtv=rtv: m.tensor_copy(rtv[0:NEXP, 0:T2B],
                                                         bank[0][0:NEXP, 0:T2B]),
                        reads=[buf("pb0")], writes=[brt], tag="rtev")
                for sub in range(T2B // 128):
                    sc0 = sub * 128
                    mm(bank[1][:, 0:NEXP], rtv[0:NEXP, sc0:sc0 + 128],
                       id_sb[0:NEXP, 0:NEXP], True, True,
                       reads=[brt, buf("c_ident")], writes=[buf("pb1")], tag="rtT")
                    sch.add("vector",
                            lambda m: m.tensor_copy(rtT_f[:, 0:NEXP], bank[1][:, 0:NEXP]),
                            reads=[buf("pb1")], writes=[buf("rtT")], tag="rtTev")
                    sch.add("vector",
                            lambda m: m.tensor_reduce(out=m1c, in_=rtT_f[:, 0:NEXP],
                                                      axis=AX.X, op=ALU.max),
                            reads=[buf("rtT")], writes=[buf("gcols")], strict=True,
                            tag="m1")
                    sch.add("vector",
                            lambda m: m.tensor_single_scalar(
                                out=gmask[:, 0:NEXP], in_=rtT_f[:, 0:NEXP], scalar=m1c,
                                op=ALU.is_ge),
                            reads=[buf("rtT"), buf("gcols")], writes=[buf("gmask")],
                            strict=True, tag="ismax")
                    sch.add("vector",
                            lambda m: m.scalar_tensor_tensor(
                                out=gmask[:, NEXP:2 * NEXP], in0=gmask[:, 0:NEXP],
                                scalar=-1e30, in1=rtT_f[:, 0:NEXP],
                                op0=ALU.mult, op1=ALU.add),
                            reads=[buf("gmask"), buf("rtT")], writes=[buf("gmask")],
                            strict=True, tag="maskmax")
                    sch.add("vector",
                            lambda m: m.tensor_reduce(out=m2c,
                                                      in_=gmask[:, NEXP:2 * NEXP],
                                                      axis=AX.X, op=ALU.max),
                            reads=[buf("gmask")], writes=[buf("gcols")], strict=True,
                            tag="m2")
                    sch.add("vector",
                            lambda m: m.tensor_scalar(out=nm1c, in0=m1c, scalar1=-1.0,
                                                      scalar2=None, op0=ALU.mult),
                            reads=[buf("gcols")], writes=[buf("gcols")], strict=True,
                            tag="nm1")
                    sch.add("scalar",
                            lambda m: m.activation(out=e2c, in_=m2c, func=AF.Exp,
                                                   bias=nm1c, scale=1.0),
                            reads=[buf("gcols")], writes=[buf("gcols")], strict=True,
                            tag="e2")
                    sch.add("vector",
                            lambda m: m.tensor_scalar(out=w1c, in0=e2c, scalar1=1.0,
                                                      scalar2=None, op0=ALU.add),
                            reads=[buf("gcols")], writes=[buf("gcols")], strict=True,
                            tag="den")
                    sch.add("vector",
                            lambda m: m.reciprocal(out=w1c, in_=w1c),
                            reads=[buf("gcols")], writes=[buf("gcols")], strict=True,
                            tag="w1")
                    sch.add("vector",
                            lambda m: m.tensor_mul(w2c, e2c, w1c),
                            reads=[buf("gcols")], writes=[buf("gcols")], strict=True,
                            tag="w2")
                    sch.add("vector",
                            lambda m: m.tensor_single_scalar(
                                out=gmask[:, 2 * NEXP:3 * NEXP],
                                in_=gmask[:, NEXP:2 * NEXP], scalar=m2c, op=ALU.is_ge),
                            reads=[buf("gmask"), buf("gcols")], writes=[buf("gmask")],
                            strict=True, tag="ismax2")
                    sch.add("vector",
                            lambda m: m.tensor_single_scalar(
                                out=gmask[:, 0:NEXP], in_=gmask[:, 0:NEXP], scalar=w1c,
                                op=ALU.mult),
                            reads=[buf("gmask"), buf("gcols")], writes=[buf("gmask")],
                            strict=True, tag="gw1")
                    sch.add("vector",
                            lambda m: m.scalar_tensor_tensor(
                                out=gmask[:, 3 * NEXP:4 * NEXP],
                                in0=gmask[:, 2 * NEXP:3 * NEXP], scalar=w2c,
                                in1=gmask[:, 0:NEXP], op0=ALU.mult, op1=ALU.add),
                            reads=[buf("gmask"), buf("gcols")], writes=[buf("gmask")],
                            strict=True, tag="gates")
                    mm(bank[2][0:NEXP, 0:128], gmask[:, 3 * NEXP:4 * NEXP],
                       id_sb[:, 0:128], True, True,
                       reads=[buf("gmask"), buf("c_ident")], writes=[buf("pb2")],
                       tag="gTmm")
                    sch.add("vector",
                            lambda m, sc0=sc0, gTv=gTv: m.tensor_copy(
                                gTv[0:NEXP, sc0:sc0 + 128], bank[2][0:NEXP, 0:128]),
                            reads=[buf("pb2")], writes=[bgT], tag="gTev")
                for e in range(NEXP):
                    stg = gstage[0:1, 0:T2B]
                    bst = buf("gst")
                    dma("sync", stg, gTv[e:e + 1, 0:T2B], reads=[bgT],
                        writes=[bst], tag="gstage")
                    mm(bank[3][:, 0:T2B], onesr_sb[0:1, :], stg, True, True,
                       reads=[bst, buf("c_onesr")], writes=[buf("pb3")], tag="gbcmm")
                    sch.add("scalar",
                            lambda m, e=e, gbcv=gbcv: m.activation(
                                out=gbcv[:, e * T2B:(e + 1) * T2B], in_=bank[3][:, 0:T2B],
                                func=AF.Copy, bias=0.0, scale=1.0),
                            reads=[buf("pb3")], writes=[bgbc], tag="gbcev")
                for e in range(NEXP):
                    ep_ = e % 2
                    bwfc = buf(f"w_fc{ep_}")
                    dma("gpsimd",
                        wfc_bf[ep_][:, 0:NKM * IM].rearrange("p (k i) -> p k i", i=IM),
                        dram["w_fcT"][e].rearrange("(k p) i -> p k i", p=128),
                        writes=[bwfc], tag="ldwfc")
                    for mi in range(4):
                        _fb = (2, 4, 5)[(e * 4 + mi) % 3]
                        pbk = bank[_fb]
                        bpb = buf(f"pb{_fb}")
                        for k in range(NKM):
                            mm(pbk[:, 0:T2B],
                               wfc_bf[ep_][:, k * IM + mi * 128:k * IM + (mi + 1) * 128],
                               xn1b[:, k * T2B:(k + 1) * T2B], k == 0, k == NKM - 1,
                               reads=[bxn1b, bwfc], writes=[bpb], tag="fc")
                        hslot = e * 4 + mi
                        sch.add("scalar",
                                lambda m, p=pbk, hs=hslot: m.activation(
                                    out=hg_bf[:, hs * T2B:(hs + 1) * T2B],
                                    in_=p[:, 0:T2B], func=AF.Gelu, bias=0.0, scale=1.0),
                                reads=[bpb], writes=[buf("hg")], tag="gelu")
                        sch.add("vector",
                                lambda m, hs=hslot, e=e, gbcv=gbcv: m.tensor_mul(
                                    hg_bf[:, hs * T2B:(hs + 1) * T2B],
                                    hg_bf[:, hs * T2B:(hs + 1) * T2B],
                                    gbcv[:, e * T2B:(e + 1) * T2B]),
                                reads=[buf("hg"), bgbc], writes=[buf("hg")],
                                strict=True, tag="hgmul")
                for mo in range(NKM):
                    pbk = bank[6 + mo % 2]
                    bpb = buf(f"pb{6 + mo % 2}")
                    first = True
                    for e in range(NEXP):
                        for ki in range(4):
                            mm(pbk[:, 0:T2B],
                               wpj_bf[:, (e * 4 + ki) * DM + mo * 128:
                                      (e * 4 + ki) * DM + (mo + 1) * 128],
                               hg_bf[:, (e * 4 + ki) * T2B:(e * 4 + ki + 1) * T2B],
                               first, (e == NEXP - 1 and ki == 3),
                               reads=[buf("hg"), buf("w_pj")], writes=[bpb], tag="proj")
                            first = False
                    dma("scalar", xn1f_s,
                        sc_xn1f[b, mo * 128:(mo + 1) * 128, c0:c0 + T2B],
                        writes=[buf("xn1f2b")], tag="ldxn1f")
                    sch.add("vector",
                            lambda m, mo=mo, p=pbk, qp_=qp_: m.tensor_add(
                                r2_f[qp_][:, mo * T2B:(mo + 1) * T2B], p[:, 0:T2B],
                                xn1f_s),
                            reads=[bpb, buf("xn1f2b")], writes=[buf(f"r2{qp_}")],
                            tag="r2")
                if pending_ln2 is not None:
                    pending_ln2()
                pending_ln2 = (lambda b=b, blk=blk, qp_=qp_: emit_ln2(b, blk, qp_))
        if pending_ln2 is not None:
            pending_ln2()

        sch.barrier()

    sch.finalize(nc, stack)
    block = stack.enter_context(nc.Block())

    @block.sync
    def _(m):
        sch.emit_engine(m, "sync")

    @block.tensor
    def _(m):
        sch.emit_engine(m, "tensor")

    @block.vector
    def _(m):
        sch.emit_engine(m, "vector")

    @block.scalar
    def _(m):
        sch.emit_engine(m, "scalar")

    @block.gpsimd
    def _(m):
        sch.emit_engine(m, "gpsimd")


# version 26
# speedup vs baseline: 1.0304x; 1.0088x over previous
import contextlib
import numpy as np
import concourse.bass as bass
import concourse.mybir as mybir
from concourse.bass_utils import run_bass_kernel_spmd

F32 = mybir.dt.float32
BF16 = mybir.dt.bfloat16
AF = mybir.ActivationFunctionType
ALU = mybir.AluOpType
AX = mybir.AxisListType

NCORES = 8
B = 2
L = 16384
LC = L // NCORES
DM = 1024
DI = 2048
E4 = 2 * DI
NS = 16
RK = 64
NEXP = 8
IM = 512
NDT = DI // 128
NKM = DM // 128

T1A = 512
T1B = 1024
T2A = 512
T2B = 512
EPS = 1e-5

ENGINES = ("sync", "tensor", "vector", "scalar", "gpsimd")


class _Op:
    __slots__ = ("eng", "emit", "deps", "inc", "count", "used", "strict", "tag",
                 "semkey")

    def __init__(self, eng, emit, deps, inc, strict, tag):
        self.eng = eng
        self.emit = emit
        self.deps = deps
        self.inc = inc
        self.count = 0
        self.used = False
        self.strict = strict
        self.tag = tag
        self.semkey = eng


class Buf:
    __slots__ = ("name", "writer", "readers")

    def __init__(self, name=""):
        self.name = name
        self.writer = None
        self.readers = []


NDMALANES = 8


class Sched:
    def __init__(self):
        self.q = {e: [] for e in ENGINES}
        self.all = []
        self._dma_idx = {e: 0 for e in ENGINES}

    def add(self, eng, emit, reads=(), writes=(), deps=(), dma=False,
            strict=False, tag=""):
        d = []
        for b in reads:
            if b.writer is not None:
                d.append(b.writer)
        for b in writes:
            if b.writer is not None:
                d.append(b.writer)
            d.extend(b.readers)
        d.extend(x for x in deps if x is not None)
        op = _Op(eng, emit, d, 16 if dma else 1, strict, tag)
        if dma:
            lane = self._dma_idx[eng] % NDMALANES
            self._dma_idx[eng] += 1
            op.semkey = (eng, lane)
        for b in reads:
            b.readers.append(op)
        for b in writes:
            b.writer = op
            b.readers = []
        self.q[eng].append(op)
        self.all.append(op)
        return op

    def barrier(self):
        last = {}
        for e in ENGINES:
            for op in reversed(self.q[e]):
                if op.emit is None:
                    continue
                if op.semkey not in last:
                    last[op.semkey] = op
                if op.semkey == e:
                    break
        for e in ENGINES:
            deps = [op for op in last.values() if op.eng != e]
            deps += [op for op in last.values()
                     if op.eng == e and op.semkey != e]
            self.add(e, None, deps=deps, strict=True, tag="barrier")

    def finalize(self, nc, stack):
        pos = {}
        for e in ENGINES:
            for i, op in enumerate(self.q[e]):
                pos[id(op)] = i
        for op in self.all:
            if op.inc == 16 and op.emit is not None:
                op.used = True
            wd = []
            for dp in op.deps:
                if dp.eng != op.eng or dp.semkey != dp.eng:
                    dp.used = True
                    wd.append(dp)
                elif op.strict or pos[id(dp)] >= pos[id(op)] - 2:
                    if dp.emit is not None:
                        dp.used = True
                        wd.append(dp)
            op.deps = wd
        self.sems = {}
        counts = {}
        for e in ENGINES:
            for op in self.q[e]:
                if op.used:
                    assert op.emit is not None, f"dep on no-op ({op.tag})"
                    counts[op.semkey] = counts.get(op.semkey, 0) + op.inc
                op.count = counts.get(op.semkey, 0)
        for k, c in counts.items():
            assert c < 60000, f"semaphore overflow on {k}: {c}"
        for e in ENGINES:
            self.sems[e] = stack.enter_context(nc.semaphore(f"s_{e}"))
            for ln in range(NDMALANES):
                self.sems[(e, ln)] = stack.enter_context(
                    nc.semaphore(f"s_{e}_d{ln}"))

    def emit_engine(self, engmod, e):
        sems = self.sems
        waited = {}
        for op in self.q[e]:
            for dp in op.deps:
                if dp.count > waited.get(dp.semkey, 0):
                    engmod.wait_ge(sems[dp.semkey], dp.count)
                    waited[dp.semkey] = dp.count
            if op.emit is not None:
                inst = op.emit(engmod)
                if op.used:
                    inst.then_inc(sems[op.semkey], op.inc)

    def stats(self):
        return {e: (len(self.q[e]), self.q[e][-1].count if self.q[e] else 0)
                for e in ENGINES}


def _split_shards(x):
    xt = np.ascontiguousarray(np.swapaxes(np.asarray(x, np.float32), 1, 2))
    shards = []
    for i in range(NCORES):
        t0 = i * LC
        sl = np.zeros((B, DM, 3 + LC), np.float32)
        sl[:, :, 3:] = xt[:, :, t0:t0 + LC]
        if t0 >= 3:
            sl[:, :, 0:3] = xt[:, :, t0 - 3:t0]
        shards.append(np.ascontiguousarray(sl))
    return shards


def _vec2d(v):
    return np.ascontiguousarray(np.asarray(v, np.float32).reshape(NDT, 128).T)


def prepare_in_maps(L0_attn_x, L0_attn_W_in, L0_attn_conv_w, L0_attn_conv_b,
                    L0_attn_W_x, L0_attn_W_dt, L0_attn_b_dt, L0_attn_A_log,
                    L0_attn_Dp, L0_attn_W_out, L0_ffn_w_gate, L0_ffn_w_fc,
                    L0_ffn_w_proj):
    f32 = np.float32
    xsh = _split_shards(L0_attn_x)

    w_inT = np.ascontiguousarray(np.asarray(L0_attn_W_in, f32).T)
    w_xT = np.ascontiguousarray(np.asarray(L0_attn_W_x, f32).T)
    w_dtT = np.ascontiguousarray(np.asarray(L0_attn_W_dt, f32).T)
    w_outT = np.ascontiguousarray(np.asarray(L0_attn_W_out, f32).T)
    w_gateT = np.ascontiguousarray(np.asarray(L0_ffn_w_gate, f32).T)
    w_fcT = np.ascontiguousarray(np.swapaxes(np.asarray(L0_ffn_w_fc, f32), 1, 2))
    w_projT = np.ascontiguousarray(np.swapaxes(np.asarray(L0_ffn_w_proj, f32), 1, 2))

    cw2 = np.zeros((128, NDT * 4), f32)
    cwa = np.asarray(L0_attn_conv_w, f32)
    for dt in range(NDT):
        cw2[:, dt * 4:(dt + 1) * 4] = cwa[dt * 128:(dt + 1) * 128, :]
    a2 = np.zeros((128, NDT * NS), f32)
    al = np.asarray(L0_attn_A_log, f32)
    for dt in range(NDT):
        a2[:, dt * NS:(dt + 1) * NS] = al[dt * 128:(dt + 1) * 128, :]

    cb2v = _vec2d(L0_attn_conv_b)
    bdt2v = _vec2d(L0_attn_b_dt)
    dp2v = _vec2d(L0_attn_Dp)
    in_maps = []
    for i in range(NCORES):
        mask = np.zeros((128, 8), f32)
        mask[:, :i] = 1.0
        in_maps.append({
            "xT": xsh[i], "w_inT": w_inT, "w_xT": w_xT, "w_dtT": w_dtT,
            "w_outT": w_outT, "w_gateT": w_gateT, "w_fcT": w_fcT,
            "w_projT": w_projT, "conv_w2": cw2,
            "conv_b2": cb2v, "b_dt2": bdt2v,
            "dp2": dp2v, "a_log2": a2,
            "ident": np.eye(128, dtype=f32), "rkmask": mask,
        })
    return in_maps


def kernel(**inputs):
    in_maps = prepare_in_maps(**inputs)
    nc = build_program()
    res = run_bass_kernel_spmd(nc, in_maps, core_ids=list(range(NCORES)))
    out = np.empty((B, L, DM), np.float32)
    for i in range(NCORES):
        out[:, i * LC:(i + 1) * LC, :] = res.results[i]["out"]
    return out


class Arena:
    def __init__(self, nc, total_f32=53100):
        self.t = nc.alloc_sbuf_tensor("arena", [128, total_f32], F32)
        self.total = total_f32
        self.perm_off = 0
        self.off = 0

    def perm(self, n_f32, dtype=F32, name=""):
        ap = self._view(self.perm_off, n_f32, dtype)
        self.perm_off += n_f32
        self.off = max(self.off, self.perm_off)
        return ap

    def phase_reset(self):
        self.off = self.perm_off

    def alloc(self, n_f32, dtype=F32, name=""):
        ap = self._view(self.off, n_f32, dtype)
        self.off += n_f32
        assert self.off <= self.total, f"arena overflow at {name}: {self.off}"
        return ap

    def _view(self, off, n_f32, dtype):
        ap = self.t.ap()[:, off:off + n_f32]
        if dtype == BF16:
            ap = ap.bitcast(BF16)
        return ap


def build_program(stop_phase=99, dbg_out=False):
    nc = bass.Bass("TRN2", target_bir_lowering=False)
    dram = {}

    def din(name, shape, dtype=F32):
        dram[name] = nc.dram_tensor(name, list(shape), dtype, kind="ExternalInput").ap()

    def dint(name, shape, dtype=F32):
        kind = "ExternalOutput" if dbg_out else "Internal"
        dram[name] = nc.dram_tensor(name, list(shape), dtype, kind=kind).ap()

    din("xT", (B, DM, 3 + LC))
    din("w_inT", (DM, E4))
    din("w_xT", (DI, 96))
    din("w_dtT", (RK, DI))
    din("w_outT", (DI, DM))
    din("w_gateT", (DM, NEXP))
    din("w_fcT", (NEXP, DM, IM))
    din("w_projT", (NEXP, IM, DM))
    din("conv_w2", (128, NDT * 4))
    din("conv_b2", (128, NDT))
    din("b_dt2", (128, NDT))
    din("dp2", (128, NDT))
    din("a_log2", (128, NDT * NS))
    din("ident", (128, 128))
    din("rkmask", (128, NCORES))

    dram["out"] = nc.dram_tensor("out", [B, LC, DM], F32, kind="ExternalOutput").ap()

    dint("sc_xconv", (B, DI, LC))
    dint("sc_sz", (B, DI, LC))
    dint("sc_dt", (B, DI, LC))
    dint("sc_S", (B, DI, LC))
    dint("sc_BC", (B, 32, LC), BF16)
    dint("sc_gated", (B, DI, LC), BF16)
    dint("sc_xn1f", (B, DM, LC))
    dint("sc_xn1b", (B, DM, LC), BF16)
    if dbg_out:
        dram["dbg_bbc"] = nc.dram_tensor("dbg_bbc", [B, 2, 2, 128, NS * T1B], F32, kind="ExternalOutput").ap()
        dram["dbg_u"] = nc.dram_tensor("dbg_u", [B, DI, LC], F32, kind="ExternalOutput").ap()
        dram["dbg_y"] = nc.dram_tensor("dbg_y", [B, DI, LC], F32, kind="ExternalOutput").ap()
        dram["dbg_dA"] = nc.dram_tensor("dbg_dA", [B, DI, LC], F32, kind="ExternalOutput").ap()
    dram["cc_in"] = nc.dram_tensor("cc_in", [128, 1024], F32, kind="Internal").ap()
    dram["cc_out"] = nc.dram_tensor("cc_out", [128 * NCORES, 1024], F32, kind="Internal").ap()

    sch = Sched()
    with contextlib.ExitStack() as stack:
        build_body(nc, stack, sch, dram, stop_phase)
    return nc


def build_body(nc, stack, sch, dram, stop_phase):
    DBG = "dbg_u" in dram
    ar = Arena(nc)
    banks = [stack.enter_context(nc.psum_tensor(f"pb{i}", [128, 512], F32))
             for i in range(8)]
    bank = [bk.ap() for bk in banks]

    A_sb = ar.perm(NDT * NS, name="A")
    cw_sb = ar.perm(NDT * 4, name="cw")
    cb_sb = ar.perm(NDT, name="cb")
    bdt_sb = ar.perm(NDT, name="bdt")
    dp_sb = ar.perm(NDT, name="dp")
    rk_sb = ar.perm(NCORES, name="rk")
    rkc_sb = ar.perm(NCORES, name="rkc")
    id_sb = ar.perm(128, name="ident")
    onesr_sb = ar.perm(128, name="onesrow")
    onesc_sb = ar.perm(1, name="onescol")
    epsc_sb = ar.perm(1, name="epscol")
    zeros_sb = ar.perm(T1A, name="zeros")
    srun_sb = ar.perm(B * NDT, name="srun")
    hst_sb = ar.perm(B * NDT * NS, name="hst")
    D_sb = ar.perm(B * NDT * NS, name="D")
    h0_sb = ar.perm(B * NDT * NS, name="h0")


    bufs = {}

    def buf(name):
        if name not in bufs:
            bufs[name] = Buf(name)
        return bufs[name]

    def dma(eng, dst, src, *, reads=(), writes=(), deps=(), tag=""):
        return sch.add(eng, lambda m, d=dst, s=src: m.dma_start(out=d, in_=s),
                       reads=reads, writes=writes, deps=deps, dma=True, tag=tag)

    def mm(pb, lhsT, rhs, start, stop, reads, writes, tag=""):
        return sch.add("tensor",
                       lambda m, o=pb, a=lhsT, r=rhs, s1=start, s2=stop:
                       m.matmul(o, a, r, start=s1, stop=s2),
                       reads=reads, writes=writes, tag=tag)

    xT = dram["xT"]
    sc_xconv, sc_sz, sc_dt, sc_S, sc_BC = (dram["sc_xconv"], dram["sc_sz"],
                                           dram["sc_dt"], dram["sc_S"],
                                           dram["sc_BC"])
    sc_gated, sc_xn1f, sc_xn1b = (dram["sc_gated"], dram["sc_xn1f"],
                                  dram["sc_xn1b"])

    ar.phase_reset()
    wi_bf = [ar.alloc(E4 // 2, BF16, "wi") for _ in range(NKM)]
    wx_bf = ar.alloc(NDT * 96 // 2 + 48, BF16, "wx")
    wdt_bf = ar.alloc(DI // 2, BF16, "wdt")

    for name, dst in (("conv_w2", cw_sb), ("conv_b2", cb_sb),
                      ("b_dt2", bdt_sb), ("dp2", dp_sb), ("rkmask", rk_sb),
                      ("ident", id_sb), ("a_log2", A_sb)):
        dma("sync", dst, dram[name], writes=[buf("c_" + name)], tag="ldc")

    sch.add("scalar",
            lambda m: m.activation(out=A_sb, in_=A_sb, func=AF.Exp, bias=0.0, scale=1.0),
            reads=[buf("c_a_log2")], writes=[buf("c_a_log2")], tag="Aexp")
    sch.add("vector",
            lambda m: m.tensor_scalar(out=A_sb, in0=A_sb, scalar1=-1.0,
                                      scalar2=None, op0=ALU.mult),
            reads=[buf("c_a_log2")], writes=[buf("c_a_log2")], strict=True, tag="Aneg")
    sch.add("vector",
            lambda m: m.tensor_scalar(out=rkc_sb, in0=rk_sb, scalar1=-1.0,
                                      scalar2=1.0, op0=ALU.mult, op1=ALU.add),
            reads=[buf("c_rkmask")], writes=[buf("c_rkc")], tag="rkc")
    sch.add("vector", lambda m: m.memset(zeros_sb, 0.0), writes=[buf("c_zeros")])
    sch.add("vector", lambda m: m.memset(srun_sb, 0.0), writes=[buf("c_srun")])
    sch.add("vector", lambda m: m.memset(hst_sb, 0.0), writes=[buf("c_hst")])
    sch.add("vector", lambda m: m.memset(onesr_sb[0:1, :], 1.0), writes=[buf("c_onesr")])
    sch.add("vector", lambda m: m.memset(onesc_sb, 1.0), writes=[buf("c_onesc")])
    sch.add("vector", lambda m: m.memset(epsc_sb, EPS), writes=[buf("c_eps")])

    for k in range(NKM):
        dma("gpsimd", wi_bf[k], dram["w_inT"][k * 128:(k + 1) * 128, :],
            writes=[buf(f"w_wi{k}")], tag="ldwi")
    for k in range(NDT):
        dma("gpsimd", wx_bf[:, k * 96:(k + 1) * 96],
            dram["w_xT"][k * 128:(k + 1) * 128, :], writes=[buf("w_wx")], tag="ldwx")
    dma("gpsimd", wdt_bf[0:RK, 0:DI], dram["w_dtT"], writes=[buf("w_wdt")], tag="ldwdt")

    xb_bf = [ar.alloc(NKM * (T1A + 3) // 2 + 64, BF16, "xb") for _ in range(2)]
    xcf = [ar.alloc(T1A + 3, name="xcf") for _ in range(2)]
    halbuf = ar.alloc(NDT * 3, name="hal")
    accb_t = [ar.alloc(T1A, name="acc") for _ in range(NDT)]
    xconv_t = [ar.alloc(T1A, name="xconv") for _ in range(2)]
    sz_t = [ar.alloc(T1A, name="sz") for _ in range(2)]
    xcv_bf = ar.alloc(NDT * T1A // 2, BF16, "xcv")
    dtl_bf = ar.alloc(T1A // 2, BF16, "dtl")
    bc_f = [ar.alloc(T1A, name="bc") for _ in range(2)]
    e1b_t = [ar.alloc(T1A, name="e1") for _ in range(2)]
    e2b_t = [ar.alloc(T1A, name="e2") for _ in range(2)]
    rlb_t = [ar.alloc(T1A, name="rl") for _ in range(2)]
    dt_t = [ar.alloc(T1A, name="dt") for _ in range(2)]
    S_t = [ar.alloc(T1A, name="S") for _ in range(2)]

    NT1A = LC // T1A

    if stop_phase >= 1:
        for b in range(B):
            for tt in range(NT1A):
                c0 = tt * T1A
                tp_ = tt % 2
                xb = xb_bf[tp_]
                bxb = buf(f"xb{tp_}")
                for k in range(NKM):
                    dma("gpsimd", xb[:, k * (T1A + 3):(k + 1) * (T1A + 3)],
                        xT[b, k * 128:(k + 1) * 128, c0:c0 + T1A + 3],
                        writes=[bxb], tag="ldx")
                if tt == 0:
                    for m_ in range(NDT):
                        for k in range(NKM):
                            mm(bank[7][:, 0:3], wi_bf[k][:, m_ * 128:(m_ + 1) * 128],
                               xb[:, k * (T1A + 3):k * (T1A + 3) + 3],
                               k == 0, k == NKM - 1,
                               reads=[bxb, buf(f"w_wi{k}")], writes=[buf("pb7")], tag="halo")
                        sch.add("vector",
                                lambda m, m_=m_: m.tensor_copy(
                                    halbuf[:, m_ * 3:(m_ + 1) * 3], bank[7][:, 0:3]),
                                reads=[buf("pb7")], writes=[buf("hal")], tag="haloev")

                for m_ in range(NDT):
                    pbk = bank[m_ % 4]
                    bpb = buf(f"pb{m_ % 4}")
                    for k in range(NKM):
                        mm(pbk, wi_bf[k][:, m_ * 128:(m_ + 1) * 128],
                           xb[:, k * (T1A + 3) + 3:(k + 1) * (T1A + 3)],
                           k == 0, k == NKM - 1,
                           reads=[bxb, buf(f"w_wi{k}")], writes=[bpb], tag="xz")
                    xcfull = xcf[m_ % 2]
                    bxc = buf(f"xcf{m_ % 2}")
                    sch.add("vector",
                            lambda m, x=xcfull, m_=m_: m.tensor_copy(
                                x[:, 0:3], halbuf[:, m_ * 3:(m_ + 1) * 3]),
                            reads=[buf("hal")], writes=[bxc], strict=True, tag="halcp")
                    sch.add("scalar",
                            lambda m, x=xcfull, p=pbk: m.activation(
                                out=x[:, 3:3 + T1A], in_=p, func=AF.Copy,
                                bias=0.0, scale=1.0),
                            reads=[bpb], writes=[bxc], strict=True, tag="xcev")
                    sch.add("vector",
                            lambda m, x=xcfull, m_=m_: m.tensor_copy(
                                halbuf[:, m_ * 3:(m_ + 1) * 3], x[:, T1A:T1A + 3]),
                            reads=[bxc], writes=[buf("hal")], strict=True, tag="halupd")
                    accb = accb_t[m_]
                    bacc = buf(f"acc{m_}")
                    sch.add("scalar",
                            lambda m, x=xcfull, m_=m_, accb=accb: m.activation(
                                out=accb, in_=x[:, 0:T1A], func=AF.Copy,
                                bias=0.0, scale=cw_sb[:, m_ * 4:m_ * 4 + 1]),
                            reads=[bxc, buf("c_conv_w2")], writes=[bacc], tag="conv0")
                    for j in (1, 2, 3):
                        sch.add("vector",
                                lambda m, x=xcfull, m_=m_, j=j, accb=accb:
                                m.scalar_tensor_tensor(
                                    out=accb, in0=x[:, j:j + T1A],
                                    scalar=cw_sb[:, m_ * 4 + j:m_ * 4 + j + 1],
                                    in1=accb, op0=ALU.mult, op1=ALU.add),
                                reads=[bxc, buf("c_conv_w2"), bacc],
                                writes=[bacc], strict=True, tag=f"conv{j}")
                for m_ in range(NDT):
                    accb = accb_t[m_]
                    bacc = buf(f"acc{m_}")
                    xcv = xconv_t[m_ % 2]
                    bxcv = buf(f"xconv{m_ % 2}")
                    sch.add("scalar",
                            lambda m, o=xcv, m_=m_, accb=accb: m.activation(
                                out=o, in_=accb, func=AF.Silu,
                                bias=cb_sb[:, m_:m_ + 1], scale=1.0),
                            reads=[bacc, buf("c_conv_b2")], writes=[bxcv], tag="silu")
                    sch.add("vector",
                            lambda m, o=xcv_bf[:, m_ * T1A:(m_ + 1) * T1A], i=xcv:
                            m.tensor_copy(o, i),
                            reads=[bxcv], writes=[buf("xcvbf")], tag="xcvcast")
                    dma("gpsimd", sc_xconv[b, m_ * 128:(m_ + 1) * 128, c0:c0 + T1A], xcv,
                        reads=[bxcv], tag="stxconv")

                for m_ in range(NDT):
                    pbk = bank[4 + m_ % 3]
                    bpb = buf(f"pb{4 + m_ % 3}")
                    for k in range(NKM):
                        mm(pbk, wi_bf[k][:, (NDT + m_) * 128:(NDT + m_ + 1) * 128],
                           xb[:, k * (T1A + 3) + 3:(k + 1) * (T1A + 3)],
                           k == 0, k == NKM - 1,
                           reads=[bxb, buf(f"w_wi{k}")], writes=[bpb], tag="z")
                    szv = sz_t[m_ % 2]
                    bsz = buf(f"sz{m_ % 2}")
                    sch.add("scalar",
                            lambda m, o=szv, p=pbk: m.activation(
                                out=o, in_=p, func=AF.Silu, bias=0.0, scale=1.0),
                            reads=[bpb], writes=[bsz], tag="siluz")
                    dma("gpsimd", sc_sz[b, m_ * 128:(m_ + 1) * 128, c0:c0 + T1A], szv,
                        reads=[bsz], tag="stsz")

                for k in range(NDT):
                    mm(bank[4][0:96, :], wx_bf[:, k * 96:(k + 1) * 96],
                       xcv_bf[:, k * T1A:(k + 1) * T1A],
                       k == 0, k == NDT - 1,
                       reads=[buf("xcvbf"), buf("w_wx")], writes=[buf("pb4")], tag="xdbl")
                sch.add("scalar",
                        lambda m: m.activation(out=dtl_bf[0:RK, 0:T1A],
                                               in_=bank[4][0:RK, :],
                                               func=AF.Copy, bias=0.0, scale=1.0),
                        reads=[buf("pb4")], writes=[buf("dtl")], tag="dtlcast")
                bcv = bc_f[tt % 2]
                bbc = buf(f"bc{tt % 2}")
                sch.add("vector",
                        lambda m, o=bcv: m.tensor_copy(o[0:32, :], bank[4][64:96, :]),
                        reads=[buf("pb4")], writes=[bbc], tag="bcev")
                dma("gpsimd", sc_BC[b, :, c0:c0 + T1A], bcv[0:32, :], reads=[bbc], tag="stbc")

                for m_ in range(NDT):
                    pbk = bank[m_ % 2]
                    bpb = buf(f"pb{m_ % 2}")
                    mm(pbk, wdt_bf[0:RK, m_ * 128:(m_ + 1) * 128], dtl_bf[0:RK, 0:T1A],
                       True, True, reads=[buf("dtl"), buf("w_wdt")], writes=[bpb], tag="dtp")
                    e1b = e1b_t[m_ % 2]
                    e2b = e2b_t[m_ % 2]
                    rlb = rlb_t[m_ % 2]
                    be1 = buf(f"e1{m_ % 2}")
                    be2 = buf(f"e2{m_ % 2}")
                    brl = buf(f"rl{m_ % 2}")
                    sch.add("scalar",
                            lambda m, p=pbk, m_=m_, e1b=e1b: m.activation(
                                out=e1b, in_=p, func=AF.Abs,
                                bias=bdt_sb[:, m_:m_ + 1], scale=1.0),
                            reads=[bpb, buf("c_b_dt2")], writes=[be1], tag="sp_abs")
                    sch.add("scalar",
                            lambda m, e1b=e1b, e2b=e2b: m.activation(
                                out=e2b, in_=e1b, func=AF.Exp, bias=0.0, scale=-1.0),
                            reads=[be1], writes=[be2], strict=True, tag="sp_exp")
                    sch.add("scalar",
                            lambda m, e2b=e2b: m.activation(
                                out=e2b, in_=e2b, func=AF.Ln, bias=1.0, scale=1.0),
                            reads=[be2], writes=[be2], strict=True, tag="sp_ln")
                    sch.add("scalar",
                            lambda m, p=pbk, m_=m_, rlb=rlb: m.activation(
                                out=rlb, in_=p, func=AF.Relu,
                                bias=bdt_sb[:, m_:m_ + 1], scale=1.0),
                            reads=[bpb, buf("c_b_dt2")], writes=[brl], tag="sp_relu")
                    dtv = dt_t[m_ % 2]
                    bdtv = buf(f"dt{m_ % 2}")
                    sch.add("vector",
                            lambda m, o=dtv, e2b=e2b, rlb=rlb: m.tensor_add(o, e2b, rlb),
                            reads=[be2, brl], writes=[bdtv], tag="sp_add")
                    Sv = S_t[m_ % 2]
                    bS = buf(f"S{m_ % 2}")
                    scol = srun_sb[:, b * NDT + m_:b * NDT + m_ + 1]
                    sch.add("vector",
                            lambda m, o=Sv, i=dtv, s=scol: m.tensor_tensor_scan(
                                out=o, data0=zeros_sb, data1=i, initial=s,
                                op0=ALU.add, op1=ALU.add),
                            reads=[bdtv, buf("c_zeros"), buf("c_srun")], writes=[bS],
                            strict=True, tag="scum")
                    sch.add("vector",
                            lambda m, i=Sv, s=scol: m.tensor_copy(s, i[:, T1A - 1:T1A]),
                            reads=[bS], writes=[buf("c_srun")], strict=True, tag="scol")
                    dma("gpsimd", sc_dt[b, m_ * 128:(m_ + 1) * 128, c0:c0 + T1A], dtv,
                        reads=[bdtv], tag="stdt")
                    dma("gpsimd", sc_S[b, m_ * 128:(m_ + 1) * 128, c0:c0 + T1A], Sv,
                        reads=[bS], tag="stS")

        sch.barrier()

    ar.phase_reset()
    Bbc = ar.alloc(NS * T1B // 2, BF16, "Bbc")
    Cbc = ar.alloc(NS * T1B // 2, BF16, "Cbc")
    dt_s = [ar.alloc(T1B, name="dt1b")] * 2
    xc_s = [ar.alloc(T1B, name="xc1b") for _ in range(2)]
    sz_s = [ar.alloc(T1B, name="sz1b") for _ in range(2)]
    u_s = [ar.alloc(T1B // 2, BF16, "u1b")] * 2
    dA_s = [ar.alloc(T1B, name="dA") for _ in range(2)]
    dBx_s = [ar.alloc(T1B // 2, BF16, "dBx") for _ in range(2)]
    h_s = [ar.alloc(T1B, name="h") for _ in range(2)]
    g_s = [ar.alloc(T1B // 2, BF16, "g1b") for _ in range(4)]
    yp_s = [[ar.alloc(T1B, name="yp") for _ in range(4)] for _ in range(2)]
    y_s = yp_s[0][0]
    yf_s = [ar.alloc(T1B, name="yf") for _ in range(2)]
    g_bf = [ar.alloc(T1B // 2, BF16, "gbf") for _ in range(2)]
    FD_j = yp_s[0][1]

    NT1B = LC // T1B
    if stop_phase >= 2:
        pending_gate = None
        for b in range(B):
            for blk in range(NT1B):
                c0 = blk * T1B
                for which, dst, bname in ((0, Bbc, "bbcB"), (1, Cbc, "bbcC")):
                    for n in range(NS):
                        dma("sync", dst[:, n * T1B:(n + 1) * T1B],
                            sc_BC[b, which * NS + n, c0:c0 + T1B]
                            .unsqueeze(0).partition_broadcast(128),
                            writes=[buf(f"{bname}{n}")], tag="ldbc")
                if DBG:
                    dma("sync", dram["dbg_bbc"][b, blk, 0, :, :], Bbc,
                        reads=[buf("bbcB")], tag="dbgb")
                    dma("sync", dram["dbg_bbc"][b, blk, 1, :, :], Cbc,
                        reads=[buf("bbcC")], tag="dbgc")
                for dtile in range(NDT):
                    dp_ = dtile % 2
                    bdt = buf("dt1b")
                    bxc = buf(f"xc1b{dp_}")
                    bsz = buf(f"sz1b{dp_}")
                    bu = buf("u1b")
                    dma("scalar", dt_s[dp_],
                        sc_dt[b, dtile * 128:(dtile + 1) * 128, c0:c0 + T1B],
                        writes=[bdt], tag="lddt")
                    dma("scalar", xc_s[dp_],
                        sc_xconv[b, dtile * 128:(dtile + 1) * 128, c0:c0 + T1B],
                        writes=[bxc], tag="ldxc")
                    dma("scalar", sz_s[dp_],
                        sc_sz[b, dtile * 128:(dtile + 1) * 128, c0:c0 + T1B],
                        writes=[bsz], tag="ldsz")
                    sch.add("vector",
                            lambda m, dp_=dp_: m.tensor_mul(u_s[dp_], dt_s[dp_], xc_s[dp_]),
                            reads=[bdt, bxc], writes=[bu], tag="umul")
                    if DBG:
                        dma("sync", dram["dbg_u"][b, dtile * 128:(dtile + 1) * 128, c0:c0 + T1B],
                            u_s[dp_], reads=[bu], tag="dbgu")
                    for n in range(NS):
                        np_ = n % 2
                        acol = A_sb[:, dtile * NS + n:dtile * NS + n + 1]
                        hcol = hst_sb[:, (b * NDT + dtile) * NS + n:
                                      (b * NDT + dtile) * NS + n + 1]
                        bdA = buf(f"dA{np_}")
                        bdBx = buf(f"dBx{np_}")
                        bh = buf(f"h{np_}")
                        bg = buf(f"g{np_}")
                        sch.add("scalar",
                                lambda m, a=acol, dp_=dp_, np_=np_: m.activation(
                                    out=dA_s[np_], in_=dt_s[dp_], func=AF.Exp,
                                    bias=0.0, scale=a),
                                reads=[bdt, buf("c_a_log2")], writes=[bdA], tag="dAexp")
                        sch.add("vector",
                                lambda m, n=n, dp_=dp_, np_=np_: m.tensor_mul(
                                    dBx_s[np_], u_s[dp_], Bbc[:, n * T1B:(n + 1) * T1B]),
                                reads=[bu, buf(f"bbcB{n}")], writes=[bdBx], tag="dBx")
                        sch.add("vector",
                                lambda m, h=hcol, np_=np_: m.tensor_tensor_scan(
                                    out=h_s[np_], data0=dA_s[np_], data1=dBx_s[np_],
                                    initial=h, op0=ALU.mult, op1=ALU.add),
                                reads=[bdA, bdBx, buf("c_hst")],
                                writes=[bh], strict=True, tag="scan")
                        sch.add("vector",
                                lambda m, h=hcol, np_=np_: m.tensor_copy(
                                    h, h_s[np_][:, T1B - 1:T1B]),
                                reads=[bh], writes=[buf("c_hst")], strict=True,
                                tag="hcol")
                        g4 = n % 4
                        bg = buf(f"g{n % 4}")
                        sch.add("vector",
                                lambda m, n=n, np_=np_: m.tensor_mul(
                                    g_s[n % 4], h_s[np_], Cbc[:, n * T1B:(n + 1) * T1B]),
                                reads=[bh, buf(f"bbcC{n}")], writes=[bg],
                                tag="ymul")
                        bp = buf(f"yp{dp_}{g4}")
                        if n < 4:
                            dma("gpsimd", yp_s[dp_][g4], g_s[g4],
                                reads=[bg], writes=[bp], tag="ycp")
                        else:
                            sch.add("gpsimd",
                                    lambda m, g4=g4, dp_=dp_: m.dma_start(
                                        out=yp_s[dp_][g4], in_=g_s[g4],
                                        accum_op=ALU.add),
                                    reads=[bg, bp], writes=[bp], dma=True, tag="yacc")
                    if DBG:
                        dma("sync", dram["dbg_y"][b, dtile * 128:(dtile + 1) * 128, c0:c0 + T1B],
                            y_s, reads=[buf("yp00")], tag="dbgy")
                    sch.add("gpsimd",
                            lambda m, dp_=dp_: m.dma_start(
                                out=yp_s[dp_][0], in_=yp_s[dp_][1], accum_op=ALU.add),
                            reads=[buf(f"yp{dp_}1"), buf(f"yp{dp_}0")],
                            writes=[buf(f"yp{dp_}0")], dma=True, tag="ycomb")
                    sch.add("gpsimd",
                            lambda m, dp_=dp_: m.dma_start(
                                out=yp_s[dp_][2], in_=yp_s[dp_][3], accum_op=ALU.add),
                            reads=[buf(f"yp{dp_}3"), buf(f"yp{dp_}2")],
                            writes=[buf(f"yp{dp_}2")], dma=True, tag="ycomb")
                    sch.add("gpsimd",
                            lambda m, dp_=dp_: m.dma_start(
                                out=yp_s[dp_][0], in_=yp_s[dp_][2], accum_op=ALU.add),
                            reads=[buf(f"yp{dp_}2"), buf(f"yp{dp_}0")],
                            writes=[buf(f"yp{dp_}0")], dma=True, tag="ycomb")

                    def emit_gate(b=b, dtile=dtile, dp_=dp_, c0=c0,
                                  bxc=bxc, bsz=bsz):
                        sch.add("vector",
                                lambda m, dtile=dtile, dp_=dp_:
                                m.scalar_tensor_tensor(
                                    out=yf_s[dp_], in0=xc_s[dp_],
                                    scalar=dp_sb[:, dtile:dtile + 1],
                                    in1=yp_s[dp_][0], op0=ALU.mult, op1=ALU.add),
                                reads=[bxc, buf(f"yp{dp_}0"), buf("c_dp2")],
                                writes=[buf(f"yf{dp_}")], tag="yfull")
                        sch.add("vector",
                                lambda m, dp_=dp_: m.tensor_mul(
                                    g_bf[dp_][:, 0:T1B], yf_s[dp_], sz_s[dp_]),
                                reads=[buf(f"yf{dp_}"), bsz], writes=[buf(f"gbf{dp_}")],
                                tag="gate")
                        dma("sync",
                            sc_gated[b, dtile * 128:(dtile + 1) * 128, c0:c0 + T1B],
                            g_bf[dp_][:, 0:T1B], reads=[buf(f"gbf{dp_}")], tag="stgated")

                    if pending_gate is not None:
                        pending_gate()
                    pending_gate = emit_gate

        if pending_gate is not None:
            pending_gate()
            pending_gate = None
        for b in range(B):
            for dtile in range(NDT):
                scol = srun_sb[:, b * NDT + dtile:b * NDT + dtile + 1]
                sch.add("scalar",
                        lambda m, s=scol, dtile=dtile, b=b: m.activation(
                            out=D_sb[:, (b * NDT + dtile) * NS:
                                     (b * NDT + dtile + 1) * NS],
                            in_=A_sb[:, dtile * NS:(dtile + 1) * NS], func=AF.Exp,
                            bias=0.0, scale=s),
                        reads=[buf("c_srun"), buf("c_a_log2")], writes=[buf("c_D")],
                        tag="Dexp")

        sch.barrier()

        ar.phase_reset()
        wo_bf = ar.alloc(NDT * DM // 2, BF16, "wo")
        Cbc2 = [ar.alloc(NS * T2A // 2, BF16, "Cbc2") for _ in range(2)]
        S_2 = [ar.alloc(T2A, name="S2a") for _ in range(2)]
        sz_2 = [ar.alloc(T2A, name="sz2a") for _ in range(2)]
        Pc_2 = [ar.alloc(T2A // 2, BF16, "Pc") for _ in range(4)]
        w_2 = [ar.alloc(T2A // 2, BF16, "w2a") for _ in range(2)]
        corr_2 = ar.alloc(T2A, name="corr")
        corrB_2 = ar.alloc(T2A, name="corrB")
        gat_bf = [ar.alloc(T2A // 2, BF16, "gat2a") for _ in range(2)]
        gs_bf = ar.alloc(NDT * T2A // 2, BF16, "gs")
        r1_f = ar.alloc(NKM * T2A, name="r1")
        xt_f = ar.alloc(NKM * T2A, name="xt2a")
        sq_f = ar.alloc(T2A, name="sq2a")
        xn1_f = ar.alloc(NKM * T2A, name="xn1")
        row_f = ar.alloc(3 * T2A, name="rows")

        for k in range(NDT):
            dma("gpsimd", wo_bf[:, k * DM:(k + 1) * DM],
                dram["w_outT"][k * 128:(k + 1) * 128, :], writes=[buf("w_wo")],
                tag="ldwo")
        cc_in, cc_out = dram["cc_in"], dram["cc_out"]
        dma("gpsimd", cc_in[:, 0:512], hst_sb, reads=[buf("c_hst")],
            writes=[buf("ccin")], tag="ccF")
        dma("gpsimd", cc_in[:, 512:1024], D_sb, reads=[buf("c_D")],
            writes=[buf("ccin")], tag="ccD")
        sch.add("gpsimd",
                lambda m: m.collective_compute(
                    "AllGather", ALU.bypass, replica_groups=[list(range(NCORES))],
                    ins=[cc_in.opt()], outs=[cc_out.opt()]),
                reads=[buf("ccin")], writes=[buf("ccout")], tag="allgather")
        sch.add("vector", lambda m: m.memset(h0_sb, 0.0), writes=[buf("c_h0")])
        for j in range(NCORES):
            bfd = buf("fdj")
            dma("sync", FD_j, cc_out[j * 128:(j + 1) * 128, :],
                reads=[buf("ccout")], writes=[bfd], tag="ldfd")
            mcol = rk_sb[:, j:j + 1]
            sch.add("vector",
                    lambda m, mc=mcol, j=j: m.scalar_tensor_tensor(
                        out=FD_j[:, 512:1024], in0=FD_j[:, 512:1024], scalar=mc,
                        in1=rkc_sb[:, j:j + 1].broadcast_to([128, 512]),
                        op0=ALU.mult, op1=ALU.add),
                    reads=[bfd, buf("c_rkmask"), buf("c_rkc")], writes=[bfd],
                    strict=True, tag="dm")
            sch.add("vector",
                    lambda m: m.tensor_mul(h0_sb, h0_sb, FD_j[:, 512:1024]),
                    reads=[bfd, buf("c_h0")], writes=[buf("c_h0")], strict=True, tag="h0d")
            sch.add("vector",
                    lambda m, mc=mcol: m.scalar_tensor_tensor(
                        out=h0_sb, in0=FD_j[:, 0:512], scalar=mc, in1=h0_sb,
                        op0=ALU.mult, op1=ALU.add),
                    reads=[bfd, buf("c_rkmask"), buf("c_h0")], writes=[buf("c_h0")],
                    strict=True, tag="h0f")

    NT2A = LC // T2A
    if stop_phase >= 3:
        def ldc2(b, blk):
            c0 = blk * T2A
            bp_ = blk % 2
            for n in range(NS):
                dma("sync", Cbc2[bp_][:, n * T2A:(n + 1) * T2A],
                    sc_BC[b, NS + n, c0:c0 + T2A]
                    .unsqueeze(0).partition_broadcast(128),
                    writes=[buf(f"cbc2{bp_}")], tag="ldc2")

        blocks2a = [(b, blk) for b in range(B) for blk in range(NT2A)]
        ldc2(*blocks2a[0])
        for bi, (b, blk) in enumerate(blocks2a):
                c0 = blk * T2A
                bp_ = blk % 2
                bcbc2 = buf(f"cbc2{bp_}")
                for dtile in range(NDT):
                    dp_ = dtile % 2
                    bS = buf(f"S2a{dp_}")
                    bsz = buf(f"sz2a{dp_}")
                    bgat = buf(f"gat2a{dp_}")
                    dma("scalar", S_2[dp_],
                        sc_S[b, dtile * 128:(dtile + 1) * 128, c0:c0 + T2A],
                        writes=[bS], tag="ldS")
                    dma("scalar", sz_2[dp_],
                        sc_sz[b, dtile * 128:(dtile + 1) * 128, c0:c0 + T2A],
                        writes=[bsz], tag="ldsz2")
                    dma("scalar", gat_bf[dp_][:, 0:T2A],
                        sc_gated[b, dtile * 128:(dtile + 1) * 128, c0:c0 + T2A],
                        writes=[bgat], tag="ldgat")
                    for n in range(NS):
                        np_ = n % 2
                        pp4 = n % 4
                        acol = A_sb[:, dtile * NS + n:dtile * NS + n + 1]
                        hcol = h0_sb[:, (b * NDT + dtile) * NS + n:
                                     (b * NDT + dtile) * NS + n + 1]
                        bPc = buf(f"Pc{pp4}")
                        bw = buf(f"w2a{np_}")
                        sch.add("scalar",
                                lambda m, a=acol, dp_=dp_, pp4=pp4: m.activation(
                                    out=Pc_2[pp4], in_=S_2[dp_], func=AF.Exp,
                                    bias=0.0, scale=a),
                                reads=[bS, buf("c_a_log2")], writes=[bPc],
                                tag="Pcexp")
                        tgt = (corr_2, corrB_2)[n % 2] if n < 2 else w_2[np_]
                        btg = (buf("corr"), buf("corrB"))[n % 2] if n < 2 else bw
                        sch.add("vector",
                                lambda m, h=hcol, n=n, o=tgt, pp4=pp4, bp_=bp_:
                                m.scalar_tensor_tensor(
                                    out=o, in0=Pc_2[pp4], scalar=h,
                                    in1=Cbc2[bp_][:, n * T2A:(n + 1) * T2A],
                                    op0=ALU.mult, op1=ALU.mult),
                                reads=[bPc, buf("c_h0"), bcbc2],
                                writes=[btg], tag="wstt")
                        if n >= 2:
                            acc, bacc = ((corr_2, buf("corr")) if n % 2 == 0
                                         else (corrB_2, buf("corrB")))
                            sch.add("vector",
                                    lambda m, np_=np_, acc=acc: m.tensor_add(
                                        acc, acc, w_2[np_]),
                                    reads=[bw, bacc], writes=[bacc],
                                    strict=True, tag="corradd")
                    sch.add("vector",
                            lambda m: m.tensor_add(corr_2, corr_2, corrB_2),
                            reads=[buf("corrB"), buf("corr")], writes=[buf("corr")],
                            strict=True, tag="corrfin")
                    sch.add("vector",
                            lambda m, dp_=dp_: m.tensor_mul(corr_2, corr_2, sz_2[dp_]),
                            reads=[buf("corr"), bsz], writes=[buf("corr")],
                            strict=True, tag="corrsz")
                    sch.add("vector",
                            lambda m, dtile=dtile, dp_=dp_: m.tensor_add(
                                gs_bf[:, dtile * T2A:(dtile + 1) * T2A],
                                gat_bf[dp_][:, 0:T2A], corr_2),
                            reads=[buf("corr"), bgat], writes=[buf(f"gs{dtile}")],
                            strict=True, tag="gs")
                if bi + 1 < len(blocks2a):
                    ldc2(*blocks2a[bi + 1])
                for k in range(NKM):
                    dma("sync", xt_f[:, k * T2A:(k + 1) * T2A],
                        xT[b, k * 128:(k + 1) * 128, 3 + c0:3 + c0 + T2A],
                        writes=[buf("xt2a")], tag="ldxt2")
                for k in range(NDT):
                    for mo in range(NKM):
                        mm(bank[mo], wo_bf[:, k * DM + mo * 128:k * DM + (mo + 1) * 128],
                           gs_bf[:, k * T2A:(k + 1) * T2A], k == 0, k == NDT - 1,
                           reads=[buf(f"gs{k}"), buf("w_wo")], writes=[buf(f"pb{mo}")],
                           tag="wout")
                for mo in range(NKM):
                    sch.add("vector",
                            lambda m, mo=mo: m.tensor_add(
                                r1_f[:, mo * T2A:(mo + 1) * T2A], bank[mo],
                                xt_f[:, mo * T2A:(mo + 1) * T2A]),
                            reads=[buf(f"pb{mo}"), buf("xt2a")], writes=[buf("r1")],
                            tag="r1")
                for k in range(NKM):
                    mm(bank[4][0:1, 0:T2A], onesc_sb,
                       r1_f[:, k * T2A:(k + 1) * T2A], k == 0, k == NKM - 1,
                       reads=[buf("r1"), buf("c_onesc")], writes=[buf("pb4")], tag="ln1m")
                for k in range(NKM):
                    sch.add("scalar",
                            lambda m, k=k: m.activation(
                                out=sq_f, in_=r1_f[:, k * T2A:(k + 1) * T2A],
                                func=AF.Square, bias=0.0, scale=1.0),
                            reads=[buf("r1")], writes=[buf("sq2a")], tag="ln1sq")
                    mm(bank[5][0:1, 0:T2A], onesc_sb, sq_f, k == 0, k == NKM - 1,
                       reads=[buf("sq2a"), buf("c_onesc")], writes=[buf("pb5")],
                       tag="ln1v")
                mrow = row_f[0:1, 0:T2A]
                vrow = row_f[0:1, T2A:2 * T2A]
                rrow = row_f[0:1, 2 * T2A:3 * T2A]
                sch.add("vector",
                        lambda m: m.tensor_scalar(out=mrow, in0=bank[4][0:1, 0:T2A],
                                                  scalar1=1.0 / DM, scalar2=None,
                                                  op0=ALU.mult),
                        reads=[buf("pb4")], writes=[buf("rows")], tag="mean")
                sch.add("vector",
                        lambda m: m.tensor_scalar(out=vrow, in0=bank[5][0:1, 0:T2A],
                                                  scalar1=1.0 / DM, scalar2=None,
                                                  op0=ALU.mult),
                        reads=[buf("pb5")], writes=[buf("rows")], strict=True, tag="var0")
                sch.add("vector",
                        lambda m: m.tensor_mul(rrow, mrow, mrow),
                        reads=[buf("rows")], writes=[buf("rows")], strict=True, tag="m2")
                sch.add("vector",
                        lambda m: m.tensor_sub(vrow, vrow, rrow),
                        reads=[buf("rows")], writes=[buf("rows")], strict=True, tag="var")
                sch.add("scalar",
                        lambda m: m.activation(out=rrow, in_=vrow, func=AF.Ln,
                                               bias=epsc_sb[0:1, :], scale=1.0),
                        reads=[buf("rows"), buf("c_eps")], writes=[buf("rows")], strict=True, tag="lnv")
                sch.add("scalar",
                        lambda m: m.activation(out=rrow, in_=rrow, func=AF.Exp,
                                               bias=0.0, scale=-0.5),
                        reads=[buf("rows")], writes=[buf("rows")], strict=True, tag="rstd")
                mm(bank[6][:, 0:T2A], onesr_sb[0:1, :], mrow, True, True,
                   reads=[buf("rows"), buf("c_onesr")], writes=[buf("pb6")], tag="mbc")
                mm(bank[7][:, 0:T2A], onesr_sb[0:1, :], rrow, True, True,
                   reads=[buf("rows"), buf("c_onesr")], writes=[buf("pb7")], tag="rbc")
                for k in range(NKM):
                    sch.add("vector",
                            lambda m, k=k: m.tensor_sub(
                                xn1_f[:, k * T2A:(k + 1) * T2A],
                                r1_f[:, k * T2A:(k + 1) * T2A], bank[6][:, 0:T2A]),
                            reads=[buf("r1"), buf("pb6")], writes=[buf("xn1")], tag="xm")
                    sch.add("vector",
                            lambda m, k=k: m.tensor_mul(
                                xn1_f[:, k * T2A:(k + 1) * T2A],
                                xn1_f[:, k * T2A:(k + 1) * T2A], bank[7][:, 0:T2A]),
                            reads=[buf("xn1"), buf("pb7")], writes=[buf("xn1")],
                            strict=True, tag="xr")
                    dma("sync", sc_xn1f[b, k * 128:(k + 1) * 128, c0:c0 + T2A],
                        xn1_f[:, k * T2A:(k + 1) * T2A], reads=[buf("xn1")], tag="stxn1f")
                    dma("gpsimd", sc_xn1b[b, k * 128:(k + 1) * 128, c0:c0 + T2A],
                        xn1_f[:, k * T2A:(k + 1) * T2A], reads=[buf("xn1")], tag="stxn1b")

        sch.barrier()

    ar.phase_reset()
    wfc_bf = [ar.alloc(NKM * IM // 2, BF16, "wfc") for _ in range(2)]
    wpj_bf = ar.alloc(NEXP * 4 * DM // 2, BF16, "wpj")
    wg_bf = ar.alloc(NKM * NEXP // 2 + 8, BF16, "wg")
    xn1b_s = [ar.alloc(NKM * T2B // 2, BF16, "xn1b2b") for _ in range(2)]
    xn1f_s = ar.alloc(T2B, name="xn1f2b")
    rt_f = [ar.alloc(T2B, name="rtf")] * 2
    rtT_f = ar.alloc(NEXP, name="rtT")
    gcol = ar.alloc(8, name="gcols")
    gmask = ar.alloc(4 * NEXP, name="gmask")
    gT_f = [ar.alloc(T2B, name="gT")] * 2
    gstage = ar.alloc(T2B + 8, name="gstage")
    gbc_f = [ar.alloc(NEXP * T2B // 2, BF16, "gbc") for _ in range(2)]
    hg_bf = ar.alloc(4 * NEXP * T2B // 2, BF16, "hg")
    r2_f = [ar.alloc(NKM * T2B, name="r2") for _ in range(2)]
    r2T_f = ar.alloc(DM, name="r2T")
    sq2_f = ar.alloc(DM, name="sq2b")
    st_c = ar.alloc(8, name="stats2b")
    out_f = ar.alloc(DM, name="out2b")

    out_d = dram["out"]
    NT2B = LC // T2B
    if stop_phase >= 4:
        for e in range(NEXP):
            for k in range(4):
                dma("gpsimd", wpj_bf[:, (e * 4 + k) * DM:(e * 4 + k + 1) * DM],
                    dram["w_projT"][e, k * 128:(k + 1) * 128, :], writes=[buf("w_pj")],
                    tag="ldwpj")
        for k in range(NKM):
            dma("gpsimd", wg_bf[:, k * NEXP:(k + 1) * NEXP],
                dram["w_gateT"][k * 128:(k + 1) * 128, :], writes=[buf("w_wg")],
                tag="ldwg")

        m1c = gcol[:, 0:1]
        nm1c = gcol[:, 1:2]
        m2c = gcol[:, 2:3]
        e2c = gcol[:, 3:4]
        w1c = gcol[:, 4:5]
        w2c = gcol[:, 5:6]
        scol2 = st_c[:, 0:1]
        vcol2 = st_c[:, 1:2]
        rcol2 = st_c[:, 2:3]
        mcol2 = st_c[:, 3:4]

        def emit_ln2(b, blk, qp_):
            c0 = blk * T2B
            for sub in range(T2B // 128):
                sc0 = sub * 128
                for k in range(NKM):
                    pbk = bank[k % 2]
                    bpb = buf(f"pb{k % 2}")
                    sch.add("tensor",
                            lambda m, k=k, sc0=sc0, p=pbk, qp_=qp_: m.transpose(
                                p[:, 0:128],
                                r2_f[qp_][:, k * T2B + sc0:k * T2B + sc0 + 128], id_sb),
                            reads=[buf(f"r2{qp_}"), buf("c_ident")], writes=[bpb],
                            tag="trans2")
                    sch.add("scalar",
                            lambda m, k=k, p=pbk: m.activation(
                                out=r2T_f[:, k * 128:(k + 1) * 128], in_=p[:, 0:128],
                                func=AF.Copy, bias=0.0, scale=1.0),
                            reads=[bpb], writes=[buf("r2T")], tag="trev")
                sch.add("vector",
                        lambda m: m.tensor_reduce(out=scol2, in_=r2T_f, axis=AX.X,
                                                  op=ALU.add),
                        reads=[buf("r2T")], writes=[buf("st2")], strict=True,
                        tag="sum2")
                sch.add("vector",
                        lambda m: m.tensor_scalar(out=mcol2, in0=scol2,
                                                  scalar1=1.0 / DM, scalar2=None,
                                                  op0=ALU.mult),
                        reads=[buf("st2")], writes=[buf("st2")], strict=True,
                        tag="mean2")
                sch.add("scalar",
                        lambda m: m.activation(out=sq2_f, in_=r2T_f, func=AF.Square,
                                               bias=0.0, scale=1.0),
                        reads=[buf("r2T")], writes=[buf("sq2b")], tag="sq2")
                sch.add("vector",
                        lambda m: m.tensor_reduce(out=vcol2, in_=sq2_f, axis=AX.X,
                                                  op=ALU.add),
                        reads=[buf("sq2b")], writes=[buf("st2")], strict=True,
                        tag="sumsq2")
                sch.add("vector",
                        lambda m: m.tensor_scalar(out=vcol2, in0=vcol2,
                                                  scalar1=1.0 / DM, scalar2=None,
                                                  op0=ALU.mult),
                        reads=[buf("st2")], writes=[buf("st2")], strict=True, tag="v2")
                sch.add("vector",
                        lambda m: m.tensor_mul(rcol2, mcol2, mcol2),
                        reads=[buf("st2")], writes=[buf("st2")], strict=True,
                        tag="m22")
                sch.add("vector",
                        lambda m: m.tensor_sub(vcol2, vcol2, rcol2),
                        reads=[buf("st2")], writes=[buf("st2")], strict=True,
                        tag="var2")
                sch.add("scalar",
                        lambda m: m.activation(out=rcol2, in_=vcol2, func=AF.Ln,
                                               bias=epsc_sb, scale=1.0),
                        reads=[buf("st2"), buf("c_eps")], writes=[buf("st2")],
                        strict=True, tag="lnv2")
                sch.add("scalar",
                        lambda m: m.activation(out=rcol2, in_=rcol2, func=AF.Exp,
                                               bias=0.0, scale=-0.5),
                        reads=[buf("st2")], writes=[buf("st2")], strict=True,
                        tag="rstd2")
                sch.add("vector",
                        lambda m: m.tensor_scalar(
                            out=out_f, in0=r2T_f, scalar1=mcol2, scalar2=rcol2,
                            op0=ALU.subtract, op1=ALU.mult),
                        reads=[buf("r2T"), buf("st2")], writes=[buf("outb")],
                        strict=True, tag="xn2")
                dma("sync", out_d[b, c0 + sc0:c0 + sc0 + 128, :], out_f,
                    reads=[buf("outb")], tag="stout")

        def ld_xn1b(b, blk):
            c0 = blk * T2B
            qp_ = blk % 2
            for k in range(NKM):
                dma("scalar", xn1b_s[qp_][:, k * T2B:(k + 1) * T2B],
                    sc_xn1b[b, k * 128:(k + 1) * 128, c0:c0 + T2B],
                    writes=[buf(f"xn1b2b{qp_}")], tag="ldxn1b")

        blocks2b = [(b, blk) for b in range(B) for blk in range(NT2B)]
        ld_xn1b(*blocks2b[0])
        pending_ln2 = None
        for bi2, (b, blk) in enumerate(blocks2b):
                c0 = blk * T2B
                qp_ = blk % 2
                xn1b = xn1b_s[qp_]
                bxn1b = buf(f"xn1b2b{qp_}")
                rtv = rt_f[qp_]
                brt = buf("rtf")
                gTv = gT_f[qp_]
                bgT = buf("gTf")
                gbcv = gbc_f[qp_]
                bgbc = buf(f"gbc{qp_}")
                for k in range(NKM):
                    mm(bank[0][0:NEXP, 0:T2B], wg_bf[:, k * NEXP:(k + 1) * NEXP],
                       xn1b[:, k * T2B:(k + 1) * T2B], k == 0, k == NKM - 1,
                       reads=[bxn1b, buf("w_wg")], writes=[buf("pb0")],
                       tag="router")
                sch.add("vector",
                        lambda m, rtv=rtv: m.tensor_copy(rtv[0:NEXP, 0:T2B],
                                                         bank[0][0:NEXP, 0:T2B]),
                        reads=[buf("pb0")], writes=[brt], tag="rtev")
                for sub in range(T2B // 128):
                    sc0 = sub * 128
                    mm(bank[1][:, 0:NEXP], rtv[0:NEXP, sc0:sc0 + 128],
                       id_sb[0:NEXP, 0:NEXP], True, True,
                       reads=[brt, buf("c_ident")], writes=[buf("pb1")], tag="rtT")
                    sch.add("vector",
                            lambda m: m.tensor_copy(rtT_f[:, 0:NEXP], bank[1][:, 0:NEXP]),
                            reads=[buf("pb1")], writes=[buf("rtT")], tag="rtTev")
                    sch.add("vector",
                            lambda m: m.tensor_reduce(out=m1c, in_=rtT_f[:, 0:NEXP],
                                                      axis=AX.X, op=ALU.max),
                            reads=[buf("rtT")], writes=[buf("gcols")], strict=True,
                            tag="m1")
                    sch.add("vector",
                            lambda m: m.tensor_single_scalar(
                                out=gmask[:, 0:NEXP], in_=rtT_f[:, 0:NEXP], scalar=m1c,
                                op=ALU.is_ge),
                            reads=[buf("rtT"), buf("gcols")], writes=[buf("gmask")],
                            strict=True, tag="ismax")
                    sch.add("vector",
                            lambda m: m.scalar_tensor_tensor(
                                out=gmask[:, NEXP:2 * NEXP], in0=gmask[:, 0:NEXP],
                                scalar=-1e30, in1=rtT_f[:, 0:NEXP],
                                op0=ALU.mult, op1=ALU.add),
                            reads=[buf("gmask"), buf("rtT")], writes=[buf("gmask")],
                            strict=True, tag="maskmax")
                    sch.add("vector",
                            lambda m: m.tensor_reduce(out=m2c,
                                                      in_=gmask[:, NEXP:2 * NEXP],
                                                      axis=AX.X, op=ALU.max),
                            reads=[buf("gmask")], writes=[buf("gcols")], strict=True,
                            tag="m2")
                    sch.add("vector",
                            lambda m: m.tensor_scalar(out=nm1c, in0=m1c, scalar1=-1.0,
                                                      scalar2=None, op0=ALU.mult),
                            reads=[buf("gcols")], writes=[buf("gcols")], strict=True,
                            tag="nm1")
                    sch.add("scalar",
                            lambda m: m.activation(out=e2c, in_=m2c, func=AF.Exp,
                                                   bias=nm1c, scale=1.0),
                            reads=[buf("gcols")], writes=[buf("gcols")], strict=True,
                            tag="e2")
                    sch.add("vector",
                            lambda m: m.tensor_scalar(out=w1c, in0=e2c, scalar1=1.0,
                                                      scalar2=None, op0=ALU.add),
                            reads=[buf("gcols")], writes=[buf("gcols")], strict=True,
                            tag="den")
                    sch.add("vector",
                            lambda m: m.reciprocal(out=w1c, in_=w1c),
                            reads=[buf("gcols")], writes=[buf("gcols")], strict=True,
                            tag="w1")
                    sch.add("vector",
                            lambda m: m.tensor_mul(w2c, e2c, w1c),
                            reads=[buf("gcols")], writes=[buf("gcols")], strict=True,
                            tag="w2")
                    sch.add("vector",
                            lambda m: m.tensor_single_scalar(
                                out=gmask[:, 2 * NEXP:3 * NEXP],
                                in_=gmask[:, NEXP:2 * NEXP], scalar=m2c, op=ALU.is_ge),
                            reads=[buf("gmask"), buf("gcols")], writes=[buf("gmask")],
                            strict=True, tag="ismax2")
                    sch.add("vector",
                            lambda m: m.tensor_single_scalar(
                                out=gmask[:, 0:NEXP], in_=gmask[:, 0:NEXP], scalar=w1c,
                                op=ALU.mult),
                            reads=[buf("gmask"), buf("gcols")], writes=[buf("gmask")],
                            strict=True, tag="gw1")
                    sch.add("vector",
                            lambda m: m.scalar_tensor_tensor(
                                out=gmask[:, 3 * NEXP:4 * NEXP],
                                in0=gmask[:, 2 * NEXP:3 * NEXP], scalar=w2c,
                                in1=gmask[:, 0:NEXP], op0=ALU.mult, op1=ALU.add),
                            reads=[buf("gmask"), buf("gcols")], writes=[buf("gmask")],
                            strict=True, tag="gates")
                    mm(bank[2][0:NEXP, 0:128], gmask[:, 3 * NEXP:4 * NEXP],
                       id_sb[:, 0:128], True, True,
                       reads=[buf("gmask"), buf("c_ident")], writes=[buf("pb2")],
                       tag="gTmm")
                    sch.add("vector",
                            lambda m, sc0=sc0, gTv=gTv: m.tensor_copy(
                                gTv[0:NEXP, sc0:sc0 + 128], bank[2][0:NEXP, 0:128]),
                            reads=[buf("pb2")], writes=[bgT], tag="gTev")
                for e in range(NEXP):
                    stg = gstage[0:1, 0:T2B]
                    bst = buf("gst")
                    dma("sync", stg, gTv[e:e + 1, 0:T2B], reads=[bgT],
                        writes=[bst], tag="gstage")
                    mm(bank[3][:, 0:T2B], onesr_sb[0:1, :], stg, True, True,
                       reads=[bst, buf("c_onesr")], writes=[buf("pb3")], tag="gbcmm")
                    sch.add("scalar",
                            lambda m, e=e, gbcv=gbcv: m.activation(
                                out=gbcv[:, e * T2B:(e + 1) * T2B], in_=bank[3][:, 0:T2B],
                                func=AF.Copy, bias=0.0, scale=1.0),
                            reads=[buf("pb3")], writes=[bgbc], tag="gbcev")
                for e in range(NEXP):
                    ep_ = e % 2
                    bwfc = buf(f"w_fc{ep_}")
                    dma("gpsimd",
                        wfc_bf[ep_][:, 0:NKM * IM].rearrange("p (k i) -> p k i", i=IM),
                        dram["w_fcT"][e].rearrange("(k p) i -> p k i", p=128),
                        writes=[bwfc], tag="ldwfc")
                    for mi in range(4):
                        _fb = (2, 4, 5)[(e * 4 + mi) % 3]
                        pbk = bank[_fb]
                        bpb = buf(f"pb{_fb}")
                        for k in range(NKM):
                            mm(pbk[:, 0:T2B],
                               wfc_bf[ep_][:, k * IM + mi * 128:k * IM + (mi + 1) * 128],
                               xn1b[:, k * T2B:(k + 1) * T2B], k == 0, k == NKM - 1,
                               reads=[bxn1b, bwfc], writes=[bpb], tag="fc")
                        hslot = e * 4 + mi
                        sch.add("scalar",
                                lambda m, p=pbk, hs=hslot: m.activation(
                                    out=hg_bf[:, hs * T2B:(hs + 1) * T2B],
                                    in_=p[:, 0:T2B], func=AF.Gelu, bias=0.0, scale=1.0),
                                reads=[bpb], writes=[buf("hg")], tag="gelu")
                        sch.add("vector",
                                lambda m, hs=hslot, e=e, gbcv=gbcv: m.tensor_mul(
                                    hg_bf[:, hs * T2B:(hs + 1) * T2B],
                                    hg_bf[:, hs * T2B:(hs + 1) * T2B],
                                    gbcv[:, e * T2B:(e + 1) * T2B]),
                                reads=[buf("hg"), bgbc], writes=[buf("hg")],
                                strict=True, tag="hgmul")
                if bi2 + 1 < len(blocks2b):
                    ld_xn1b(*blocks2b[bi2 + 1])
                for mo in range(NKM):
                    pbk = bank[6 + mo % 2]
                    bpb = buf(f"pb{6 + mo % 2}")
                    first = True
                    for e in range(NEXP):
                        for ki in range(4):
                            mm(pbk[:, 0:T2B],
                               wpj_bf[:, (e * 4 + ki) * DM + mo * 128:
                                      (e * 4 + ki) * DM + (mo + 1) * 128],
                               hg_bf[:, (e * 4 + ki) * T2B:(e * 4 + ki + 1) * T2B],
                               first, (e == NEXP - 1 and ki == 3),
                               reads=[buf("hg"), buf("w_pj")], writes=[bpb], tag="proj")
                            first = False
                    dma("scalar", xn1f_s,
                        sc_xn1f[b, mo * 128:(mo + 1) * 128, c0:c0 + T2B],
                        writes=[buf("xn1f2b")], tag="ldxn1f")
                    sch.add("vector",
                            lambda m, mo=mo, p=pbk, qp_=qp_: m.tensor_add(
                                r2_f[qp_][:, mo * T2B:(mo + 1) * T2B], p[:, 0:T2B],
                                xn1f_s),
                            reads=[bpb, buf("xn1f2b")], writes=[buf(f"r2{qp_}")],
                            tag="r2")
                if pending_ln2 is not None:
                    pending_ln2()
                pending_ln2 = (lambda b=b, blk=blk, qp_=qp_: emit_ln2(b, blk, qp_))
        if pending_ln2 is not None:
            pending_ln2()

        sch.barrier()

    sch.finalize(nc, stack)
    block = stack.enter_context(nc.Block())

    @block.sync
    def _(m):
        sch.emit_engine(m, "sync")

    @block.tensor
    def _(m):
        sch.emit_engine(m, "tensor")

    @block.vector
    def _(m):
        sch.emit_engine(m, "vector")

    @block.scalar
    def _(m):
        sch.emit_engine(m, "scalar")

    @block.gpsimd
    def _(m):
        sch.emit_engine(m, "gpsimd")


# version 27
# speedup vs baseline: 1.0321x; 1.0016x over previous
import contextlib
import numpy as np
import concourse.bass as bass
import concourse.mybir as mybir
from concourse.bass_utils import run_bass_kernel_spmd

F32 = mybir.dt.float32
BF16 = mybir.dt.bfloat16
AF = mybir.ActivationFunctionType
ALU = mybir.AluOpType
AX = mybir.AxisListType

NCORES = 8
B = 2
L = 16384
LC = L // NCORES
DM = 1024
DI = 2048
E4 = 2 * DI
NS = 16
RK = 64
NEXP = 8
IM = 512
NDT = DI // 128
NKM = DM // 128

T1A = 512
T1B = 1024
T2A = 512
T2B = 512
EPS = 1e-5

ENGINES = ("sync", "tensor", "vector", "scalar", "gpsimd")


class _Op:
    __slots__ = ("eng", "emit", "deps", "inc", "count", "used", "strict", "tag",
                 "semkey")

    def __init__(self, eng, emit, deps, inc, strict, tag):
        self.eng = eng
        self.emit = emit
        self.deps = deps
        self.inc = inc
        self.count = 0
        self.used = False
        self.strict = strict
        self.tag = tag
        self.semkey = eng


class Buf:
    __slots__ = ("name", "writer", "readers")

    def __init__(self, name=""):
        self.name = name
        self.writer = None
        self.readers = []


NDMALANES = 8


class Sched:
    def __init__(self):
        self.q = {e: [] for e in ENGINES}
        self.all = []
        self._dma_idx = {e: 0 for e in ENGINES}

    def add(self, eng, emit, reads=(), writes=(), deps=(), dma=False,
            strict=False, tag=""):
        d = []
        for b in reads:
            if b.writer is not None:
                d.append(b.writer)
        for b in writes:
            if b.writer is not None:
                d.append(b.writer)
            d.extend(b.readers)
        d.extend(x for x in deps if x is not None)
        op = _Op(eng, emit, d, 16 if dma else 1, strict, tag)
        if dma:
            lane = self._dma_idx[eng] % NDMALANES
            self._dma_idx[eng] += 1
            op.semkey = (eng, lane)
        for b in reads:
            b.readers.append(op)
        for b in writes:
            b.writer = op
            b.readers = []
        self.q[eng].append(op)
        self.all.append(op)
        return op

    def barrier(self):
        last = {}
        for e in ENGINES:
            for op in reversed(self.q[e]):
                if op.emit is None:
                    continue
                if op.semkey not in last:
                    last[op.semkey] = op
                if op.semkey == e:
                    break
        for e in ENGINES:
            deps = [op for op in last.values() if op.eng != e]
            deps += [op for op in last.values()
                     if op.eng == e and op.semkey != e]
            self.add(e, None, deps=deps, strict=True, tag="barrier")

    def finalize(self, nc, stack):
        pos = {}
        for e in ENGINES:
            for i, op in enumerate(self.q[e]):
                pos[id(op)] = i
        for op in self.all:
            if op.inc == 16 and op.emit is not None:
                op.used = True
            wd = []
            for dp in op.deps:
                if dp.eng != op.eng or dp.semkey != dp.eng:
                    dp.used = True
                    wd.append(dp)
                elif op.strict or pos[id(dp)] >= pos[id(op)] - 2:
                    if dp.emit is not None:
                        dp.used = True
                        wd.append(dp)
            op.deps = wd
        self.sems = {}
        counts = {}
        for e in ENGINES:
            for op in self.q[e]:
                if op.used:
                    assert op.emit is not None, f"dep on no-op ({op.tag})"
                    counts[op.semkey] = counts.get(op.semkey, 0) + op.inc
                op.count = counts.get(op.semkey, 0)
        for k, c in counts.items():
            assert c < 60000, f"semaphore overflow on {k}: {c}"
        for e in ENGINES:
            self.sems[e] = stack.enter_context(nc.semaphore(f"s_{e}"))
            for ln in range(NDMALANES):
                self.sems[(e, ln)] = stack.enter_context(
                    nc.semaphore(f"s_{e}_d{ln}"))

    def emit_engine(self, engmod, e):
        sems = self.sems
        waited = {}
        for op in self.q[e]:
            for dp in op.deps:
                if dp.count > waited.get(dp.semkey, 0):
                    engmod.wait_ge(sems[dp.semkey], dp.count)
                    waited[dp.semkey] = dp.count
            if op.emit is not None:
                inst = op.emit(engmod)
                if op.used:
                    inst.then_inc(sems[op.semkey], op.inc)

    def stats(self):
        return {e: (len(self.q[e]), self.q[e][-1].count if self.q[e] else 0)
                for e in ENGINES}


def _split_shards(x):
    xt = np.ascontiguousarray(np.swapaxes(np.asarray(x, np.float32), 1, 2))
    shards = []
    for i in range(NCORES):
        t0 = i * LC
        sl = np.zeros((B, DM, 3 + LC), np.float32)
        sl[:, :, 3:] = xt[:, :, t0:t0 + LC]
        if t0 >= 3:
            sl[:, :, 0:3] = xt[:, :, t0 - 3:t0]
        shards.append(np.ascontiguousarray(sl))
    return shards


def _vec2d(v):
    return np.ascontiguousarray(np.asarray(v, np.float32).reshape(NDT, 128).T)


def prepare_in_maps(L0_attn_x, L0_attn_W_in, L0_attn_conv_w, L0_attn_conv_b,
                    L0_attn_W_x, L0_attn_W_dt, L0_attn_b_dt, L0_attn_A_log,
                    L0_attn_Dp, L0_attn_W_out, L0_ffn_w_gate, L0_ffn_w_fc,
                    L0_ffn_w_proj):
    f32 = np.float32
    xsh = _split_shards(L0_attn_x)

    w_inT = np.ascontiguousarray(np.asarray(L0_attn_W_in, f32).T)
    w_xT = np.ascontiguousarray(np.asarray(L0_attn_W_x, f32).T)
    w_dtT = np.ascontiguousarray(np.asarray(L0_attn_W_dt, f32).T)
    w_outT = np.ascontiguousarray(np.asarray(L0_attn_W_out, f32).T)
    w_gateT = np.ascontiguousarray(np.asarray(L0_ffn_w_gate, f32).T)
    w_fcT = np.ascontiguousarray(np.swapaxes(np.asarray(L0_ffn_w_fc, f32), 1, 2))
    w_projT = np.ascontiguousarray(np.swapaxes(np.asarray(L0_ffn_w_proj, f32), 1, 2))

    cw2 = np.zeros((128, NDT * 4), f32)
    cwa = np.asarray(L0_attn_conv_w, f32)
    for dt in range(NDT):
        cw2[:, dt * 4:(dt + 1) * 4] = cwa[dt * 128:(dt + 1) * 128, :]
    a2 = np.zeros((128, NDT * NS), f32)
    al = np.asarray(L0_attn_A_log, f32)
    for dt in range(NDT):
        a2[:, dt * NS:(dt + 1) * NS] = al[dt * 128:(dt + 1) * 128, :]

    cb2v = _vec2d(L0_attn_conv_b)
    bdt2v = _vec2d(L0_attn_b_dt)
    dp2v = _vec2d(L0_attn_Dp)
    in_maps = []
    for i in range(NCORES):
        mask = np.zeros((128, 8), f32)
        mask[:, :i] = 1.0
        in_maps.append({
            "xT": xsh[i], "w_inT": w_inT, "w_xT": w_xT, "w_dtT": w_dtT,
            "w_outT": w_outT, "w_gateT": w_gateT, "w_fcT": w_fcT,
            "w_projT": w_projT, "conv_w2": cw2,
            "conv_b2": cb2v, "b_dt2": bdt2v,
            "dp2": dp2v, "a_log2": a2,
            "ident": np.eye(128, dtype=f32), "rkmask": mask,
        })
    return in_maps


def kernel(**inputs):
    in_maps = prepare_in_maps(**inputs)
    nc = build_program()
    res = run_bass_kernel_spmd(nc, in_maps, core_ids=list(range(NCORES)))
    out = np.empty((B, L, DM), np.float32)
    for i in range(NCORES):
        out[:, i * LC:(i + 1) * LC, :] = res.results[i]["out"]
    return out


class Arena:
    def __init__(self, nc, total_f32=53100):
        self.t = nc.alloc_sbuf_tensor("arena", [128, total_f32], F32)
        self.total = total_f32
        self.perm_off = 0
        self.off = 0

    def perm(self, n_f32, dtype=F32, name=""):
        ap = self._view(self.perm_off, n_f32, dtype)
        self.perm_off += n_f32
        self.off = max(self.off, self.perm_off)
        return ap

    def phase_reset(self):
        self.off = self.perm_off

    def alloc(self, n_f32, dtype=F32, name=""):
        ap = self._view(self.off, n_f32, dtype)
        self.off += n_f32
        assert self.off <= self.total, f"arena overflow at {name}: {self.off}"
        return ap

    def _view(self, off, n_f32, dtype):
        ap = self.t.ap()[:, off:off + n_f32]
        if dtype == BF16:
            ap = ap.bitcast(BF16)
        return ap


def build_program(stop_phase=99, dbg_out=False):
    nc = bass.Bass("TRN2", target_bir_lowering=False)
    dram = {}

    def din(name, shape, dtype=F32):
        dram[name] = nc.dram_tensor(name, list(shape), dtype, kind="ExternalInput").ap()

    def dint(name, shape, dtype=F32):
        kind = "ExternalOutput" if dbg_out else "Internal"
        dram[name] = nc.dram_tensor(name, list(shape), dtype, kind=kind).ap()

    din("xT", (B, DM, 3 + LC))
    din("w_inT", (DM, E4))
    din("w_xT", (DI, 96))
    din("w_dtT", (RK, DI))
    din("w_outT", (DI, DM))
    din("w_gateT", (DM, NEXP))
    din("w_fcT", (NEXP, DM, IM))
    din("w_projT", (NEXP, IM, DM))
    din("conv_w2", (128, NDT * 4))
    din("conv_b2", (128, NDT))
    din("b_dt2", (128, NDT))
    din("dp2", (128, NDT))
    din("a_log2", (128, NDT * NS))
    din("ident", (128, 128))
    din("rkmask", (128, NCORES))

    dram["out"] = nc.dram_tensor("out", [B, LC, DM], F32, kind="ExternalOutput").ap()

    dint("sc_xconv", (B, DI, LC))
    dint("sc_sz", (B, DI, LC))
    dint("sc_dt", (B, DI, LC))
    dint("sc_S", (B, DI, LC))
    dint("sc_BC", (B, 32, LC), BF16)
    dint("sc_gated", (B, DI, LC), BF16)
    dint("sc_xn1f", (B, DM, LC))
    dint("sc_xn1b", (B, DM, LC), BF16)
    if dbg_out:
        dram["dbg_bbc"] = nc.dram_tensor("dbg_bbc", [B, 2, 2, 128, NS * T1B], F32, kind="ExternalOutput").ap()
        dram["dbg_u"] = nc.dram_tensor("dbg_u", [B, DI, LC], F32, kind="ExternalOutput").ap()
        dram["dbg_y"] = nc.dram_tensor("dbg_y", [B, DI, LC], F32, kind="ExternalOutput").ap()
        dram["dbg_dA"] = nc.dram_tensor("dbg_dA", [B, DI, LC], F32, kind="ExternalOutput").ap()
    dram["cc_in"] = nc.dram_tensor("cc_in", [128, 1024], F32, kind="Internal").ap()
    dram["cc_out"] = nc.dram_tensor("cc_out", [128 * NCORES, 1024], F32, kind="Internal").ap()

    sch = Sched()
    with contextlib.ExitStack() as stack:
        build_body(nc, stack, sch, dram, stop_phase)
    return nc


def build_body(nc, stack, sch, dram, stop_phase):
    DBG = "dbg_u" in dram
    ar = Arena(nc)
    banks = [stack.enter_context(nc.psum_tensor(f"pb{i}", [128, 512], F32))
             for i in range(8)]
    bank = [bk.ap() for bk in banks]

    A_sb = ar.perm(NDT * NS, name="A")
    cw_sb = ar.perm(NDT * 4, name="cw")
    cb_sb = ar.perm(NDT, name="cb")
    bdt_sb = ar.perm(NDT, name="bdt")
    dp_sb = ar.perm(NDT, name="dp")
    rk_sb = ar.perm(NCORES, name="rk")
    rkc_sb = ar.perm(NCORES, name="rkc")
    id_sb = ar.perm(128, name="ident")
    onesr_sb = ar.perm(128, name="onesrow")
    onesc_sb = ar.perm(1, name="onescol")
    epsc_sb = ar.perm(1, name="epscol")
    zeros_sb = ar.perm(T1A, name="zeros")
    srun_sb = ar.perm(B * NDT, name="srun")
    hst_sb = ar.perm(B * NDT * NS, name="hst")
    D_sb = ar.perm(B * NDT * NS, name="D")
    h0_sb = ar.perm(B * NDT * NS, name="h0")


    bufs = {}

    def buf(name):
        if name not in bufs:
            bufs[name] = Buf(name)
        return bufs[name]

    def dma(eng, dst, src, *, reads=(), writes=(), deps=(), tag=""):
        return sch.add(eng, lambda m, d=dst, s=src: m.dma_start(out=d, in_=s),
                       reads=reads, writes=writes, deps=deps, dma=True, tag=tag)

    def mm(pb, lhsT, rhs, start, stop, reads, writes, tag=""):
        return sch.add("tensor",
                       lambda m, o=pb, a=lhsT, r=rhs, s1=start, s2=stop:
                       m.matmul(o, a, r, start=s1, stop=s2),
                       reads=reads, writes=writes, tag=tag)

    xT = dram["xT"]
    sc_xconv, sc_sz, sc_dt, sc_S, sc_BC = (dram["sc_xconv"], dram["sc_sz"],
                                           dram["sc_dt"], dram["sc_S"],
                                           dram["sc_BC"])
    sc_gated, sc_xn1f, sc_xn1b = (dram["sc_gated"], dram["sc_xn1f"],
                                  dram["sc_xn1b"])

    ar.phase_reset()
    wi_bf = [ar.alloc(E4 // 2, BF16, "wi") for _ in range(NKM)]
    wx_bf = ar.alloc(NDT * 96 // 2 + 48, BF16, "wx")
    wdt_bf = ar.alloc(DI // 2, BF16, "wdt")

    for name, dst in (("conv_w2", cw_sb), ("conv_b2", cb_sb),
                      ("b_dt2", bdt_sb), ("dp2", dp_sb), ("rkmask", rk_sb),
                      ("ident", id_sb), ("a_log2", A_sb)):
        dma("sync", dst, dram[name], writes=[buf("c_" + name)], tag="ldc")

    sch.add("scalar",
            lambda m: m.activation(out=A_sb, in_=A_sb, func=AF.Exp, bias=0.0, scale=1.0),
            reads=[buf("c_a_log2")], writes=[buf("c_a_log2")], tag="Aexp")
    sch.add("vector",
            lambda m: m.tensor_scalar(out=A_sb, in0=A_sb, scalar1=-1.0,
                                      scalar2=None, op0=ALU.mult),
            reads=[buf("c_a_log2")], writes=[buf("c_a_log2")], strict=True, tag="Aneg")
    sch.add("vector",
            lambda m: m.tensor_scalar(out=rkc_sb, in0=rk_sb, scalar1=-1.0,
                                      scalar2=1.0, op0=ALU.mult, op1=ALU.add),
            reads=[buf("c_rkmask")], writes=[buf("c_rkc")], tag="rkc")
    sch.add("vector", lambda m: m.memset(zeros_sb, 0.0), writes=[buf("c_zeros")])
    sch.add("vector", lambda m: m.memset(srun_sb, 0.0), writes=[buf("c_srun")])
    sch.add("vector", lambda m: m.memset(hst_sb, 0.0), writes=[buf("c_hst")])
    sch.add("vector", lambda m: m.memset(onesr_sb[0:1, :], 1.0), writes=[buf("c_onesr")])
    sch.add("vector", lambda m: m.memset(onesc_sb, 1.0), writes=[buf("c_onesc")])
    sch.add("vector", lambda m: m.memset(epsc_sb, EPS), writes=[buf("c_eps")])

    for k in range(NKM):
        dma("gpsimd", wi_bf[k], dram["w_inT"][k * 128:(k + 1) * 128, :],
            writes=[buf(f"w_wi{k}")], tag="ldwi")
    for k in range(NDT):
        dma("gpsimd", wx_bf[:, k * 96:(k + 1) * 96],
            dram["w_xT"][k * 128:(k + 1) * 128, :], writes=[buf("w_wx")], tag="ldwx")
    dma("gpsimd", wdt_bf[0:RK, 0:DI], dram["w_dtT"], writes=[buf("w_wdt")], tag="ldwdt")

    xb_bf = [ar.alloc(NKM * (T1A + 3) // 2 + 64, BF16, "xb") for _ in range(2)]
    xcf = [ar.alloc(T1A + 3, name="xcf") for _ in range(2)]
    halbuf = ar.alloc(NDT * 3, name="hal")
    accb_t = [ar.alloc(T1A, name="acc") for _ in range(NDT)]
    xconv_t = [ar.alloc(T1A, name="xconv") for _ in range(2)]
    sz_t = [ar.alloc(T1A, name="sz") for _ in range(2)]
    xcv_bf = ar.alloc(NDT * T1A // 2, BF16, "xcv")
    dtl_bf = ar.alloc(T1A // 2, BF16, "dtl")
    bc_f = [ar.alloc(T1A, name="bc") for _ in range(2)]
    e1b_t = [ar.alloc(T1A, name="e1") for _ in range(2)]
    e2b_t = [ar.alloc(T1A, name="e2") for _ in range(2)]
    rlb_t = [ar.alloc(T1A, name="rl") for _ in range(2)]
    dt_t = [ar.alloc(T1A, name="dt") for _ in range(2)]
    S_t = [ar.alloc(T1A, name="S") for _ in range(2)]

    NT1A = LC // T1A

    if stop_phase >= 1:
        for b in range(B):
            for tt in range(NT1A):
                c0 = tt * T1A
                tp_ = tt % 2
                xb = xb_bf[tp_]
                bxb = buf(f"xb{tp_}")
                for k in range(NKM):
                    dma("gpsimd", xb[:, k * (T1A + 3):(k + 1) * (T1A + 3)],
                        xT[b, k * 128:(k + 1) * 128, c0:c0 + T1A + 3],
                        writes=[bxb], tag="ldx")
                if tt == 0:
                    for m_ in range(NDT):
                        for k in range(NKM):
                            mm(bank[7][:, 0:3], wi_bf[k][:, m_ * 128:(m_ + 1) * 128],
                               xb[:, k * (T1A + 3):k * (T1A + 3) + 3],
                               k == 0, k == NKM - 1,
                               reads=[bxb, buf(f"w_wi{k}")], writes=[buf("pb7")], tag="halo")
                        sch.add("vector",
                                lambda m, m_=m_: m.tensor_copy(
                                    halbuf[:, m_ * 3:(m_ + 1) * 3], bank[7][:, 0:3]),
                                reads=[buf("pb7")], writes=[buf("hal")], tag="haloev")

                for m_ in range(NDT):
                    pbk = bank[m_ % 4]
                    bpb = buf(f"pb{m_ % 4}")
                    for k in range(NKM):
                        mm(pbk, wi_bf[k][:, m_ * 128:(m_ + 1) * 128],
                           xb[:, k * (T1A + 3) + 3:(k + 1) * (T1A + 3)],
                           k == 0, k == NKM - 1,
                           reads=[bxb, buf(f"w_wi{k}")], writes=[bpb], tag="xz")
                    xcfull = xcf[m_ % 2]
                    bxc = buf(f"xcf{m_ % 2}")
                    sch.add("vector",
                            lambda m, x=xcfull, m_=m_: m.tensor_copy(
                                x[:, 0:3], halbuf[:, m_ * 3:(m_ + 1) * 3]),
                            reads=[buf("hal")], writes=[bxc], strict=True, tag="halcp")
                    sch.add("scalar",
                            lambda m, x=xcfull, p=pbk: m.activation(
                                out=x[:, 3:3 + T1A], in_=p, func=AF.Copy,
                                bias=0.0, scale=1.0),
                            reads=[bpb], writes=[bxc], strict=True, tag="xcev")
                    sch.add("vector",
                            lambda m, x=xcfull, m_=m_: m.tensor_copy(
                                halbuf[:, m_ * 3:(m_ + 1) * 3], x[:, T1A:T1A + 3]),
                            reads=[bxc], writes=[buf("hal")], strict=True, tag="halupd")
                    accb = accb_t[m_]
                    bacc = buf(f"acc{m_}")
                    sch.add("scalar",
                            lambda m, x=xcfull, m_=m_, accb=accb: m.activation(
                                out=accb, in_=x[:, 0:T1A], func=AF.Copy,
                                bias=0.0, scale=cw_sb[:, m_ * 4:m_ * 4 + 1]),
                            reads=[bxc, buf("c_conv_w2")], writes=[bacc], tag="conv0")
                    for j in (1, 2, 3):
                        sch.add("vector",
                                lambda m, x=xcfull, m_=m_, j=j, accb=accb:
                                m.scalar_tensor_tensor(
                                    out=accb, in0=x[:, j:j + T1A],
                                    scalar=cw_sb[:, m_ * 4 + j:m_ * 4 + j + 1],
                                    in1=accb, op0=ALU.mult, op1=ALU.add),
                                reads=[bxc, buf("c_conv_w2"), bacc],
                                writes=[bacc], strict=True, tag=f"conv{j}")
                for m_ in range(NDT):
                    accb = accb_t[m_]
                    bacc = buf(f"acc{m_}")
                    xcv = xconv_t[m_ % 2]
                    bxcv = buf(f"xconv{m_ % 2}")
                    sch.add("scalar",
                            lambda m, o=xcv, m_=m_, accb=accb: m.activation(
                                out=o, in_=accb, func=AF.Silu,
                                bias=cb_sb[:, m_:m_ + 1], scale=1.0),
                            reads=[bacc, buf("c_conv_b2")], writes=[bxcv], tag="silu")
                    sch.add("vector",
                            lambda m, o=xcv_bf[:, m_ * T1A:(m_ + 1) * T1A], i=xcv:
                            m.tensor_copy(o, i),
                            reads=[bxcv], writes=[buf("xcvbf")], tag="xcvcast")
                    dma("gpsimd", sc_xconv[b, m_ * 128:(m_ + 1) * 128, c0:c0 + T1A], xcv,
                        reads=[bxcv], tag="stxconv")

                for m_ in range(NDT):
                    pbk = bank[4 + m_ % 3]
                    bpb = buf(f"pb{4 + m_ % 3}")
                    for k in range(NKM):
                        mm(pbk, wi_bf[k][:, (NDT + m_) * 128:(NDT + m_ + 1) * 128],
                           xb[:, k * (T1A + 3) + 3:(k + 1) * (T1A + 3)],
                           k == 0, k == NKM - 1,
                           reads=[bxb, buf(f"w_wi{k}")], writes=[bpb], tag="z")
                    szv = sz_t[m_ % 2]
                    bsz = buf(f"sz{m_ % 2}")
                    sch.add("scalar",
                            lambda m, o=szv, p=pbk: m.activation(
                                out=o, in_=p, func=AF.Silu, bias=0.0, scale=1.0),
                            reads=[bpb], writes=[bsz], tag="siluz")
                    dma("gpsimd", sc_sz[b, m_ * 128:(m_ + 1) * 128, c0:c0 + T1A], szv,
                        reads=[bsz], tag="stsz")

                for k in range(NDT):
                    mm(bank[4][0:96, :], wx_bf[:, k * 96:(k + 1) * 96],
                       xcv_bf[:, k * T1A:(k + 1) * T1A],
                       k == 0, k == NDT - 1,
                       reads=[buf("xcvbf"), buf("w_wx")], writes=[buf("pb4")], tag="xdbl")
                sch.add("scalar",
                        lambda m: m.activation(out=dtl_bf[0:RK, 0:T1A],
                                               in_=bank[4][0:RK, :],
                                               func=AF.Copy, bias=0.0, scale=1.0),
                        reads=[buf("pb4")], writes=[buf("dtl")], tag="dtlcast")
                bcv = bc_f[tt % 2]
                bbc = buf(f"bc{tt % 2}")
                sch.add("vector",
                        lambda m, o=bcv: m.tensor_copy(o[0:32, :], bank[4][64:96, :]),
                        reads=[buf("pb4")], writes=[bbc], tag="bcev")
                dma("gpsimd", sc_BC[b, :, c0:c0 + T1A], bcv[0:32, :], reads=[bbc], tag="stbc")

                for m_ in range(NDT):
                    pbk = bank[m_ % 2]
                    bpb = buf(f"pb{m_ % 2}")
                    mm(pbk, wdt_bf[0:RK, m_ * 128:(m_ + 1) * 128], dtl_bf[0:RK, 0:T1A],
                       True, True, reads=[buf("dtl"), buf("w_wdt")], writes=[bpb], tag="dtp")
                    e1b = e1b_t[m_ % 2]
                    e2b = e2b_t[m_ % 2]
                    rlb = rlb_t[m_ % 2]
                    be1 = buf(f"e1{m_ % 2}")
                    be2 = buf(f"e2{m_ % 2}")
                    brl = buf(f"rl{m_ % 2}")
                    sch.add("scalar",
                            lambda m, p=pbk, m_=m_, e1b=e1b: m.activation(
                                out=e1b, in_=p, func=AF.Abs,
                                bias=bdt_sb[:, m_:m_ + 1], scale=1.0),
                            reads=[bpb, buf("c_b_dt2")], writes=[be1], tag="sp_abs")
                    sch.add("scalar",
                            lambda m, e1b=e1b, e2b=e2b: m.activation(
                                out=e2b, in_=e1b, func=AF.Exp, bias=0.0, scale=-1.0),
                            reads=[be1], writes=[be2], strict=True, tag="sp_exp")
                    sch.add("scalar",
                            lambda m, e2b=e2b: m.activation(
                                out=e2b, in_=e2b, func=AF.Ln, bias=1.0, scale=1.0),
                            reads=[be2], writes=[be2], strict=True, tag="sp_ln")
                    sch.add("scalar",
                            lambda m, p=pbk, m_=m_, rlb=rlb: m.activation(
                                out=rlb, in_=p, func=AF.Relu,
                                bias=bdt_sb[:, m_:m_ + 1], scale=1.0),
                            reads=[bpb, buf("c_b_dt2")], writes=[brl], tag="sp_relu")
                    dtv = dt_t[m_ % 2]
                    bdtv = buf(f"dt{m_ % 2}")
                    sch.add("vector",
                            lambda m, o=dtv, e2b=e2b, rlb=rlb: m.tensor_add(o, e2b, rlb),
                            reads=[be2, brl], writes=[bdtv], tag="sp_add")
                    Sv = S_t[m_ % 2]
                    bS = buf(f"S{m_ % 2}")
                    scol = srun_sb[:, b * NDT + m_:b * NDT + m_ + 1]
                    sch.add("vector",
                            lambda m, o=Sv, i=dtv, s=scol: m.tensor_tensor_scan(
                                out=o, data0=zeros_sb, data1=i, initial=s,
                                op0=ALU.add, op1=ALU.add),
                            reads=[bdtv, buf("c_zeros"), buf("c_srun")], writes=[bS],
                            strict=True, tag="scum")
                    sch.add("vector",
                            lambda m, i=Sv, s=scol: m.tensor_copy(s, i[:, T1A - 1:T1A]),
                            reads=[bS], writes=[buf("c_srun")], strict=True, tag="scol")
                    dma("gpsimd", sc_dt[b, m_ * 128:(m_ + 1) * 128, c0:c0 + T1A], dtv,
                        reads=[bdtv], tag="stdt")
                    dma("gpsimd", sc_S[b, m_ * 128:(m_ + 1) * 128, c0:c0 + T1A], Sv,
                        reads=[bS], tag="stS")

        sch.barrier()

    ar.phase_reset()
    Bbc = ar.alloc(NS * T1B // 2, BF16, "Bbc")
    Cbc = ar.alloc(NS * T1B // 2, BF16, "Cbc")
    dt_s = [ar.alloc(T1B, name="dt1b")] * 2
    xc_s = [ar.alloc(T1B, name="xc1b") for _ in range(2)]
    sz_s = [ar.alloc(T1B, name="sz1b") for _ in range(2)]
    u_s = [ar.alloc(T1B // 2, BF16, "u1b")] * 2
    dA_s = [ar.alloc(T1B, name="dA") for _ in range(2)]
    dBx_s = [ar.alloc(T1B // 2, BF16, "dBx") for _ in range(2)]
    h_s = [ar.alloc(T1B, name="h") for _ in range(2)]
    g_s = [ar.alloc(T1B // 2, BF16, "g1b") for _ in range(4)]
    yp_s = [[ar.alloc(T1B, name="yp") for _ in range(4)] for _ in range(2)]
    y_s = yp_s[0][0]
    yf_s = [ar.alloc(T1B, name="yf") for _ in range(2)]
    g_bf = [ar.alloc(T1B // 2, BF16, "gbf") for _ in range(2)]
    FD_j = yp_s[0][1]

    NT1B = LC // T1B
    if stop_phase >= 2:
        pending_gate = None
        for b in range(B):
            for blk in range(NT1B):
                c0 = blk * T1B
                for which, dst, bname in ((0, Bbc, "bbcB"), (1, Cbc, "bbcC")):
                    for n in range(NS):
                        dma("sync", dst[:, n * T1B:(n + 1) * T1B],
                            sc_BC[b, which * NS + n, c0:c0 + T1B]
                            .unsqueeze(0).partition_broadcast(128),
                            writes=[buf(f"{bname}{n}")], tag="ldbc")
                if DBG:
                    dma("sync", dram["dbg_bbc"][b, blk, 0, :, :], Bbc,
                        reads=[buf("bbcB")], tag="dbgb")
                    dma("sync", dram["dbg_bbc"][b, blk, 1, :, :], Cbc,
                        reads=[buf("bbcC")], tag="dbgc")
                for dtile in range(NDT):
                    dp_ = dtile % 2
                    bdt = buf("dt1b")
                    bxc = buf(f"xc1b{dp_}")
                    bsz = buf(f"sz1b{dp_}")
                    bu = buf("u1b")
                    dma("scalar", dt_s[dp_],
                        sc_dt[b, dtile * 128:(dtile + 1) * 128, c0:c0 + T1B],
                        writes=[bdt], tag="lddt")
                    dma("scalar", xc_s[dp_],
                        sc_xconv[b, dtile * 128:(dtile + 1) * 128, c0:c0 + T1B],
                        writes=[bxc], tag="ldxc")
                    dma("scalar", sz_s[dp_],
                        sc_sz[b, dtile * 128:(dtile + 1) * 128, c0:c0 + T1B],
                        writes=[bsz], tag="ldsz")
                    sch.add("vector",
                            lambda m, dp_=dp_: m.tensor_mul(u_s[dp_], dt_s[dp_], xc_s[dp_]),
                            reads=[bdt, bxc], writes=[bu], tag="umul")
                    if DBG:
                        dma("sync", dram["dbg_u"][b, dtile * 128:(dtile + 1) * 128, c0:c0 + T1B],
                            u_s[dp_], reads=[bu], tag="dbgu")
                    for n in range(NS):
                        np_ = n % 2
                        acol = A_sb[:, dtile * NS + n:dtile * NS + n + 1]
                        hcol = hst_sb[:, (b * NDT + dtile) * NS + n:
                                      (b * NDT + dtile) * NS + n + 1]
                        bdA = buf(f"dA{np_}")
                        bdBx = buf(f"dBx{np_}")
                        bh = buf(f"h{np_}")
                        bg = buf(f"g{np_}")
                        sch.add("scalar",
                                lambda m, a=acol, dp_=dp_, np_=np_: m.activation(
                                    out=dA_s[np_], in_=dt_s[dp_], func=AF.Exp,
                                    bias=0.0, scale=a),
                                reads=[bdt, buf("c_a_log2")], writes=[bdA], tag="dAexp")
                        sch.add("vector",
                                lambda m, n=n, dp_=dp_, np_=np_: m.tensor_mul(
                                    dBx_s[np_], u_s[dp_], Bbc[:, n * T1B:(n + 1) * T1B]),
                                reads=[bu, buf(f"bbcB{n}")], writes=[bdBx], tag="dBx")
                        sch.add("vector",
                                lambda m, h=hcol, np_=np_: m.tensor_tensor_scan(
                                    out=h_s[np_], data0=dA_s[np_], data1=dBx_s[np_],
                                    initial=h, op0=ALU.mult, op1=ALU.add),
                                reads=[bdA, bdBx, buf("c_hst")],
                                writes=[bh], strict=True, tag="scan")
                        sch.add("vector",
                                lambda m, h=hcol, np_=np_: m.tensor_copy(
                                    h, h_s[np_][:, T1B - 1:T1B]),
                                reads=[bh], writes=[buf("c_hst")], strict=True,
                                tag="hcol")
                        g4 = n % 4
                        bp = buf(f"yp{dp_}{g4}")
                        if n < 4:
                            sch.add("vector",
                                    lambda m, n=n, np_=np_, g4=g4, dp_=dp_:
                                    m.tensor_mul(yp_s[dp_][g4], h_s[np_],
                                                 Cbc[:, n * T1B:(n + 1) * T1B]),
                                    reads=[bh, buf(f"bbcC{n}")], writes=[bp],
                                    tag="ymul0")
                        else:
                            bg = buf(f"g{g4}")
                            sch.add("vector",
                                    lambda m, n=n, np_=np_: m.tensor_mul(
                                        g_s[n % 4], h_s[np_],
                                        Cbc[:, n * T1B:(n + 1) * T1B]),
                                    reads=[bh, buf(f"bbcC{n}")], writes=[bg],
                                    tag="ymul")
                            sch.add("gpsimd",
                                    lambda m, g4=g4, dp_=dp_: m.dma_start(
                                        out=yp_s[dp_][g4], in_=g_s[g4],
                                        accum_op=ALU.add),
                                    reads=[bg, bp], writes=[bp], dma=True, tag="yacc")
                    if DBG:
                        dma("sync", dram["dbg_y"][b, dtile * 128:(dtile + 1) * 128, c0:c0 + T1B],
                            y_s, reads=[buf("yp00")], tag="dbgy")
                    sch.add("gpsimd",
                            lambda m, dp_=dp_: m.dma_start(
                                out=yp_s[dp_][0], in_=yp_s[dp_][1], accum_op=ALU.add),
                            reads=[buf(f"yp{dp_}1"), buf(f"yp{dp_}0")],
                            writes=[buf(f"yp{dp_}0")], dma=True, tag="ycomb")
                    sch.add("gpsimd",
                            lambda m, dp_=dp_: m.dma_start(
                                out=yp_s[dp_][2], in_=yp_s[dp_][3], accum_op=ALU.add),
                            reads=[buf(f"yp{dp_}3"), buf(f"yp{dp_}2")],
                            writes=[buf(f"yp{dp_}2")], dma=True, tag="ycomb")
                    sch.add("gpsimd",
                            lambda m, dp_=dp_: m.dma_start(
                                out=yp_s[dp_][0], in_=yp_s[dp_][2], accum_op=ALU.add),
                            reads=[buf(f"yp{dp_}2"), buf(f"yp{dp_}0")],
                            writes=[buf(f"yp{dp_}0")], dma=True, tag="ycomb")

                    def emit_gate(b=b, dtile=dtile, dp_=dp_, c0=c0,
                                  bxc=bxc, bsz=bsz):
                        sch.add("vector",
                                lambda m, dtile=dtile, dp_=dp_:
                                m.scalar_tensor_tensor(
                                    out=yf_s[dp_], in0=xc_s[dp_],
                                    scalar=dp_sb[:, dtile:dtile + 1],
                                    in1=yp_s[dp_][0], op0=ALU.mult, op1=ALU.add),
                                reads=[bxc, buf(f"yp{dp_}0"), buf("c_dp2")],
                                writes=[buf(f"yf{dp_}")], tag="yfull")
                        sch.add("vector",
                                lambda m, dp_=dp_: m.tensor_mul(
                                    g_bf[dp_][:, 0:T1B], yf_s[dp_], sz_s[dp_]),
                                reads=[buf(f"yf{dp_}"), bsz], writes=[buf(f"gbf{dp_}")],
                                tag="gate")
                        dma("sync",
                            sc_gated[b, dtile * 128:(dtile + 1) * 128, c0:c0 + T1B],
                            g_bf[dp_][:, 0:T1B], reads=[buf(f"gbf{dp_}")], tag="stgated")

                    if pending_gate is not None:
                        pending_gate()
                    pending_gate = emit_gate

        if pending_gate is not None:
            pending_gate()
            pending_gate = None
        for b in range(B):
            for dtile in range(NDT):
                scol = srun_sb[:, b * NDT + dtile:b * NDT + dtile + 1]
                sch.add("scalar",
                        lambda m, s=scol, dtile=dtile, b=b: m.activation(
                            out=D_sb[:, (b * NDT + dtile) * NS:
                                     (b * NDT + dtile + 1) * NS],
                            in_=A_sb[:, dtile * NS:(dtile + 1) * NS], func=AF.Exp,
                            bias=0.0, scale=s),
                        reads=[buf("c_srun"), buf("c_a_log2")], writes=[buf("c_D")],
                        tag="Dexp")

        sch.barrier()

        ar.phase_reset()
        wo_bf = ar.alloc(NDT * DM // 2, BF16, "wo")
        Cbc2 = [ar.alloc(NS * T2A // 2, BF16, "Cbc2") for _ in range(2)]
        S_2 = [ar.alloc(T2A, name="S2a") for _ in range(2)]
        sz_2 = [ar.alloc(T2A, name="sz2a") for _ in range(2)]
        Pc_2 = [ar.alloc(T2A // 2, BF16, "Pc") for _ in range(4)]
        w_2 = [ar.alloc(T2A // 2, BF16, "w2a") for _ in range(2)]
        corr_2 = ar.alloc(T2A, name="corr")
        corrB_2 = ar.alloc(T2A, name="corrB")
        gat_bf = [ar.alloc(T2A // 2, BF16, "gat2a") for _ in range(2)]
        gs_bf = ar.alloc(NDT * T2A // 2, BF16, "gs")
        r1_f = ar.alloc(NKM * T2A, name="r1")
        xt_f = ar.alloc(NKM * T2A, name="xt2a")
        sq_f = ar.alloc(T2A, name="sq2a")
        xn1_f = ar.alloc(NKM * T2A, name="xn1")
        row_f = ar.alloc(3 * T2A, name="rows")

        for k in range(NDT):
            dma("gpsimd", wo_bf[:, k * DM:(k + 1) * DM],
                dram["w_outT"][k * 128:(k + 1) * 128, :], writes=[buf("w_wo")],
                tag="ldwo")
        cc_in, cc_out = dram["cc_in"], dram["cc_out"]
        dma("gpsimd", cc_in[:, 0:512], hst_sb, reads=[buf("c_hst")],
            writes=[buf("ccin")], tag="ccF")
        dma("gpsimd", cc_in[:, 512:1024], D_sb, reads=[buf("c_D")],
            writes=[buf("ccin")], tag="ccD")
        sch.add("gpsimd",
                lambda m: m.collective_compute(
                    "AllGather", ALU.bypass, replica_groups=[list(range(NCORES))],
                    ins=[cc_in.opt()], outs=[cc_out.opt()]),
                reads=[buf("ccin")], writes=[buf("ccout")], tag="allgather")
        sch.add("vector", lambda m: m.memset(h0_sb, 0.0), writes=[buf("c_h0")])
        for j in range(NCORES):
            bfd = buf("fdj")
            dma("sync", FD_j, cc_out[j * 128:(j + 1) * 128, :],
                reads=[buf("ccout")], writes=[bfd], tag="ldfd")
            mcol = rk_sb[:, j:j + 1]
            sch.add("vector",
                    lambda m, mc=mcol, j=j: m.scalar_tensor_tensor(
                        out=FD_j[:, 512:1024], in0=FD_j[:, 512:1024], scalar=mc,
                        in1=rkc_sb[:, j:j + 1].broadcast_to([128, 512]),
                        op0=ALU.mult, op1=ALU.add),
                    reads=[bfd, buf("c_rkmask"), buf("c_rkc")], writes=[bfd],
                    strict=True, tag="dm")
            sch.add("vector",
                    lambda m: m.tensor_mul(h0_sb, h0_sb, FD_j[:, 512:1024]),
                    reads=[bfd, buf("c_h0")], writes=[buf("c_h0")], strict=True, tag="h0d")
            sch.add("vector",
                    lambda m, mc=mcol: m.scalar_tensor_tensor(
                        out=h0_sb, in0=FD_j[:, 0:512], scalar=mc, in1=h0_sb,
                        op0=ALU.mult, op1=ALU.add),
                    reads=[bfd, buf("c_rkmask"), buf("c_h0")], writes=[buf("c_h0")],
                    strict=True, tag="h0f")

    NT2A = LC // T2A
    if stop_phase >= 3:
        def ldc2(b, blk):
            c0 = blk * T2A
            bp_ = blk % 2
            for n in range(NS):
                dma("sync", Cbc2[bp_][:, n * T2A:(n + 1) * T2A],
                    sc_BC[b, NS + n, c0:c0 + T2A]
                    .unsqueeze(0).partition_broadcast(128),
                    writes=[buf(f"cbc2{bp_}")], tag="ldc2")

        blocks2a = [(b, blk) for b in range(B) for blk in range(NT2A)]
        ldc2(*blocks2a[0])
        for bi, (b, blk) in enumerate(blocks2a):
                c0 = blk * T2A
                bp_ = blk % 2
                bcbc2 = buf(f"cbc2{bp_}")
                for dtile in range(NDT):
                    dp_ = dtile % 2
                    bS = buf(f"S2a{dp_}")
                    bsz = buf(f"sz2a{dp_}")
                    bgat = buf(f"gat2a{dp_}")
                    dma("scalar", S_2[dp_],
                        sc_S[b, dtile * 128:(dtile + 1) * 128, c0:c0 + T2A],
                        writes=[bS], tag="ldS")
                    dma("scalar", sz_2[dp_],
                        sc_sz[b, dtile * 128:(dtile + 1) * 128, c0:c0 + T2A],
                        writes=[bsz], tag="ldsz2")
                    dma("scalar", gat_bf[dp_][:, 0:T2A],
                        sc_gated[b, dtile * 128:(dtile + 1) * 128, c0:c0 + T2A],
                        writes=[bgat], tag="ldgat")
                    for n in range(NS):
                        np_ = n % 2
                        pp4 = n % 4
                        acol = A_sb[:, dtile * NS + n:dtile * NS + n + 1]
                        hcol = h0_sb[:, (b * NDT + dtile) * NS + n:
                                     (b * NDT + dtile) * NS + n + 1]
                        bPc = buf(f"Pc{pp4}")
                        bw = buf(f"w2a{np_}")
                        sch.add("scalar",
                                lambda m, a=acol, dp_=dp_, pp4=pp4: m.activation(
                                    out=Pc_2[pp4], in_=S_2[dp_], func=AF.Exp,
                                    bias=0.0, scale=a),
                                reads=[bS, buf("c_a_log2")], writes=[bPc],
                                tag="Pcexp")
                        tgt = (corr_2, corrB_2)[n % 2] if n < 2 else w_2[np_]
                        btg = (buf("corr"), buf("corrB"))[n % 2] if n < 2 else bw
                        sch.add("vector",
                                lambda m, h=hcol, n=n, o=tgt, pp4=pp4, bp_=bp_:
                                m.scalar_tensor_tensor(
                                    out=o, in0=Pc_2[pp4], scalar=h,
                                    in1=Cbc2[bp_][:, n * T2A:(n + 1) * T2A],
                                    op0=ALU.mult, op1=ALU.mult),
                                reads=[bPc, buf("c_h0"), bcbc2],
                                writes=[btg], tag="wstt")
                        if n >= 2:
                            acc, bacc = ((corr_2, buf("corr")) if n % 2 == 0
                                         else (corrB_2, buf("corrB")))
                            sch.add("vector",
                                    lambda m, np_=np_, acc=acc: m.tensor_add(
                                        acc, acc, w_2[np_]),
                                    reads=[bw, bacc], writes=[bacc],
                                    strict=True, tag="corradd")
                    sch.add("vector",
                            lambda m: m.tensor_add(corr_2, corr_2, corrB_2),
                            reads=[buf("corrB"), buf("corr")], writes=[buf("corr")],
                            strict=True, tag="corrfin")
                    sch.add("vector",
                            lambda m, dp_=dp_: m.tensor_mul(corr_2, corr_2, sz_2[dp_]),
                            reads=[buf("corr"), bsz], writes=[buf("corr")],
                            strict=True, tag="corrsz")
                    sch.add("vector",
                            lambda m, dtile=dtile, dp_=dp_: m.tensor_add(
                                gs_bf[:, dtile * T2A:(dtile + 1) * T2A],
                                gat_bf[dp_][:, 0:T2A], corr_2),
                            reads=[buf("corr"), bgat], writes=[buf(f"gs{dtile}")],
                            strict=True, tag="gs")
                if bi + 1 < len(blocks2a):
                    ldc2(*blocks2a[bi + 1])
                for k in range(NKM):
                    dma("sync", xt_f[:, k * T2A:(k + 1) * T2A],
                        xT[b, k * 128:(k + 1) * 128, 3 + c0:3 + c0 + T2A],
                        writes=[buf("xt2a")], tag="ldxt2")
                for k in range(NDT):
                    for mo in range(NKM):
                        mm(bank[mo], wo_bf[:, k * DM + mo * 128:k * DM + (mo + 1) * 128],
                           gs_bf[:, k * T2A:(k + 1) * T2A], k == 0, k == NDT - 1,
                           reads=[buf(f"gs{k}"), buf("w_wo")], writes=[buf(f"pb{mo}")],
                           tag="wout")
                for mo in range(NKM):
                    sch.add("vector",
                            lambda m, mo=mo: m.tensor_add(
                                r1_f[:, mo * T2A:(mo + 1) * T2A], bank[mo],
                                xt_f[:, mo * T2A:(mo + 1) * T2A]),
                            reads=[buf(f"pb{mo}"), buf("xt2a")], writes=[buf("r1")],
                            tag="r1")
                for k in range(NKM):
                    mm(bank[4][0:1, 0:T2A], onesc_sb,
                       r1_f[:, k * T2A:(k + 1) * T2A], k == 0, k == NKM - 1,
                       reads=[buf("r1"), buf("c_onesc")], writes=[buf("pb4")], tag="ln1m")
                for k in range(NKM):
                    sch.add("scalar",
                            lambda m, k=k: m.activation(
                                out=sq_f, in_=r1_f[:, k * T2A:(k + 1) * T2A],
                                func=AF.Square, bias=0.0, scale=1.0),
                            reads=[buf("r1")], writes=[buf("sq2a")], tag="ln1sq")
                    mm(bank[5][0:1, 0:T2A], onesc_sb, sq_f, k == 0, k == NKM - 1,
                       reads=[buf("sq2a"), buf("c_onesc")], writes=[buf("pb5")],
                       tag="ln1v")
                mrow = row_f[0:1, 0:T2A]
                vrow = row_f[0:1, T2A:2 * T2A]
                rrow = row_f[0:1, 2 * T2A:3 * T2A]
                sch.add("vector",
                        lambda m: m.tensor_scalar(out=mrow, in0=bank[4][0:1, 0:T2A],
                                                  scalar1=1.0 / DM, scalar2=None,
                                                  op0=ALU.mult),
                        reads=[buf("pb4")], writes=[buf("rows")], tag="mean")
                sch.add("vector",
                        lambda m: m.tensor_scalar(out=vrow, in0=bank[5][0:1, 0:T2A],
                                                  scalar1=1.0 / DM, scalar2=None,
                                                  op0=ALU.mult),
                        reads=[buf("pb5")], writes=[buf("rows")], strict=True, tag="var0")
                sch.add("vector",
                        lambda m: m.tensor_mul(rrow, mrow, mrow),
                        reads=[buf("rows")], writes=[buf("rows")], strict=True, tag="m2")
                sch.add("vector",
                        lambda m: m.tensor_sub(vrow, vrow, rrow),
                        reads=[buf("rows")], writes=[buf("rows")], strict=True, tag="var")
                sch.add("scalar",
                        lambda m: m.activation(out=rrow, in_=vrow, func=AF.Ln,
                                               bias=epsc_sb[0:1, :], scale=1.0),
                        reads=[buf("rows"), buf("c_eps")], writes=[buf("rows")], strict=True, tag="lnv")
                sch.add("scalar",
                        lambda m: m.activation(out=rrow, in_=rrow, func=AF.Exp,
                                               bias=0.0, scale=-0.5),
                        reads=[buf("rows")], writes=[buf("rows")], strict=True, tag="rstd")
                mm(bank[6][:, 0:T2A], onesr_sb[0:1, :], mrow, True, True,
                   reads=[buf("rows"), buf("c_onesr")], writes=[buf("pb6")], tag="mbc")
                mm(bank[7][:, 0:T2A], onesr_sb[0:1, :], rrow, True, True,
                   reads=[buf("rows"), buf("c_onesr")], writes=[buf("pb7")], tag="rbc")
                for k in range(NKM):
                    sch.add("vector",
                            lambda m, k=k: m.tensor_sub(
                                xn1_f[:, k * T2A:(k + 1) * T2A],
                                r1_f[:, k * T2A:(k + 1) * T2A], bank[6][:, 0:T2A]),
                            reads=[buf("r1"), buf("pb6")], writes=[buf("xn1")], tag="xm")
                    sch.add("vector",
                            lambda m, k=k: m.tensor_mul(
                                xn1_f[:, k * T2A:(k + 1) * T2A],
                                xn1_f[:, k * T2A:(k + 1) * T2A], bank[7][:, 0:T2A]),
                            reads=[buf("xn1"), buf("pb7")], writes=[buf("xn1")],
                            strict=True, tag="xr")
                    dma("sync", sc_xn1f[b, k * 128:(k + 1) * 128, c0:c0 + T2A],
                        xn1_f[:, k * T2A:(k + 1) * T2A], reads=[buf("xn1")], tag="stxn1f")
                    dma("gpsimd", sc_xn1b[b, k * 128:(k + 1) * 128, c0:c0 + T2A],
                        xn1_f[:, k * T2A:(k + 1) * T2A], reads=[buf("xn1")], tag="stxn1b")

        sch.barrier()

    ar.phase_reset()
    wfc_bf = [ar.alloc(NKM * IM // 2, BF16, "wfc") for _ in range(2)]
    wpj_bf = ar.alloc(NEXP * 4 * DM // 2, BF16, "wpj")
    wg_bf = ar.alloc(NKM * NEXP // 2 + 8, BF16, "wg")
    xn1b_s = [ar.alloc(NKM * T2B // 2, BF16, "xn1b2b") for _ in range(2)]
    xn1f_s = ar.alloc(T2B, name="xn1f2b")
    rt_f = [ar.alloc(T2B, name="rtf")] * 2
    rtT_f = ar.alloc(NEXP, name="rtT")
    gcol = ar.alloc(8, name="gcols")
    gmask = ar.alloc(4 * NEXP, name="gmask")
    gT_f = [ar.alloc(T2B, name="gT")] * 2
    gstage = ar.alloc(T2B + 8, name="gstage")
    gbc_f = [ar.alloc(NEXP * T2B // 2, BF16, "gbc") for _ in range(2)]
    hg_bf = ar.alloc(4 * NEXP * T2B // 2, BF16, "hg")
    r2_f = [ar.alloc(NKM * T2B, name="r2") for _ in range(2)]
    r2T_f = ar.alloc(DM, name="r2T")
    sq2_f = ar.alloc(DM, name="sq2b")
    st_c = ar.alloc(8, name="stats2b")
    out_f = ar.alloc(DM, name="out2b")

    out_d = dram["out"]
    NT2B = LC // T2B
    if stop_phase >= 4:
        for e in range(NEXP):
            for k in range(4):
                dma("gpsimd", wpj_bf[:, (e * 4 + k) * DM:(e * 4 + k + 1) * DM],
                    dram["w_projT"][e, k * 128:(k + 1) * 128, :], writes=[buf("w_pj")],
                    tag="ldwpj")
        for k in range(NKM):
            dma("gpsimd", wg_bf[:, k * NEXP:(k + 1) * NEXP],
                dram["w_gateT"][k * 128:(k + 1) * 128, :], writes=[buf("w_wg")],
                tag="ldwg")

        m1c = gcol[:, 0:1]
        nm1c = gcol[:, 1:2]
        m2c = gcol[:, 2:3]
        e2c = gcol[:, 3:4]
        w1c = gcol[:, 4:5]
        w2c = gcol[:, 5:6]
        scol2 = st_c[:, 0:1]
        vcol2 = st_c[:, 1:2]
        rcol2 = st_c[:, 2:3]
        mcol2 = st_c[:, 3:4]

        def emit_ln2(b, blk, qp_):
            c0 = blk * T2B
            for sub in range(T2B // 128):
                sc0 = sub * 128
                for k in range(NKM):
                    pbk = bank[k % 2]
                    bpb = buf(f"pb{k % 2}")
                    sch.add("tensor",
                            lambda m, k=k, sc0=sc0, p=pbk, qp_=qp_: m.transpose(
                                p[:, 0:128],
                                r2_f[qp_][:, k * T2B + sc0:k * T2B + sc0 + 128], id_sb),
                            reads=[buf(f"r2{qp_}"), buf("c_ident")], writes=[bpb],
                            tag="trans2")
                    sch.add("scalar",
                            lambda m, k=k, p=pbk: m.activation(
                                out=r2T_f[:, k * 128:(k + 1) * 128], in_=p[:, 0:128],
                                func=AF.Copy, bias=0.0, scale=1.0),
                            reads=[bpb], writes=[buf("r2T")], tag="trev")
                sch.add("vector",
                        lambda m: m.tensor_reduce(out=scol2, in_=r2T_f, axis=AX.X,
                                                  op=ALU.add),
                        reads=[buf("r2T")], writes=[buf("st2")], strict=True,
                        tag="sum2")
                sch.add("vector",
                        lambda m: m.tensor_scalar(out=mcol2, in0=scol2,
                                                  scalar1=1.0 / DM, scalar2=None,
                                                  op0=ALU.mult),
                        reads=[buf("st2")], writes=[buf("st2")], strict=True,
                        tag="mean2")
                sch.add("scalar",
                        lambda m: m.activation(out=sq2_f, in_=r2T_f, func=AF.Square,
                                               bias=0.0, scale=1.0),
                        reads=[buf("r2T")], writes=[buf("sq2b")], tag="sq2")
                sch.add("vector",
                        lambda m: m.tensor_reduce(out=vcol2, in_=sq2_f, axis=AX.X,
                                                  op=ALU.add),
                        reads=[buf("sq2b")], writes=[buf("st2")], strict=True,
                        tag="sumsq2")
                sch.add("vector",
                        lambda m: m.tensor_scalar(out=vcol2, in0=vcol2,
                                                  scalar1=1.0 / DM, scalar2=None,
                                                  op0=ALU.mult),
                        reads=[buf("st2")], writes=[buf("st2")], strict=True, tag="v2")
                sch.add("vector",
                        lambda m: m.tensor_mul(rcol2, mcol2, mcol2),
                        reads=[buf("st2")], writes=[buf("st2")], strict=True,
                        tag="m22")
                sch.add("vector",
                        lambda m: m.tensor_sub(vcol2, vcol2, rcol2),
                        reads=[buf("st2")], writes=[buf("st2")], strict=True,
                        tag="var2")
                sch.add("scalar",
                        lambda m: m.activation(out=rcol2, in_=vcol2, func=AF.Ln,
                                               bias=epsc_sb, scale=1.0),
                        reads=[buf("st2"), buf("c_eps")], writes=[buf("st2")],
                        strict=True, tag="lnv2")
                sch.add("scalar",
                        lambda m: m.activation(out=rcol2, in_=rcol2, func=AF.Exp,
                                               bias=0.0, scale=-0.5),
                        reads=[buf("st2")], writes=[buf("st2")], strict=True,
                        tag="rstd2")
                sch.add("vector",
                        lambda m: m.tensor_scalar(
                            out=out_f, in0=r2T_f, scalar1=mcol2, scalar2=rcol2,
                            op0=ALU.subtract, op1=ALU.mult),
                        reads=[buf("r2T"), buf("st2")], writes=[buf("outb")],
                        strict=True, tag="xn2")
                dma("sync", out_d[b, c0 + sc0:c0 + sc0 + 128, :], out_f,
                    reads=[buf("outb")], tag="stout")

        def ld_xn1b(b, blk):
            c0 = blk * T2B
            qp_ = blk % 2
            for k in range(NKM):
                dma("scalar", xn1b_s[qp_][:, k * T2B:(k + 1) * T2B],
                    sc_xn1b[b, k * 128:(k + 1) * 128, c0:c0 + T2B],
                    writes=[buf(f"xn1b2b{qp_}")], tag="ldxn1b")

        blocks2b = [(b, blk) for b in range(B) for blk in range(NT2B)]
        ld_xn1b(*blocks2b[0])
        pending_ln2 = None
        for bi2, (b, blk) in enumerate(blocks2b):
                c0 = blk * T2B
                qp_ = blk % 2
                xn1b = xn1b_s[qp_]
                bxn1b = buf(f"xn1b2b{qp_}")
                rtv = rt_f[qp_]
                brt = buf("rtf")
                gTv = gT_f[qp_]
                bgT = buf("gTf")
                gbcv = gbc_f[qp_]
                bgbc = buf(f"gbc{qp_}")
                for k in range(NKM):
                    mm(bank[0][0:NEXP, 0:T2B], wg_bf[:, k * NEXP:(k + 1) * NEXP],
                       xn1b[:, k * T2B:(k + 1) * T2B], k == 0, k == NKM - 1,
                       reads=[bxn1b, buf("w_wg")], writes=[buf("pb0")],
                       tag="router")
                sch.add("vector",
                        lambda m, rtv=rtv: m.tensor_copy(rtv[0:NEXP, 0:T2B],
                                                         bank[0][0:NEXP, 0:T2B]),
                        reads=[buf("pb0")], writes=[brt], tag="rtev")
                for sub in range(T2B // 128):
                    sc0 = sub * 128
                    mm(bank[1][:, 0:NEXP], rtv[0:NEXP, sc0:sc0 + 128],
                       id_sb[0:NEXP, 0:NEXP], True, True,
                       reads=[brt, buf("c_ident")], writes=[buf("pb1")], tag="rtT")
                    sch.add("vector",
                            lambda m: m.tensor_copy(rtT_f[:, 0:NEXP], bank[1][:, 0:NEXP]),
                            reads=[buf("pb1")], writes=[buf("rtT")], tag="rtTev")
                    sch.add("vector",
                            lambda m: m.tensor_reduce(out=m1c, in_=rtT_f[:, 0:NEXP],
                                                      axis=AX.X, op=ALU.max),
                            reads=[buf("rtT")], writes=[buf("gcols")], strict=True,
                            tag="m1")
                    sch.add("vector",
                            lambda m: m.tensor_single_scalar(
                                out=gmask[:, 0:NEXP], in_=rtT_f[:, 0:NEXP], scalar=m1c,
                                op=ALU.is_ge),
                            reads=[buf("rtT"), buf("gcols")], writes=[buf("gmask")],
                            strict=True, tag="ismax")
                    sch.add("vector",
                            lambda m: m.scalar_tensor_tensor(
                                out=gmask[:, NEXP:2 * NEXP], in0=gmask[:, 0:NEXP],
                                scalar=-1e30, in1=rtT_f[:, 0:NEXP],
                                op0=ALU.mult, op1=ALU.add),
                            reads=[buf("gmask"), buf("rtT")], writes=[buf("gmask")],
                            strict=True, tag="maskmax")
                    sch.add("vector",
                            lambda m: m.tensor_reduce(out=m2c,
                                                      in_=gmask[:, NEXP:2 * NEXP],
                                                      axis=AX.X, op=ALU.max),
                            reads=[buf("gmask")], writes=[buf("gcols")], strict=True,
                            tag="m2")
                    sch.add("vector",
                            lambda m: m.tensor_scalar(out=nm1c, in0=m1c, scalar1=-1.0,
                                                      scalar2=None, op0=ALU.mult),
                            reads=[buf("gcols")], writes=[buf("gcols")], strict=True,
                            tag="nm1")
                    sch.add("scalar",
                            lambda m: m.activation(out=e2c, in_=m2c, func=AF.Exp,
                                                   bias=nm1c, scale=1.0),
                            reads=[buf("gcols")], writes=[buf("gcols")], strict=True,
                            tag="e2")
                    sch.add("vector",
                            lambda m: m.tensor_scalar(out=w1c, in0=e2c, scalar1=1.0,
                                                      scalar2=None, op0=ALU.add),
                            reads=[buf("gcols")], writes=[buf("gcols")], strict=True,
                            tag="den")
                    sch.add("vector",
                            lambda m: m.reciprocal(out=w1c, in_=w1c),
                            reads=[buf("gcols")], writes=[buf("gcols")], strict=True,
                            tag="w1")
                    sch.add("vector",
                            lambda m: m.tensor_mul(w2c, e2c, w1c),
                            reads=[buf("gcols")], writes=[buf("gcols")], strict=True,
                            tag="w2")
                    sch.add("vector",
                            lambda m: m.tensor_single_scalar(
                                out=gmask[:, 2 * NEXP:3 * NEXP],
                                in_=gmask[:, NEXP:2 * NEXP], scalar=m2c, op=ALU.is_ge),
                            reads=[buf("gmask"), buf("gcols")], writes=[buf("gmask")],
                            strict=True, tag="ismax2")
                    sch.add("vector",
                            lambda m: m.tensor_single_scalar(
                                out=gmask[:, 0:NEXP], in_=gmask[:, 0:NEXP], scalar=w1c,
                                op=ALU.mult),
                            reads=[buf("gmask"), buf("gcols")], writes=[buf("gmask")],
                            strict=True, tag="gw1")
                    sch.add("vector",
                            lambda m: m.scalar_tensor_tensor(
                                out=gmask[:, 3 * NEXP:4 * NEXP],
                                in0=gmask[:, 2 * NEXP:3 * NEXP], scalar=w2c,
                                in1=gmask[:, 0:NEXP], op0=ALU.mult, op1=ALU.add),
                            reads=[buf("gmask"), buf("gcols")], writes=[buf("gmask")],
                            strict=True, tag="gates")
                    mm(bank[2][0:NEXP, 0:128], gmask[:, 3 * NEXP:4 * NEXP],
                       id_sb[:, 0:128], True, True,
                       reads=[buf("gmask"), buf("c_ident")], writes=[buf("pb2")],
                       tag="gTmm")
                    sch.add("vector",
                            lambda m, sc0=sc0, gTv=gTv: m.tensor_copy(
                                gTv[0:NEXP, sc0:sc0 + 128], bank[2][0:NEXP, 0:128]),
                            reads=[buf("pb2")], writes=[bgT], tag="gTev")
                for e in range(NEXP):
                    stg = gstage[0:1, 0:T2B]
                    bst = buf("gst")
                    dma("sync", stg, gTv[e:e + 1, 0:T2B], reads=[bgT],
                        writes=[bst], tag="gstage")
                    mm(bank[3][:, 0:T2B], onesr_sb[0:1, :], stg, True, True,
                       reads=[bst, buf("c_onesr")], writes=[buf("pb3")], tag="gbcmm")
                    sch.add("scalar",
                            lambda m, e=e, gbcv=gbcv: m.activation(
                                out=gbcv[:, e * T2B:(e + 1) * T2B], in_=bank[3][:, 0:T2B],
                                func=AF.Copy, bias=0.0, scale=1.0),
                            reads=[buf("pb3")], writes=[bgbc], tag="gbcev")
                for e in range(NEXP):
                    ep_ = e % 2
                    bwfc = buf(f"w_fc{ep_}")
                    dma("gpsimd",
                        wfc_bf[ep_][:, 0:NKM * IM].rearrange("p (k i) -> p k i", i=IM),
                        dram["w_fcT"][e].rearrange("(k p) i -> p k i", p=128),
                        writes=[bwfc], tag="ldwfc")
                    for mi in range(4):
                        _fb = (2, 4, 5)[(e * 4 + mi) % 3]
                        pbk = bank[_fb]
                        bpb = buf(f"pb{_fb}")
                        for k in range(NKM):
                            mm(pbk[:, 0:T2B],
                               wfc_bf[ep_][:, k * IM + mi * 128:k * IM + (mi + 1) * 128],
                               xn1b[:, k * T2B:(k + 1) * T2B], k == 0, k == NKM - 1,
                               reads=[bxn1b, bwfc], writes=[bpb], tag="fc")
                        hslot = e * 4 + mi
                        sch.add("scalar",
                                lambda m, p=pbk, hs=hslot: m.activation(
                                    out=hg_bf[:, hs * T2B:(hs + 1) * T2B],
                                    in_=p[:, 0:T2B], func=AF.Gelu, bias=0.0, scale=1.0),
                                reads=[bpb], writes=[buf("hg")], tag="gelu")
                        sch.add("vector",
                                lambda m, hs=hslot, e=e, gbcv=gbcv: m.tensor_mul(
                                    hg_bf[:, hs * T2B:(hs + 1) * T2B],
                                    hg_bf[:, hs * T2B:(hs + 1) * T2B],
                                    gbcv[:, e * T2B:(e + 1) * T2B]),
                                reads=[buf("hg"), bgbc], writes=[buf("hg")],
                                strict=True, tag="hgmul")
                if bi2 + 1 < len(blocks2b):
                    ld_xn1b(*blocks2b[bi2 + 1])
                for mo in range(NKM):
                    pbk = bank[6 + mo % 2]
                    bpb = buf(f"pb{6 + mo % 2}")
                    first = True
                    for e in range(NEXP):
                        for ki in range(4):
                            mm(pbk[:, 0:T2B],
                               wpj_bf[:, (e * 4 + ki) * DM + mo * 128:
                                      (e * 4 + ki) * DM + (mo + 1) * 128],
                               hg_bf[:, (e * 4 + ki) * T2B:(e * 4 + ki + 1) * T2B],
                               first, (e == NEXP - 1 and ki == 3),
                               reads=[buf("hg"), buf("w_pj")], writes=[bpb], tag="proj")
                            first = False
                    dma("scalar", xn1f_s,
                        sc_xn1f[b, mo * 128:(mo + 1) * 128, c0:c0 + T2B],
                        writes=[buf("xn1f2b")], tag="ldxn1f")
                    sch.add("vector",
                            lambda m, mo=mo, p=pbk, qp_=qp_: m.tensor_add(
                                r2_f[qp_][:, mo * T2B:(mo + 1) * T2B], p[:, 0:T2B],
                                xn1f_s),
                            reads=[bpb, buf("xn1f2b")], writes=[buf(f"r2{qp_}")],
                            tag="r2")
                if pending_ln2 is not None:
                    pending_ln2()
                pending_ln2 = (lambda b=b, blk=blk, qp_=qp_: emit_ln2(b, blk, qp_))
        if pending_ln2 is not None:
            pending_ln2()

        sch.barrier()

    sch.finalize(nc, stack)
    block = stack.enter_context(nc.Block())

    @block.sync
    def _(m):
        sch.emit_engine(m, "sync")

    @block.tensor
    def _(m):
        sch.emit_engine(m, "tensor")

    @block.vector
    def _(m):
        sch.emit_engine(m, "vector")

    @block.scalar
    def _(m):
        sch.emit_engine(m, "scalar")

    @block.gpsimd
    def _(m):
        sch.emit_engine(m, "gpsimd")


# version 28 (final)
# speedup vs baseline: 1.0359x; 1.0037x over previous
import contextlib
import numpy as np
import concourse.bass as bass
import concourse.mybir as mybir
from concourse.bass_utils import run_bass_kernel_spmd

F32 = mybir.dt.float32
BF16 = mybir.dt.bfloat16
AF = mybir.ActivationFunctionType
ALU = mybir.AluOpType
AX = mybir.AxisListType

NCORES = 8
B = 2
L = 16384
LC = L // NCORES
DM = 1024
DI = 2048
E4 = 2 * DI
NS = 16
RK = 64
NEXP = 8
IM = 512
NDT = DI // 128
NKM = DM // 128

T1A = 512
T1B = 1024
T2A = 512
T2B = 512
EPS = 1e-5

ENGINES = ("sync", "tensor", "vector", "scalar", "gpsimd")


class _Op:
    __slots__ = ("eng", "emit", "deps", "inc", "count", "used", "strict", "tag",
                 "semkey")

    def __init__(self, eng, emit, deps, inc, strict, tag):
        self.eng = eng
        self.emit = emit
        self.deps = deps
        self.inc = inc
        self.count = 0
        self.used = False
        self.strict = strict
        self.tag = tag
        self.semkey = eng


class Buf:
    __slots__ = ("name", "writer", "readers")

    def __init__(self, name=""):
        self.name = name
        self.writer = None
        self.readers = []


NDMALANES = 8


class Sched:
    def __init__(self):
        self.q = {e: [] for e in ENGINES}
        self.all = []
        self._dma_idx = {e: 0 for e in ENGINES}

    def add(self, eng, emit, reads=(), writes=(), deps=(), dma=False,
            strict=False, tag=""):
        d = []
        for b in reads:
            if b.writer is not None:
                d.append(b.writer)
        for b in writes:
            if b.writer is not None:
                d.append(b.writer)
            d.extend(b.readers)
        d.extend(x for x in deps if x is not None)
        op = _Op(eng, emit, d, 16 if dma else 1, strict, tag)
        if dma:
            lane = self._dma_idx[eng] % NDMALANES
            self._dma_idx[eng] += 1
            op.semkey = (eng, lane)
        for b in reads:
            b.readers.append(op)
        for b in writes:
            b.writer = op
            b.readers = []
        self.q[eng].append(op)
        self.all.append(op)
        return op

    def barrier(self):
        last = {}
        for e in ENGINES:
            for op in reversed(self.q[e]):
                if op.emit is None:
                    continue
                if op.semkey not in last:
                    last[op.semkey] = op
                if op.semkey == e:
                    break
        for e in ENGINES:
            deps = [op for op in last.values() if op.eng != e]
            deps += [op for op in last.values()
                     if op.eng == e and op.semkey != e]
            self.add(e, None, deps=deps, strict=True, tag="barrier")

    def finalize(self, nc, stack):
        pos = {}
        for e in ENGINES:
            for i, op in enumerate(self.q[e]):
                pos[id(op)] = i
        for op in self.all:
            if op.inc == 16 and op.emit is not None:
                op.used = True
            wd = []
            for dp in op.deps:
                if dp.eng != op.eng or dp.semkey != dp.eng:
                    dp.used = True
                    wd.append(dp)
                elif op.strict or pos[id(dp)] >= pos[id(op)] - 2:
                    if dp.emit is not None:
                        dp.used = True
                        wd.append(dp)
            op.deps = wd
        self.sems = {}
        counts = {}
        for e in ENGINES:
            for op in self.q[e]:
                if op.used:
                    assert op.emit is not None, f"dep on no-op ({op.tag})"
                    counts[op.semkey] = counts.get(op.semkey, 0) + op.inc
                op.count = counts.get(op.semkey, 0)
        for k, c in counts.items():
            assert c < 60000, f"semaphore overflow on {k}: {c}"
        for e in ENGINES:
            self.sems[e] = stack.enter_context(nc.semaphore(f"s_{e}"))
            for ln in range(NDMALANES):
                self.sems[(e, ln)] = stack.enter_context(
                    nc.semaphore(f"s_{e}_d{ln}"))

    def emit_engine(self, engmod, e):
        sems = self.sems
        waited = {}
        for op in self.q[e]:
            for dp in op.deps:
                if dp.count > waited.get(dp.semkey, 0):
                    engmod.wait_ge(sems[dp.semkey], dp.count)
                    waited[dp.semkey] = dp.count
            if op.emit is not None:
                inst = op.emit(engmod)
                if op.used:
                    inst.then_inc(sems[op.semkey], op.inc)

    def stats(self):
        return {e: (len(self.q[e]), self.q[e][-1].count if self.q[e] else 0)
                for e in ENGINES}


def _split_shards(x):
    xt = np.ascontiguousarray(np.swapaxes(np.asarray(x, np.float32), 1, 2))
    shards = []
    for i in range(NCORES):
        t0 = i * LC
        sl = np.zeros((B, DM, 3 + LC), np.float32)
        sl[:, :, 3:] = xt[:, :, t0:t0 + LC]
        if t0 >= 3:
            sl[:, :, 0:3] = xt[:, :, t0 - 3:t0]
        shards.append(np.ascontiguousarray(sl))
    return shards


def _vec2d(v):
    return np.ascontiguousarray(np.asarray(v, np.float32).reshape(NDT, 128).T)


def prepare_in_maps(L0_attn_x, L0_attn_W_in, L0_attn_conv_w, L0_attn_conv_b,
                    L0_attn_W_x, L0_attn_W_dt, L0_attn_b_dt, L0_attn_A_log,
                    L0_attn_Dp, L0_attn_W_out, L0_ffn_w_gate, L0_ffn_w_fc,
                    L0_ffn_w_proj):
    f32 = np.float32
    xsh = _split_shards(L0_attn_x)

    w_inT = np.ascontiguousarray(np.asarray(L0_attn_W_in, f32).T)
    w_xT = np.ascontiguousarray(np.asarray(L0_attn_W_x, f32).T)
    w_dtT = np.ascontiguousarray(np.asarray(L0_attn_W_dt, f32).T)
    w_outT = np.ascontiguousarray(np.asarray(L0_attn_W_out, f32).T)
    w_gateT = np.ascontiguousarray(np.asarray(L0_ffn_w_gate, f32).T)
    w_fcT = np.ascontiguousarray(np.swapaxes(np.asarray(L0_ffn_w_fc, f32), 1, 2))
    w_projT = np.ascontiguousarray(np.swapaxes(np.asarray(L0_ffn_w_proj, f32), 1, 2))

    cw2 = np.zeros((128, NDT * 4), f32)
    cwa = np.asarray(L0_attn_conv_w, f32)
    for dt in range(NDT):
        cw2[:, dt * 4:(dt + 1) * 4] = cwa[dt * 128:(dt + 1) * 128, :]
    a2 = np.zeros((128, NDT * NS), f32)
    al = np.asarray(L0_attn_A_log, f32)
    for dt in range(NDT):
        a2[:, dt * NS:(dt + 1) * NS] = al[dt * 128:(dt + 1) * 128, :]

    cb2v = _vec2d(L0_attn_conv_b)
    bdt2v = _vec2d(L0_attn_b_dt)
    dp2v = _vec2d(L0_attn_Dp)
    in_maps = []
    for i in range(NCORES):
        mask = np.zeros((128, 8), f32)
        mask[:, :i] = 1.0
        in_maps.append({
            "xT": xsh[i], "w_inT": w_inT, "w_xT": w_xT, "w_dtT": w_dtT,
            "w_outT": w_outT, "w_gateT": w_gateT, "w_fcT": w_fcT,
            "w_projT": w_projT, "conv_w2": cw2,
            "conv_b2": cb2v, "b_dt2": bdt2v,
            "dp2": dp2v, "a_log2": a2,
            "ident": np.eye(128, dtype=f32), "rkmask": mask,
        })
    return in_maps


def kernel(**inputs):
    in_maps = prepare_in_maps(**inputs)
    nc = build_program()
    res = run_bass_kernel_spmd(nc, in_maps, core_ids=list(range(NCORES)))
    out = np.empty((B, L, DM), np.float32)
    for i in range(NCORES):
        out[:, i * LC:(i + 1) * LC, :] = res.results[i]["out"]
    return out


class Arena:
    def __init__(self, nc, total_f32=53100):
        self.t = nc.alloc_sbuf_tensor("arena", [128, total_f32], F32)
        self.total = total_f32
        self.perm_off = 0
        self.off = 0

    def perm(self, n_f32, dtype=F32, name=""):
        ap = self._view(self.perm_off, n_f32, dtype)
        self.perm_off += n_f32
        self.off = max(self.off, self.perm_off)
        return ap

    def phase_reset(self):
        self.off = self.perm_off

    def alloc(self, n_f32, dtype=F32, name=""):
        ap = self._view(self.off, n_f32, dtype)
        self.off += n_f32
        assert self.off <= self.total, f"arena overflow at {name}: {self.off}"
        return ap

    def _view(self, off, n_f32, dtype):
        ap = self.t.ap()[:, off:off + n_f32]
        if dtype == BF16:
            ap = ap.bitcast(BF16)
        return ap


def build_program(stop_phase=99, dbg_out=False):
    nc = bass.Bass("TRN2", target_bir_lowering=False)
    dram = {}

    def din(name, shape, dtype=F32):
        dram[name] = nc.dram_tensor(name, list(shape), dtype, kind="ExternalInput").ap()

    def dint(name, shape, dtype=F32):
        kind = "ExternalOutput" if dbg_out else "Internal"
        dram[name] = nc.dram_tensor(name, list(shape), dtype, kind=kind).ap()

    din("xT", (B, DM, 3 + LC))
    din("w_inT", (DM, E4))
    din("w_xT", (DI, 96))
    din("w_dtT", (RK, DI))
    din("w_outT", (DI, DM))
    din("w_gateT", (DM, NEXP))
    din("w_fcT", (NEXP, DM, IM))
    din("w_projT", (NEXP, IM, DM))
    din("conv_w2", (128, NDT * 4))
    din("conv_b2", (128, NDT))
    din("b_dt2", (128, NDT))
    din("dp2", (128, NDT))
    din("a_log2", (128, NDT * NS))
    din("ident", (128, 128))
    din("rkmask", (128, NCORES))

    dram["out"] = nc.dram_tensor("out", [B, LC, DM], F32, kind="ExternalOutput").ap()

    dint("sc_xconv", (B, DI, LC))
    dint("sc_sz", (B, DI, LC))
    dint("sc_dt", (B, DI, LC))
    dint("sc_S", (B, DI, LC))
    dint("sc_BC", (B, 32, LC), BF16)
    dint("sc_gated", (B, DI, LC), BF16)
    dint("sc_xn1f", (B, DM, LC))
    dint("sc_xn1b", (B, DM, LC), BF16)
    if dbg_out:
        dram["dbg_bbc"] = nc.dram_tensor("dbg_bbc", [B, 2, 2, 128, NS * T1B], F32, kind="ExternalOutput").ap()
        dram["dbg_u"] = nc.dram_tensor("dbg_u", [B, DI, LC], F32, kind="ExternalOutput").ap()
        dram["dbg_y"] = nc.dram_tensor("dbg_y", [B, DI, LC], F32, kind="ExternalOutput").ap()
        dram["dbg_dA"] = nc.dram_tensor("dbg_dA", [B, DI, LC], F32, kind="ExternalOutput").ap()
    dram["cc_in"] = nc.dram_tensor("cc_in", [128, 1024], F32, kind="Internal").ap()
    dram["cc_out"] = nc.dram_tensor("cc_out", [128 * NCORES, 1024], F32, kind="Internal").ap()

    sch = Sched()
    with contextlib.ExitStack() as stack:
        build_body(nc, stack, sch, dram, stop_phase)
    return nc


def build_body(nc, stack, sch, dram, stop_phase):
    DBG = "dbg_u" in dram
    ar = Arena(nc)
    banks = [stack.enter_context(nc.psum_tensor(f"pb{i}", [128, 512], F32))
             for i in range(8)]
    bank = [bk.ap() for bk in banks]

    A_sb = ar.perm(NDT * NS, name="A")
    cw_sb = ar.perm(NDT * 4, name="cw")
    cb_sb = ar.perm(NDT, name="cb")
    bdt_sb = ar.perm(NDT, name="bdt")
    dp_sb = ar.perm(NDT, name="dp")
    rk_sb = ar.perm(NCORES, name="rk")
    rkc_sb = ar.perm(NCORES, name="rkc")
    id_sb = ar.perm(128, name="ident")
    onesr_sb = ar.perm(128, name="onesrow")
    onesc_sb = ar.perm(1, name="onescol")
    epsc_sb = ar.perm(1, name="epscol")
    zeros_sb = ar.perm(T1A, name="zeros")
    srun_sb = ar.perm(B * NDT, name="srun")
    hst_sb = ar.perm(B * NDT * NS, name="hst")
    D_sb = ar.perm(B * NDT * NS, name="D")
    h0_sb = ar.perm(B * NDT * NS, name="h0")


    bufs = {}

    def buf(name):
        if name not in bufs:
            bufs[name] = Buf(name)
        return bufs[name]

    def dma(eng, dst, src, *, reads=(), writes=(), deps=(), tag=""):
        return sch.add(eng, lambda m, d=dst, s=src: m.dma_start(out=d, in_=s),
                       reads=reads, writes=writes, deps=deps, dma=True, tag=tag)

    def mm(pb, lhsT, rhs, start, stop, reads, writes, tag=""):
        return sch.add("tensor",
                       lambda m, o=pb, a=lhsT, r=rhs, s1=start, s2=stop:
                       m.matmul(o, a, r, start=s1, stop=s2),
                       reads=reads, writes=writes, tag=tag)

    xT = dram["xT"]
    sc_xconv, sc_sz, sc_dt, sc_S, sc_BC = (dram["sc_xconv"], dram["sc_sz"],
                                           dram["sc_dt"], dram["sc_S"],
                                           dram["sc_BC"])
    sc_gated, sc_xn1f, sc_xn1b = (dram["sc_gated"], dram["sc_xn1f"],
                                  dram["sc_xn1b"])

    ar.phase_reset()
    wi_bf = [ar.alloc(E4 // 2, BF16, "wi") for _ in range(NKM)]
    wx_bf = ar.alloc(NDT * 96 // 2 + 48, BF16, "wx")
    wdt_bf = ar.alloc(DI // 2, BF16, "wdt")

    for name, dst in (("conv_w2", cw_sb), ("conv_b2", cb_sb),
                      ("b_dt2", bdt_sb), ("dp2", dp_sb), ("rkmask", rk_sb),
                      ("ident", id_sb), ("a_log2", A_sb)):
        dma("sync", dst, dram[name], writes=[buf("c_" + name)], tag="ldc")

    sch.add("scalar",
            lambda m: m.activation(out=A_sb, in_=A_sb, func=AF.Exp, bias=0.0, scale=1.0),
            reads=[buf("c_a_log2")], writes=[buf("c_a_log2")], tag="Aexp")
    sch.add("vector",
            lambda m: m.tensor_scalar(out=A_sb, in0=A_sb, scalar1=-1.0,
                                      scalar2=None, op0=ALU.mult),
            reads=[buf("c_a_log2")], writes=[buf("c_a_log2")], strict=True, tag="Aneg")
    sch.add("vector",
            lambda m: m.tensor_scalar(out=rkc_sb, in0=rk_sb, scalar1=-1.0,
                                      scalar2=1.0, op0=ALU.mult, op1=ALU.add),
            reads=[buf("c_rkmask")], writes=[buf("c_rkc")], tag="rkc")
    sch.add("vector", lambda m: m.memset(zeros_sb, 0.0), writes=[buf("c_zeros")])
    sch.add("vector", lambda m: m.memset(srun_sb, 0.0), writes=[buf("c_srun")])
    sch.add("vector", lambda m: m.memset(hst_sb, 0.0), writes=[buf("c_hst")])
    sch.add("vector", lambda m: m.memset(onesr_sb[0:1, :], 1.0), writes=[buf("c_onesr")])
    sch.add("vector", lambda m: m.memset(onesc_sb, 1.0), writes=[buf("c_onesc")])
    sch.add("vector", lambda m: m.memset(epsc_sb, EPS), writes=[buf("c_eps")])

    for k in range(NKM):
        dma("gpsimd", wi_bf[k], dram["w_inT"][k * 128:(k + 1) * 128, :],
            writes=[buf(f"w_wi{k}")], tag="ldwi")
    for k in range(NDT):
        dma("gpsimd", wx_bf[:, k * 96:(k + 1) * 96],
            dram["w_xT"][k * 128:(k + 1) * 128, :], writes=[buf("w_wx")], tag="ldwx")
    dma("gpsimd", wdt_bf[0:RK, 0:DI], dram["w_dtT"], writes=[buf("w_wdt")], tag="ldwdt")

    xb_bf = [ar.alloc(NKM * (T1A + 3) // 2 + 64, BF16, "xb") for _ in range(2)]
    xcf = [ar.alloc(T1A + 3, name="xcf") for _ in range(2)]
    halbuf = ar.alloc(NDT * 3, name="hal")
    accb_t = [ar.alloc(T1A, name="acc") for _ in range(NDT)]
    xconv_t = [ar.alloc(T1A, name="xconv") for _ in range(2)]
    sz_t = [ar.alloc(T1A, name="sz") for _ in range(2)]
    xcv_bf = ar.alloc(NDT * T1A // 2, BF16, "xcv")
    dtl_bf = ar.alloc(T1A // 2, BF16, "dtl")
    bc_f = [ar.alloc(T1A, name="bc") for _ in range(2)]
    e1b_t = [ar.alloc(T1A, name="e1") for _ in range(2)]
    e2b_t = [ar.alloc(T1A, name="e2") for _ in range(2)]
    rlb_t = [ar.alloc(T1A, name="rl") for _ in range(2)]
    dt_t = [ar.alloc(T1A, name="dt") for _ in range(2)]
    S_t = [ar.alloc(T1A, name="S") for _ in range(2)]

    NT1A = LC // T1A

    if stop_phase >= 1:
        for b in range(B):
            for tt in range(NT1A):
                c0 = tt * T1A
                tp_ = tt % 2
                xb = xb_bf[tp_]
                bxb = buf(f"xb{tp_}")
                for k in range(NKM):
                    dma("gpsimd", xb[:, k * (T1A + 3):(k + 1) * (T1A + 3)],
                        xT[b, k * 128:(k + 1) * 128, c0:c0 + T1A + 3],
                        writes=[bxb], tag="ldx")
                if tt == 0:
                    for m_ in range(NDT):
                        for k in range(NKM):
                            mm(bank[7][:, 0:3], wi_bf[k][:, m_ * 128:(m_ + 1) * 128],
                               xb[:, k * (T1A + 3):k * (T1A + 3) + 3],
                               k == 0, k == NKM - 1,
                               reads=[bxb, buf(f"w_wi{k}")], writes=[buf("pb7")], tag="halo")
                        sch.add("vector",
                                lambda m, m_=m_: m.tensor_copy(
                                    halbuf[:, m_ * 3:(m_ + 1) * 3], bank[7][:, 0:3]),
                                reads=[buf("pb7")], writes=[buf("hal")], tag="haloev")

                for m_ in range(NDT):
                    pbk = bank[m_ % 4]
                    bpb = buf(f"pb{m_ % 4}")
                    for k in range(NKM):
                        mm(pbk, wi_bf[k][:, m_ * 128:(m_ + 1) * 128],
                           xb[:, k * (T1A + 3) + 3:(k + 1) * (T1A + 3)],
                           k == 0, k == NKM - 1,
                           reads=[bxb, buf(f"w_wi{k}")], writes=[bpb], tag="xz")
                    xcfull = xcf[m_ % 2]
                    bxc = buf(f"xcf{m_ % 2}")
                    sch.add("vector",
                            lambda m, x=xcfull, m_=m_: m.tensor_copy(
                                x[:, 0:3], halbuf[:, m_ * 3:(m_ + 1) * 3]),
                            reads=[buf("hal")], writes=[bxc], strict=True, tag="halcp")
                    sch.add("scalar",
                            lambda m, x=xcfull, p=pbk: m.activation(
                                out=x[:, 3:3 + T1A], in_=p, func=AF.Copy,
                                bias=0.0, scale=1.0),
                            reads=[bpb], writes=[bxc], strict=True, tag="xcev")
                    sch.add("vector",
                            lambda m, x=xcfull, m_=m_: m.tensor_copy(
                                halbuf[:, m_ * 3:(m_ + 1) * 3], x[:, T1A:T1A + 3]),
                            reads=[bxc], writes=[buf("hal")], strict=True, tag="halupd")
                    accb = accb_t[m_]
                    bacc = buf(f"acc{m_}")
                    sch.add("scalar",
                            lambda m, x=xcfull, m_=m_, accb=accb: m.activation(
                                out=accb, in_=x[:, 0:T1A], func=AF.Copy,
                                bias=0.0, scale=cw_sb[:, m_ * 4:m_ * 4 + 1]),
                            reads=[bxc, buf("c_conv_w2")], writes=[bacc], tag="conv0")
                    for j in (1, 2, 3):
                        sch.add("vector",
                                lambda m, x=xcfull, m_=m_, j=j, accb=accb:
                                m.scalar_tensor_tensor(
                                    out=accb, in0=x[:, j:j + T1A],
                                    scalar=cw_sb[:, m_ * 4 + j:m_ * 4 + j + 1],
                                    in1=accb, op0=ALU.mult, op1=ALU.add),
                                reads=[bxc, buf("c_conv_w2"), bacc],
                                writes=[bacc], strict=True, tag=f"conv{j}")
                for m_ in range(NDT):
                    accb = accb_t[m_]
                    bacc = buf(f"acc{m_}")
                    xcv = xconv_t[m_ % 2]
                    bxcv = buf(f"xconv{m_ % 2}")
                    sch.add("scalar",
                            lambda m, o=xcv, m_=m_, accb=accb: m.activation(
                                out=o, in_=accb, func=AF.Silu,
                                bias=cb_sb[:, m_:m_ + 1], scale=1.0),
                            reads=[bacc, buf("c_conv_b2")], writes=[bxcv], tag="silu")
                    sch.add("vector",
                            lambda m, o=xcv_bf[:, m_ * T1A:(m_ + 1) * T1A], i=xcv:
                            m.tensor_copy(o, i),
                            reads=[bxcv], writes=[buf("xcvbf")], tag="xcvcast")
                    dma("gpsimd", sc_xconv[b, m_ * 128:(m_ + 1) * 128, c0:c0 + T1A], xcv,
                        reads=[bxcv], tag="stxconv")

                for m_ in range(NDT):
                    pbk = bank[4 + m_ % 3]
                    bpb = buf(f"pb{4 + m_ % 3}")
                    for k in range(NKM):
                        mm(pbk, wi_bf[k][:, (NDT + m_) * 128:(NDT + m_ + 1) * 128],
                           xb[:, k * (T1A + 3) + 3:(k + 1) * (T1A + 3)],
                           k == 0, k == NKM - 1,
                           reads=[bxb, buf(f"w_wi{k}")], writes=[bpb], tag="z")
                    szv = sz_t[m_ % 2]
                    bsz = buf(f"sz{m_ % 2}")
                    sch.add("scalar",
                            lambda m, o=szv, p=pbk: m.activation(
                                out=o, in_=p, func=AF.Silu, bias=0.0, scale=1.0),
                            reads=[bpb], writes=[bsz], tag="siluz")
                    dma("gpsimd", sc_sz[b, m_ * 128:(m_ + 1) * 128, c0:c0 + T1A], szv,
                        reads=[bsz], tag="stsz")

                for k in range(NDT):
                    mm(bank[4][0:96, :], wx_bf[:, k * 96:(k + 1) * 96],
                       xcv_bf[:, k * T1A:(k + 1) * T1A],
                       k == 0, k == NDT - 1,
                       reads=[buf("xcvbf"), buf("w_wx")], writes=[buf("pb4")], tag="xdbl")
                sch.add("scalar",
                        lambda m: m.activation(out=dtl_bf[0:RK, 0:T1A],
                                               in_=bank[4][0:RK, :],
                                               func=AF.Copy, bias=0.0, scale=1.0),
                        reads=[buf("pb4")], writes=[buf("dtl")], tag="dtlcast")
                bcv = bc_f[tt % 2]
                bbc = buf(f"bc{tt % 2}")
                sch.add("vector",
                        lambda m, o=bcv: m.tensor_copy(o[0:32, :], bank[4][64:96, :]),
                        reads=[buf("pb4")], writes=[bbc], tag="bcev")
                dma("gpsimd", sc_BC[b, :, c0:c0 + T1A], bcv[0:32, :], reads=[bbc], tag="stbc")

                for m_ in range(NDT):
                    pbk = bank[m_ % 2]
                    bpb = buf(f"pb{m_ % 2}")
                    mm(pbk, wdt_bf[0:RK, m_ * 128:(m_ + 1) * 128], dtl_bf[0:RK, 0:T1A],
                       True, True, reads=[buf("dtl"), buf("w_wdt")], writes=[bpb], tag="dtp")
                    e1b = e1b_t[m_ % 2]
                    e2b = e2b_t[m_ % 2]
                    rlb = rlb_t[m_ % 2]
                    be1 = buf(f"e1{m_ % 2}")
                    be2 = buf(f"e2{m_ % 2}")
                    brl = buf(f"rl{m_ % 2}")
                    sch.add("scalar",
                            lambda m, p=pbk, m_=m_, e1b=e1b: m.activation(
                                out=e1b, in_=p, func=AF.Abs,
                                bias=bdt_sb[:, m_:m_ + 1], scale=1.0),
                            reads=[bpb, buf("c_b_dt2")], writes=[be1], tag="sp_abs")
                    sch.add("scalar",
                            lambda m, e1b=e1b, e2b=e2b: m.activation(
                                out=e2b, in_=e1b, func=AF.Exp, bias=0.0, scale=-1.0),
                            reads=[be1], writes=[be2], strict=True, tag="sp_exp")
                    sch.add("scalar",
                            lambda m, e2b=e2b: m.activation(
                                out=e2b, in_=e2b, func=AF.Ln, bias=1.0, scale=1.0),
                            reads=[be2], writes=[be2], strict=True, tag="sp_ln")
                    sch.add("scalar",
                            lambda m, p=pbk, m_=m_, rlb=rlb: m.activation(
                                out=rlb, in_=p, func=AF.Relu,
                                bias=bdt_sb[:, m_:m_ + 1], scale=1.0),
                            reads=[bpb, buf("c_b_dt2")], writes=[brl], tag="sp_relu")
                    dtv = dt_t[m_ % 2]
                    bdtv = buf(f"dt{m_ % 2}")
                    sch.add("vector",
                            lambda m, o=dtv, e2b=e2b, rlb=rlb: m.tensor_add(o, e2b, rlb),
                            reads=[be2, brl], writes=[bdtv], tag="sp_add")
                    Sv = S_t[m_ % 2]
                    bS = buf(f"S{m_ % 2}")
                    scol = srun_sb[:, b * NDT + m_:b * NDT + m_ + 1]
                    sch.add("vector",
                            lambda m, o=Sv, i=dtv, s=scol: m.tensor_tensor_scan(
                                out=o, data0=zeros_sb, data1=i, initial=s,
                                op0=ALU.add, op1=ALU.add),
                            reads=[bdtv, buf("c_zeros"), buf("c_srun")], writes=[bS],
                            strict=True, tag="scum")
                    sch.add("vector",
                            lambda m, i=Sv, s=scol: m.tensor_copy(s, i[:, T1A - 1:T1A]),
                            reads=[bS], writes=[buf("c_srun")], strict=True, tag="scol")
                    dma("gpsimd", sc_dt[b, m_ * 128:(m_ + 1) * 128, c0:c0 + T1A], dtv,
                        reads=[bdtv], tag="stdt")
                    dma("gpsimd", sc_S[b, m_ * 128:(m_ + 1) * 128, c0:c0 + T1A], Sv,
                        reads=[bS], tag="stS")

        sch.barrier()

    ar.phase_reset()
    Bbc = ar.alloc(NS * T1B // 2, BF16, "Bbc")
    Cbc = ar.alloc(NS * T1B // 2, BF16, "Cbc")
    dt_s = [ar.alloc(T1B, name="dt1b")] * 2
    xc_s = [ar.alloc(T1B, name="xc1b") for _ in range(2)]
    sz_s = [ar.alloc(T1B, name="sz1b") for _ in range(2)]
    u_s = [ar.alloc(T1B // 2, BF16, "u1b")] * 2
    dA_s = [ar.alloc(T1B, name="dA") for _ in range(2)]
    dBx_s = [ar.alloc(T1B // 2, BF16, "dBx") for _ in range(2)]
    h_s = [ar.alloc(T1B, name="h") for _ in range(2)]
    g_s = [ar.alloc(T1B // 2, BF16, "g1b") for _ in range(4)]
    yp_s = [[ar.alloc(T1B, name="yp") for _ in range(4)] for _ in range(2)]
    y_s = yp_s[0][0]
    yf_s = [ar.alloc(T1B, name="yf") for _ in range(2)]
    g_bf = [ar.alloc(T1B // 2, BF16, "gbf") for _ in range(2)]
    FD_j = yp_s[0][1]

    NT1B = LC // T1B
    if stop_phase >= 2:
        pending_gate = None
        for b in range(B):
            for blk in range(NT1B):
                c0 = blk * T1B
                for n in range(NS):
                    for which, dst, bname in ((0, Bbc, "bbcB"), (1, Cbc, "bbcC")):
                        dma("sync", dst[:, n * T1B:(n + 1) * T1B],
                            sc_BC[b, which * NS + n, c0:c0 + T1B]
                            .unsqueeze(0).partition_broadcast(128),
                            writes=[buf(f"{bname}{n}")], tag="ldbc")
                if DBG:
                    dma("sync", dram["dbg_bbc"][b, blk, 0, :, :], Bbc,
                        reads=[buf("bbcB")], tag="dbgb")
                    dma("sync", dram["dbg_bbc"][b, blk, 1, :, :], Cbc,
                        reads=[buf("bbcC")], tag="dbgc")
                for dtile in range(NDT):
                    dp_ = dtile % 2
                    bdt = buf("dt1b")
                    bxc = buf(f"xc1b{dp_}")
                    bsz = buf(f"sz1b{dp_}")
                    bu = buf("u1b")
                    dma("scalar", dt_s[dp_],
                        sc_dt[b, dtile * 128:(dtile + 1) * 128, c0:c0 + T1B],
                        writes=[bdt], tag="lddt")
                    dma("scalar", xc_s[dp_],
                        sc_xconv[b, dtile * 128:(dtile + 1) * 128, c0:c0 + T1B],
                        writes=[bxc], tag="ldxc")
                    dma("scalar", sz_s[dp_],
                        sc_sz[b, dtile * 128:(dtile + 1) * 128, c0:c0 + T1B],
                        writes=[bsz], tag="ldsz")
                    sch.add("vector",
                            lambda m, dp_=dp_: m.tensor_mul(u_s[dp_], dt_s[dp_], xc_s[dp_]),
                            reads=[bdt, bxc], writes=[bu], tag="umul")
                    if DBG:
                        dma("sync", dram["dbg_u"][b, dtile * 128:(dtile + 1) * 128, c0:c0 + T1B],
                            u_s[dp_], reads=[bu], tag="dbgu")
                    for n in range(NS):
                        np_ = n % 2
                        acol = A_sb[:, dtile * NS + n:dtile * NS + n + 1]
                        hcol = hst_sb[:, (b * NDT + dtile) * NS + n:
                                      (b * NDT + dtile) * NS + n + 1]
                        bdA = buf(f"dA{np_}")
                        bdBx = buf(f"dBx{np_}")
                        bh = buf(f"h{np_}")
                        bg = buf(f"g{np_}")
                        sch.add("scalar",
                                lambda m, a=acol, dp_=dp_, np_=np_: m.activation(
                                    out=dA_s[np_], in_=dt_s[dp_], func=AF.Exp,
                                    bias=0.0, scale=a),
                                reads=[bdt, buf("c_a_log2")], writes=[bdA], tag="dAexp")
                        sch.add("vector",
                                lambda m, n=n, dp_=dp_, np_=np_: m.tensor_mul(
                                    dBx_s[np_], u_s[dp_], Bbc[:, n * T1B:(n + 1) * T1B]),
                                reads=[bu, buf(f"bbcB{n}")], writes=[bdBx], tag="dBx")
                        sch.add("vector",
                                lambda m, h=hcol, np_=np_: m.tensor_tensor_scan(
                                    out=h_s[np_], data0=dA_s[np_], data1=dBx_s[np_],
                                    initial=h, op0=ALU.mult, op1=ALU.add),
                                reads=[bdA, bdBx, buf("c_hst")],
                                writes=[bh], strict=True, tag="scan")
                        sch.add("vector",
                                lambda m, h=hcol, np_=np_: m.tensor_copy(
                                    h, h_s[np_][:, T1B - 1:T1B]),
                                reads=[bh], writes=[buf("c_hst")], strict=True,
                                tag="hcol")
                        g4 = n % 4
                        bp = buf(f"yp{dp_}{g4}")
                        if n < 4:
                            sch.add("vector",
                                    lambda m, n=n, np_=np_, g4=g4, dp_=dp_:
                                    m.tensor_mul(yp_s[dp_][g4], h_s[np_],
                                                 Cbc[:, n * T1B:(n + 1) * T1B]),
                                    reads=[bh, buf(f"bbcC{n}")], writes=[bp],
                                    tag="ymul0")
                        else:
                            bg = buf(f"g{g4}")
                            sch.add("vector",
                                    lambda m, n=n, np_=np_: m.tensor_mul(
                                        g_s[n % 4], h_s[np_],
                                        Cbc[:, n * T1B:(n + 1) * T1B]),
                                    reads=[bh, buf(f"bbcC{n}")], writes=[bg],
                                    tag="ymul")
                            sch.add("gpsimd",
                                    lambda m, g4=g4, dp_=dp_: m.dma_start(
                                        out=yp_s[dp_][g4], in_=g_s[g4],
                                        accum_op=ALU.add),
                                    reads=[bg, bp], writes=[bp], dma=True, tag="yacc")
                    if DBG:
                        dma("sync", dram["dbg_y"][b, dtile * 128:(dtile + 1) * 128, c0:c0 + T1B],
                            y_s, reads=[buf("yp00")], tag="dbgy")
                    sch.add("gpsimd",
                            lambda m, dp_=dp_: m.dma_start(
                                out=yp_s[dp_][0], in_=yp_s[dp_][1], accum_op=ALU.add),
                            reads=[buf(f"yp{dp_}1"), buf(f"yp{dp_}0")],
                            writes=[buf(f"yp{dp_}0")], dma=True, tag="ycomb")
                    sch.add("gpsimd",
                            lambda m, dp_=dp_: m.dma_start(
                                out=yp_s[dp_][2], in_=yp_s[dp_][3], accum_op=ALU.add),
                            reads=[buf(f"yp{dp_}3"), buf(f"yp{dp_}2")],
                            writes=[buf(f"yp{dp_}2")], dma=True, tag="ycomb")
                    sch.add("gpsimd",
                            lambda m, dp_=dp_: m.dma_start(
                                out=yp_s[dp_][0], in_=yp_s[dp_][2], accum_op=ALU.add),
                            reads=[buf(f"yp{dp_}2"), buf(f"yp{dp_}0")],
                            writes=[buf(f"yp{dp_}0")], dma=True, tag="ycomb")

                    def emit_gate(b=b, dtile=dtile, dp_=dp_, c0=c0,
                                  bxc=bxc, bsz=bsz):
                        sch.add("vector",
                                lambda m, dtile=dtile, dp_=dp_:
                                m.scalar_tensor_tensor(
                                    out=yf_s[dp_], in0=xc_s[dp_],
                                    scalar=dp_sb[:, dtile:dtile + 1],
                                    in1=yp_s[dp_][0], op0=ALU.mult, op1=ALU.add),
                                reads=[bxc, buf(f"yp{dp_}0"), buf("c_dp2")],
                                writes=[buf(f"yf{dp_}")], tag="yfull")
                        sch.add("vector",
                                lambda m, dp_=dp_: m.tensor_mul(
                                    g_bf[dp_][:, 0:T1B], yf_s[dp_], sz_s[dp_]),
                                reads=[buf(f"yf{dp_}"), bsz], writes=[buf(f"gbf{dp_}")],
                                tag="gate")
                        dma("sync",
                            sc_gated[b, dtile * 128:(dtile + 1) * 128, c0:c0 + T1B],
                            g_bf[dp_][:, 0:T1B], reads=[buf(f"gbf{dp_}")], tag="stgated")

                    if pending_gate is not None:
                        pending_gate()
                    pending_gate = emit_gate

        if pending_gate is not None:
            pending_gate()
            pending_gate = None
        for b in range(B):
            for dtile in range(NDT):
                scol = srun_sb[:, b * NDT + dtile:b * NDT + dtile + 1]
                sch.add("scalar",
                        lambda m, s=scol, dtile=dtile, b=b: m.activation(
                            out=D_sb[:, (b * NDT + dtile) * NS:
                                     (b * NDT + dtile + 1) * NS],
                            in_=A_sb[:, dtile * NS:(dtile + 1) * NS], func=AF.Exp,
                            bias=0.0, scale=s),
                        reads=[buf("c_srun"), buf("c_a_log2")], writes=[buf("c_D")],
                        tag="Dexp")

        sch.barrier()

        ar.phase_reset()
        wo_bf = ar.alloc(NDT * DM // 2, BF16, "wo")
        Cbc2 = [ar.alloc(NS * T2A // 2, BF16, "Cbc2") for _ in range(2)]
        S_2 = [ar.alloc(T2A, name="S2a") for _ in range(2)]
        sz_2 = [ar.alloc(T2A, name="sz2a") for _ in range(2)]
        Pc_2 = [ar.alloc(T2A // 2, BF16, "Pc") for _ in range(4)]
        w_2 = [ar.alloc(T2A // 2, BF16, "w2a") for _ in range(2)]
        corr_2 = ar.alloc(T2A, name="corr")
        corrB_2 = ar.alloc(T2A, name="corrB")
        gat_bf = [ar.alloc(T2A // 2, BF16, "gat2a") for _ in range(2)]
        gs_bf = ar.alloc(NDT * T2A // 2, BF16, "gs")
        r1_f = ar.alloc(NKM * T2A, name="r1")
        xt_f = ar.alloc(NKM * T2A, name="xt2a")
        sq_f = ar.alloc(T2A, name="sq2a")
        xn1_f = ar.alloc(NKM * T2A, name="xn1")
        row_f = ar.alloc(3 * T2A, name="rows")

        for k in range(NDT):
            dma("gpsimd", wo_bf[:, k * DM:(k + 1) * DM],
                dram["w_outT"][k * 128:(k + 1) * 128, :], writes=[buf("w_wo")],
                tag="ldwo")
        cc_in, cc_out = dram["cc_in"], dram["cc_out"]
        dma("gpsimd", cc_in[:, 0:512], hst_sb, reads=[buf("c_hst")],
            writes=[buf("ccin")], tag="ccF")
        dma("gpsimd", cc_in[:, 512:1024], D_sb, reads=[buf("c_D")],
            writes=[buf("ccin")], tag="ccD")
        sch.add("gpsimd",
                lambda m: m.collective_compute(
                    "AllGather", ALU.bypass, replica_groups=[list(range(NCORES))],
                    ins=[cc_in.opt()], outs=[cc_out.opt()]),
                reads=[buf("ccin")], writes=[buf("ccout")], tag="allgather")
        sch.add("vector", lambda m: m.memset(h0_sb, 0.0), writes=[buf("c_h0")])
        for j in range(NCORES):
            bfd = buf("fdj")
            dma("sync", FD_j, cc_out[j * 128:(j + 1) * 128, :],
                reads=[buf("ccout")], writes=[bfd], tag="ldfd")
            mcol = rk_sb[:, j:j + 1]
            sch.add("vector",
                    lambda m, mc=mcol, j=j: m.scalar_tensor_tensor(
                        out=FD_j[:, 512:1024], in0=FD_j[:, 512:1024], scalar=mc,
                        in1=rkc_sb[:, j:j + 1].broadcast_to([128, 512]),
                        op0=ALU.mult, op1=ALU.add),
                    reads=[bfd, buf("c_rkmask"), buf("c_rkc")], writes=[bfd],
                    strict=True, tag="dm")
            sch.add("vector",
                    lambda m: m.tensor_mul(h0_sb, h0_sb, FD_j[:, 512:1024]),
                    reads=[bfd, buf("c_h0")], writes=[buf("c_h0")], strict=True, tag="h0d")
            sch.add("vector",
                    lambda m, mc=mcol: m.scalar_tensor_tensor(
                        out=h0_sb, in0=FD_j[:, 0:512], scalar=mc, in1=h0_sb,
                        op0=ALU.mult, op1=ALU.add),
                    reads=[bfd, buf("c_rkmask"), buf("c_h0")], writes=[buf("c_h0")],
                    strict=True, tag="h0f")

    NT2A = LC // T2A
    if stop_phase >= 3:
        def ldc2(b, blk):
            c0 = blk * T2A
            bp_ = blk % 2
            for n in range(NS):
                dma("sync", Cbc2[bp_][:, n * T2A:(n + 1) * T2A],
                    sc_BC[b, NS + n, c0:c0 + T2A]
                    .unsqueeze(0).partition_broadcast(128),
                    writes=[buf(f"cbc2{bp_}")], tag="ldc2")

        blocks2a = [(b, blk) for b in range(B) for blk in range(NT2A)]
        ldc2(*blocks2a[0])
        for bi, (b, blk) in enumerate(blocks2a):
                c0 = blk * T2A
                bp_ = blk % 2
                bcbc2 = buf(f"cbc2{bp_}")
                for dtile in range(NDT):
                    dp_ = dtile % 2
                    bS = buf(f"S2a{dp_}")
                    bsz = buf(f"sz2a{dp_}")
                    bgat = buf(f"gat2a{dp_}")
                    dma("scalar", S_2[dp_],
                        sc_S[b, dtile * 128:(dtile + 1) * 128, c0:c0 + T2A],
                        writes=[bS], tag="ldS")
                    dma("scalar", sz_2[dp_],
                        sc_sz[b, dtile * 128:(dtile + 1) * 128, c0:c0 + T2A],
                        writes=[bsz], tag="ldsz2")
                    dma("scalar", gat_bf[dp_][:, 0:T2A],
                        sc_gated[b, dtile * 128:(dtile + 1) * 128, c0:c0 + T2A],
                        writes=[bgat], tag="ldgat")
                    for n in range(NS):
                        np_ = n % 2
                        pp4 = n % 4
                        acol = A_sb[:, dtile * NS + n:dtile * NS + n + 1]
                        hcol = h0_sb[:, (b * NDT + dtile) * NS + n:
                                     (b * NDT + dtile) * NS + n + 1]
                        bPc = buf(f"Pc{pp4}")
                        bw = buf(f"w2a{np_}")
                        sch.add("scalar",
                                lambda m, a=acol, dp_=dp_, pp4=pp4: m.activation(
                                    out=Pc_2[pp4], in_=S_2[dp_], func=AF.Exp,
                                    bias=0.0, scale=a),
                                reads=[bS, buf("c_a_log2")], writes=[bPc],
                                tag="Pcexp")
                        tgt = (corr_2, corrB_2)[n % 2] if n < 2 else w_2[np_]
                        btg = (buf("corr"), buf("corrB"))[n % 2] if n < 2 else bw
                        sch.add("vector",
                                lambda m, h=hcol, n=n, o=tgt, pp4=pp4, bp_=bp_:
                                m.scalar_tensor_tensor(
                                    out=o, in0=Pc_2[pp4], scalar=h,
                                    in1=Cbc2[bp_][:, n * T2A:(n + 1) * T2A],
                                    op0=ALU.mult, op1=ALU.mult),
                                reads=[bPc, buf("c_h0"), bcbc2],
                                writes=[btg], tag="wstt")
                        if n >= 2:
                            acc, bacc = ((corr_2, buf("corr")) if n % 2 == 0
                                         else (corrB_2, buf("corrB")))
                            sch.add("vector",
                                    lambda m, np_=np_, acc=acc: m.tensor_add(
                                        acc, acc, w_2[np_]),
                                    reads=[bw, bacc], writes=[bacc],
                                    strict=True, tag="corradd")
                    sch.add("vector",
                            lambda m: m.tensor_add(corr_2, corr_2, corrB_2),
                            reads=[buf("corrB"), buf("corr")], writes=[buf("corr")],
                            strict=True, tag="corrfin")
                    sch.add("vector",
                            lambda m, dp_=dp_: m.tensor_mul(corr_2, corr_2, sz_2[dp_]),
                            reads=[buf("corr"), bsz], writes=[buf("corr")],
                            strict=True, tag="corrsz")
                    sch.add("vector",
                            lambda m, dtile=dtile, dp_=dp_: m.tensor_add(
                                gs_bf[:, dtile * T2A:(dtile + 1) * T2A],
                                gat_bf[dp_][:, 0:T2A], corr_2),
                            reads=[buf("corr"), bgat], writes=[buf(f"gs{dtile}")],
                            strict=True, tag="gs")
                if bi + 1 < len(blocks2a):
                    ldc2(*blocks2a[bi + 1])
                for k in range(NKM):
                    dma("sync", xt_f[:, k * T2A:(k + 1) * T2A],
                        xT[b, k * 128:(k + 1) * 128, 3 + c0:3 + c0 + T2A],
                        writes=[buf("xt2a")], tag="ldxt2")
                for k in range(NDT):
                    for mo in range(NKM):
                        mm(bank[mo], wo_bf[:, k * DM + mo * 128:k * DM + (mo + 1) * 128],
                           gs_bf[:, k * T2A:(k + 1) * T2A], k == 0, k == NDT - 1,
                           reads=[buf(f"gs{k}"), buf("w_wo")], writes=[buf(f"pb{mo}")],
                           tag="wout")
                for mo in range(NKM):
                    sch.add("vector",
                            lambda m, mo=mo: m.tensor_add(
                                r1_f[:, mo * T2A:(mo + 1) * T2A], bank[mo],
                                xt_f[:, mo * T2A:(mo + 1) * T2A]),
                            reads=[buf(f"pb{mo}"), buf("xt2a")], writes=[buf("r1")],
                            tag="r1")
                for k in range(NKM):
                    mm(bank[4][0:1, 0:T2A], onesc_sb,
                       r1_f[:, k * T2A:(k + 1) * T2A], k == 0, k == NKM - 1,
                       reads=[buf("r1"), buf("c_onesc")], writes=[buf("pb4")], tag="ln1m")
                for k in range(NKM):
                    sch.add("scalar",
                            lambda m, k=k: m.activation(
                                out=sq_f, in_=r1_f[:, k * T2A:(k + 1) * T2A],
                                func=AF.Square, bias=0.0, scale=1.0),
                            reads=[buf("r1")], writes=[buf("sq2a")], tag="ln1sq")
                    mm(bank[5][0:1, 0:T2A], onesc_sb, sq_f, k == 0, k == NKM - 1,
                       reads=[buf("sq2a"), buf("c_onesc")], writes=[buf("pb5")],
                       tag="ln1v")
                mrow = row_f[0:1, 0:T2A]
                vrow = row_f[0:1, T2A:2 * T2A]
                rrow = row_f[0:1, 2 * T2A:3 * T2A]
                sch.add("vector",
                        lambda m: m.tensor_scalar(out=mrow, in0=bank[4][0:1, 0:T2A],
                                                  scalar1=1.0 / DM, scalar2=None,
                                                  op0=ALU.mult),
                        reads=[buf("pb4")], writes=[buf("rows")], tag="mean")
                sch.add("vector",
                        lambda m: m.tensor_scalar(out=vrow, in0=bank[5][0:1, 0:T2A],
                                                  scalar1=1.0 / DM, scalar2=None,
                                                  op0=ALU.mult),
                        reads=[buf("pb5")], writes=[buf("rows")], strict=True, tag="var0")
                sch.add("vector",
                        lambda m: m.tensor_mul(rrow, mrow, mrow),
                        reads=[buf("rows")], writes=[buf("rows")], strict=True, tag="m2")
                sch.add("vector",
                        lambda m: m.tensor_sub(vrow, vrow, rrow),
                        reads=[buf("rows")], writes=[buf("rows")], strict=True, tag="var")
                sch.add("scalar",
                        lambda m: m.activation(out=rrow, in_=vrow, func=AF.Ln,
                                               bias=epsc_sb[0:1, :], scale=1.0),
                        reads=[buf("rows"), buf("c_eps")], writes=[buf("rows")], strict=True, tag="lnv")
                sch.add("scalar",
                        lambda m: m.activation(out=rrow, in_=rrow, func=AF.Exp,
                                               bias=0.0, scale=-0.5),
                        reads=[buf("rows")], writes=[buf("rows")], strict=True, tag="rstd")
                mm(bank[6][:, 0:T2A], onesr_sb[0:1, :], mrow, True, True,
                   reads=[buf("rows"), buf("c_onesr")], writes=[buf("pb6")], tag="mbc")
                mm(bank[7][:, 0:T2A], onesr_sb[0:1, :], rrow, True, True,
                   reads=[buf("rows"), buf("c_onesr")], writes=[buf("pb7")], tag="rbc")
                for k in range(NKM):
                    sch.add("vector",
                            lambda m, k=k: m.tensor_sub(
                                xn1_f[:, k * T2A:(k + 1) * T2A],
                                r1_f[:, k * T2A:(k + 1) * T2A], bank[6][:, 0:T2A]),
                            reads=[buf("r1"), buf("pb6")], writes=[buf("xn1")], tag="xm")
                    sch.add("vector",
                            lambda m, k=k: m.tensor_mul(
                                xn1_f[:, k * T2A:(k + 1) * T2A],
                                xn1_f[:, k * T2A:(k + 1) * T2A], bank[7][:, 0:T2A]),
                            reads=[buf("xn1"), buf("pb7")], writes=[buf("xn1")],
                            strict=True, tag="xr")
                    dma("sync", sc_xn1f[b, k * 128:(k + 1) * 128, c0:c0 + T2A],
                        xn1_f[:, k * T2A:(k + 1) * T2A], reads=[buf("xn1")], tag="stxn1f")
                    dma("gpsimd", sc_xn1b[b, k * 128:(k + 1) * 128, c0:c0 + T2A],
                        xn1_f[:, k * T2A:(k + 1) * T2A], reads=[buf("xn1")], tag="stxn1b")

        sch.barrier()

    ar.phase_reset()
    wfc_bf = [ar.alloc(NKM * IM // 2, BF16, "wfc") for _ in range(2)]
    wpj_bf = ar.alloc(NEXP * 4 * DM // 2, BF16, "wpj")
    wg_bf = ar.alloc(NKM * NEXP // 2 + 8, BF16, "wg")
    xn1b_s = [ar.alloc(NKM * T2B // 2, BF16, "xn1b2b") for _ in range(2)]
    xn1f_s = ar.alloc(T2B, name="xn1f2b")
    rt_f = [ar.alloc(T2B, name="rtf")] * 2
    rtT_f = ar.alloc(NEXP, name="rtT")
    gcol = ar.alloc(8, name="gcols")
    gmask = ar.alloc(4 * NEXP, name="gmask")
    gT_f = [ar.alloc(T2B, name="gT")] * 2
    gstage = ar.alloc(T2B + 8, name="gstage")
    gbc_f = [ar.alloc(NEXP * T2B // 2, BF16, "gbc") for _ in range(2)]
    hg_bf = ar.alloc(4 * NEXP * T2B // 2, BF16, "hg")
    r2_f = [ar.alloc(NKM * T2B, name="r2") for _ in range(2)]
    r2T_f = ar.alloc(DM, name="r2T")
    sq2_f = ar.alloc(DM, name="sq2b")
    st_c = ar.alloc(8, name="stats2b")
    out_f = ar.alloc(DM, name="out2b")

    out_d = dram["out"]
    NT2B = LC // T2B
    if stop_phase >= 4:
        for e in range(NEXP):
            for k in range(4):
                dma("gpsimd", wpj_bf[:, (e * 4 + k) * DM:(e * 4 + k + 1) * DM],
                    dram["w_projT"][e, k * 128:(k + 1) * 128, :], writes=[buf("w_pj")],
                    tag="ldwpj")
        for k in range(NKM):
            dma("gpsimd", wg_bf[:, k * NEXP:(k + 1) * NEXP],
                dram["w_gateT"][k * 128:(k + 1) * 128, :], writes=[buf("w_wg")],
                tag="ldwg")

        m1c = gcol[:, 0:1]
        nm1c = gcol[:, 1:2]
        m2c = gcol[:, 2:3]
        e2c = gcol[:, 3:4]
        w1c = gcol[:, 4:5]
        w2c = gcol[:, 5:6]
        scol2 = st_c[:, 0:1]
        vcol2 = st_c[:, 1:2]
        rcol2 = st_c[:, 2:3]
        mcol2 = st_c[:, 3:4]

        def emit_ln2(b, blk, qp_):
            c0 = blk * T2B
            for sub in range(T2B // 128):
                sc0 = sub * 128
                for k in range(NKM):
                    pbk = bank[k % 2]
                    bpb = buf(f"pb{k % 2}")
                    sch.add("tensor",
                            lambda m, k=k, sc0=sc0, p=pbk, qp_=qp_: m.transpose(
                                p[:, 0:128],
                                r2_f[qp_][:, k * T2B + sc0:k * T2B + sc0 + 128], id_sb),
                            reads=[buf(f"r2{qp_}"), buf("c_ident")], writes=[bpb],
                            tag="trans2")
                    sch.add("scalar",
                            lambda m, k=k, p=pbk: m.activation(
                                out=r2T_f[:, k * 128:(k + 1) * 128], in_=p[:, 0:128],
                                func=AF.Copy, bias=0.0, scale=1.0),
                            reads=[bpb], writes=[buf("r2T")], tag="trev")
                sch.add("vector",
                        lambda m: m.tensor_reduce(out=scol2, in_=r2T_f, axis=AX.X,
                                                  op=ALU.add),
                        reads=[buf("r2T")], writes=[buf("st2")], strict=True,
                        tag="sum2")
                sch.add("vector",
                        lambda m: m.tensor_scalar(out=mcol2, in0=scol2,
                                                  scalar1=1.0 / DM, scalar2=None,
                                                  op0=ALU.mult),
                        reads=[buf("st2")], writes=[buf("st2")], strict=True,
                        tag="mean2")
                sch.add("scalar",
                        lambda m: m.activation(out=sq2_f, in_=r2T_f, func=AF.Square,
                                               bias=0.0, scale=1.0),
                        reads=[buf("r2T")], writes=[buf("sq2b")], tag="sq2")
                sch.add("vector",
                        lambda m: m.tensor_reduce(out=vcol2, in_=sq2_f, axis=AX.X,
                                                  op=ALU.add),
                        reads=[buf("sq2b")], writes=[buf("st2")], strict=True,
                        tag="sumsq2")
                sch.add("vector",
                        lambda m: m.tensor_scalar(out=vcol2, in0=vcol2,
                                                  scalar1=1.0 / DM, scalar2=None,
                                                  op0=ALU.mult),
                        reads=[buf("st2")], writes=[buf("st2")], strict=True, tag="v2")
                sch.add("vector",
                        lambda m: m.tensor_mul(rcol2, mcol2, mcol2),
                        reads=[buf("st2")], writes=[buf("st2")], strict=True,
                        tag="m22")
                sch.add("vector",
                        lambda m: m.tensor_sub(vcol2, vcol2, rcol2),
                        reads=[buf("st2")], writes=[buf("st2")], strict=True,
                        tag="var2")
                sch.add("scalar",
                        lambda m: m.activation(out=rcol2, in_=vcol2, func=AF.Ln,
                                               bias=epsc_sb, scale=1.0),
                        reads=[buf("st2"), buf("c_eps")], writes=[buf("st2")],
                        strict=True, tag="lnv2")
                sch.add("scalar",
                        lambda m: m.activation(out=rcol2, in_=rcol2, func=AF.Exp,
                                               bias=0.0, scale=-0.5),
                        reads=[buf("st2")], writes=[buf("st2")], strict=True,
                        tag="rstd2")
                sch.add("vector",
                        lambda m: m.tensor_scalar(
                            out=out_f, in0=r2T_f, scalar1=mcol2, scalar2=rcol2,
                            op0=ALU.subtract, op1=ALU.mult),
                        reads=[buf("r2T"), buf("st2")], writes=[buf("outb")],
                        strict=True, tag="xn2")
                dma("sync", out_d[b, c0 + sc0:c0 + sc0 + 128, :], out_f,
                    reads=[buf("outb")], tag="stout")

        def ld_xn1b(b, blk):
            c0 = blk * T2B
            qp_ = blk % 2
            for k in range(NKM):
                dma("scalar", xn1b_s[qp_][:, k * T2B:(k + 1) * T2B],
                    sc_xn1b[b, k * 128:(k + 1) * 128, c0:c0 + T2B],
                    writes=[buf(f"xn1b2b{qp_}")], tag="ldxn1b")

        blocks2b = [(b, blk) for b in range(B) for blk in range(NT2B)]
        ld_xn1b(*blocks2b[0])
        pending_ln2 = None
        for bi2, (b, blk) in enumerate(blocks2b):
                c0 = blk * T2B
                qp_ = blk % 2
                xn1b = xn1b_s[qp_]
                bxn1b = buf(f"xn1b2b{qp_}")
                rtv = rt_f[qp_]
                brt = buf("rtf")
                gTv = gT_f[qp_]
                bgT = buf("gTf")
                gbcv = gbc_f[qp_]
                bgbc = buf(f"gbc{qp_}")
                for k in range(NKM):
                    mm(bank[0][0:NEXP, 0:T2B], wg_bf[:, k * NEXP:(k + 1) * NEXP],
                       xn1b[:, k * T2B:(k + 1) * T2B], k == 0, k == NKM - 1,
                       reads=[bxn1b, buf("w_wg")], writes=[buf("pb0")],
                       tag="router")
                sch.add("vector",
                        lambda m, rtv=rtv: m.tensor_copy(rtv[0:NEXP, 0:T2B],
                                                         bank[0][0:NEXP, 0:T2B]),
                        reads=[buf("pb0")], writes=[brt], tag="rtev")
                for sub in range(T2B // 128):
                    sc0 = sub * 128
                    mm(bank[1][:, 0:NEXP], rtv[0:NEXP, sc0:sc0 + 128],
                       id_sb[0:NEXP, 0:NEXP], True, True,
                       reads=[brt, buf("c_ident")], writes=[buf("pb1")], tag="rtT")
                    sch.add("vector",
                            lambda m: m.tensor_copy(rtT_f[:, 0:NEXP], bank[1][:, 0:NEXP]),
                            reads=[buf("pb1")], writes=[buf("rtT")], tag="rtTev")
                    sch.add("vector",
                            lambda m: m.tensor_reduce(out=m1c, in_=rtT_f[:, 0:NEXP],
                                                      axis=AX.X, op=ALU.max),
                            reads=[buf("rtT")], writes=[buf("gcols")], strict=True,
                            tag="m1")
                    sch.add("vector",
                            lambda m: m.tensor_single_scalar(
                                out=gmask[:, 0:NEXP], in_=rtT_f[:, 0:NEXP], scalar=m1c,
                                op=ALU.is_ge),
                            reads=[buf("rtT"), buf("gcols")], writes=[buf("gmask")],
                            strict=True, tag="ismax")
                    sch.add("vector",
                            lambda m: m.scalar_tensor_tensor(
                                out=gmask[:, NEXP:2 * NEXP], in0=gmask[:, 0:NEXP],
                                scalar=-1e30, in1=rtT_f[:, 0:NEXP],
                                op0=ALU.mult, op1=ALU.add),
                            reads=[buf("gmask"), buf("rtT")], writes=[buf("gmask")],
                            strict=True, tag="maskmax")
                    sch.add("vector",
                            lambda m: m.tensor_reduce(out=m2c,
                                                      in_=gmask[:, NEXP:2 * NEXP],
                                                      axis=AX.X, op=ALU.max),
                            reads=[buf("gmask")], writes=[buf("gcols")], strict=True,
                            tag="m2")
                    sch.add("vector",
                            lambda m: m.tensor_scalar(out=nm1c, in0=m1c, scalar1=-1.0,
                                                      scalar2=None, op0=ALU.mult),
                            reads=[buf("gcols")], writes=[buf("gcols")], strict=True,
                            tag="nm1")
                    sch.add("scalar",
                            lambda m: m.activation(out=e2c, in_=m2c, func=AF.Exp,
                                                   bias=nm1c, scale=1.0),
                            reads=[buf("gcols")], writes=[buf("gcols")], strict=True,
                            tag="e2")
                    sch.add("vector",
                            lambda m: m.tensor_scalar(out=w1c, in0=e2c, scalar1=1.0,
                                                      scalar2=None, op0=ALU.add),
                            reads=[buf("gcols")], writes=[buf("gcols")], strict=True,
                            tag="den")
                    sch.add("vector",
                            lambda m: m.reciprocal(out=w1c, in_=w1c),
                            reads=[buf("gcols")], writes=[buf("gcols")], strict=True,
                            tag="w1")
                    sch.add("vector",
                            lambda m: m.tensor_mul(w2c, e2c, w1c),
                            reads=[buf("gcols")], writes=[buf("gcols")], strict=True,
                            tag="w2")
                    sch.add("vector",
                            lambda m: m.tensor_single_scalar(
                                out=gmask[:, 2 * NEXP:3 * NEXP],
                                in_=gmask[:, NEXP:2 * NEXP], scalar=m2c, op=ALU.is_ge),
                            reads=[buf("gmask"), buf("gcols")], writes=[buf("gmask")],
                            strict=True, tag="ismax2")
                    sch.add("vector",
                            lambda m: m.tensor_single_scalar(
                                out=gmask[:, 0:NEXP], in_=gmask[:, 0:NEXP], scalar=w1c,
                                op=ALU.mult),
                            reads=[buf("gmask"), buf("gcols")], writes=[buf("gmask")],
                            strict=True, tag="gw1")
                    sch.add("vector",
                            lambda m: m.scalar_tensor_tensor(
                                out=gmask[:, 3 * NEXP:4 * NEXP],
                                in0=gmask[:, 2 * NEXP:3 * NEXP], scalar=w2c,
                                in1=gmask[:, 0:NEXP], op0=ALU.mult, op1=ALU.add),
                            reads=[buf("gmask"), buf("gcols")], writes=[buf("gmask")],
                            strict=True, tag="gates")
                    mm(bank[2][0:NEXP, 0:128], gmask[:, 3 * NEXP:4 * NEXP],
                       id_sb[:, 0:128], True, True,
                       reads=[buf("gmask"), buf("c_ident")], writes=[buf("pb2")],
                       tag="gTmm")
                    sch.add("vector",
                            lambda m, sc0=sc0, gTv=gTv: m.tensor_copy(
                                gTv[0:NEXP, sc0:sc0 + 128], bank[2][0:NEXP, 0:128]),
                            reads=[buf("pb2")], writes=[bgT], tag="gTev")
                for e in range(NEXP):
                    stg = gstage[0:1, 0:T2B]
                    bst = buf("gst")
                    dma("sync", stg, gTv[e:e + 1, 0:T2B], reads=[bgT],
                        writes=[bst], tag="gstage")
                    mm(bank[3][:, 0:T2B], onesr_sb[0:1, :], stg, True, True,
                       reads=[bst, buf("c_onesr")], writes=[buf("pb3")], tag="gbcmm")
                    sch.add("scalar",
                            lambda m, e=e, gbcv=gbcv: m.activation(
                                out=gbcv[:, e * T2B:(e + 1) * T2B], in_=bank[3][:, 0:T2B],
                                func=AF.Copy, bias=0.0, scale=1.0),
                            reads=[buf("pb3")], writes=[bgbc], tag="gbcev")
                for e in range(NEXP):
                    ep_ = e % 2
                    bwfc = buf(f"w_fc{ep_}")
                    dma("gpsimd",
                        wfc_bf[ep_][:, 0:NKM * IM].rearrange("p (k i) -> p k i", i=IM),
                        dram["w_fcT"][e].rearrange("(k p) i -> p k i", p=128),
                        writes=[bwfc], tag="ldwfc")
                    for mi in range(4):
                        _fb = (2, 4, 5)[(e * 4 + mi) % 3]
                        pbk = bank[_fb]
                        bpb = buf(f"pb{_fb}")
                        for k in range(NKM):
                            mm(pbk[:, 0:T2B],
                               wfc_bf[ep_][:, k * IM + mi * 128:k * IM + (mi + 1) * 128],
                               xn1b[:, k * T2B:(k + 1) * T2B], k == 0, k == NKM - 1,
                               reads=[bxn1b, bwfc], writes=[bpb], tag="fc")
                        hslot = e * 4 + mi
                        sch.add("scalar",
                                lambda m, p=pbk, hs=hslot: m.activation(
                                    out=hg_bf[:, hs * T2B:(hs + 1) * T2B],
                                    in_=p[:, 0:T2B], func=AF.Gelu, bias=0.0, scale=1.0),
                                reads=[bpb], writes=[buf("hg")], tag="gelu")
                        sch.add("vector",
                                lambda m, hs=hslot, e=e, gbcv=gbcv: m.tensor_mul(
                                    hg_bf[:, hs * T2B:(hs + 1) * T2B],
                                    hg_bf[:, hs * T2B:(hs + 1) * T2B],
                                    gbcv[:, e * T2B:(e + 1) * T2B]),
                                reads=[buf("hg"), bgbc], writes=[buf("hg")],
                                strict=True, tag="hgmul")
                if bi2 + 1 < len(blocks2b):
                    ld_xn1b(*blocks2b[bi2 + 1])
                for mo in range(NKM):
                    pbk = bank[6 + mo % 2]
                    bpb = buf(f"pb{6 + mo % 2}")
                    first = True
                    for e in range(NEXP):
                        for ki in range(4):
                            mm(pbk[:, 0:T2B],
                               wpj_bf[:, (e * 4 + ki) * DM + mo * 128:
                                      (e * 4 + ki) * DM + (mo + 1) * 128],
                               hg_bf[:, (e * 4 + ki) * T2B:(e * 4 + ki + 1) * T2B],
                               first, (e == NEXP - 1 and ki == 3),
                               reads=[buf("hg"), buf("w_pj")], writes=[bpb], tag="proj")
                            first = False
                    dma("scalar", xn1f_s,
                        sc_xn1f[b, mo * 128:(mo + 1) * 128, c0:c0 + T2B],
                        writes=[buf("xn1f2b")], tag="ldxn1f")
                    sch.add("vector",
                            lambda m, mo=mo, p=pbk, qp_=qp_: m.tensor_add(
                                r2_f[qp_][:, mo * T2B:(mo + 1) * T2B], p[:, 0:T2B],
                                xn1f_s),
                            reads=[bpb, buf("xn1f2b")], writes=[buf(f"r2{qp_}")],
                            tag="r2")
                if pending_ln2 is not None:
                    pending_ln2()
                pending_ln2 = (lambda b=b, blk=blk, qp_=qp_: emit_ln2(b, blk, qp_))
        if pending_ln2 is not None:
            pending_ln2()

        sch.barrier()

    sch.finalize(nc, stack)
    block = stack.enter_context(nc.Block())

    @block.sync
    def _(m):
        sch.emit_engine(m, "sync")

    @block.tensor
    def _(m):
        sch.emit_engine(m, "tensor")

    @block.vector
    def _(m):
        sch.emit_engine(m, "vector")

    @block.scalar
    def _(m):
        sch.emit_engine(m, "scalar")

    @block.gpsimd
    def _(m):
        sch.emit_engine(m, "gpsimd")
